# version 08
# baseline (speedup 1.0000x reference)
import numpy as np
import contextlib
import ml_dtypes
import concourse.bass as bass
import concourse.mybir as mybir
from concourse.bass_utils import run_bass_kernel_spmd
from concourse.alu_op_type import AluOpType as OP

S = 4096


F32 = mybir.dt.float32
BF16 = mybir.dt.bfloat16
AF = mybir.ActivationFunctionType
AX = mybir.AxisListType

ENGS = ("t", "v", "s", "q", "g")


class Prog:
    def __init__(self, name="prog"):
        self.nc = bass.Bass("TRN2", target_bir_lowering=False)
        self.st = contextlib.ExitStack()
        self.queues = {e: [] for e in ENGS}
        self._epoch = 0
        self.cur = {e: e + "0" for e in ENGS}
        self.cnt = {}
        for e in ENGS:
            self.cnt[e + "0"] = 0
        self.sems = {}
        self._finalized = False

    def new_epoch(self):
        self._epoch += 1
        for e in ENGS:
            name = e + str(self._epoch)
            self.cur[e] = name
            self.cnt[name] = 0

    def dram(self, name, shape, dtype=F32, kind="ExternalInput", **kw):
        return self.nc.dram_tensor(name, list(shape), dtype, kind=kind, **kw).ap()

    def sbuf(self, name, shape, dtype=F32):
        if getattr(self, "_scope", None) is not None:
            h = self.sbuf_tmp(name, shape, dtype)
            self._scope.append(h)
            return h
        return self.st.enter_context(self.nc.sbuf_tensor(name, list(shape), dtype))

    def push_scope(self):
        assert getattr(self, "_scope", None) is None
        self._scope = []
        if not hasattr(self, "_tmp_cms"):
            self._tmp_cms = {}
        self._scope_base = set(self._tmp_cms.keys())

    def pop_scope(self):
        newids = [k for k in self._tmp_cms.keys() if k not in self._scope_base]
        for k in reversed(newids):
            cm = self._tmp_cms.pop(k)
            cm.__exit__(None, None, None)
        self._scope = None

    def psum(self, name, shape, dtype=F32):
        return self.st.enter_context(self.nc.psum_tensor(name, list(shape), dtype))

    def _sem(self, name):
        if name not in self.sems:
            self.sems[name] = self.st.enter_context(self.nc.semaphore(name))
        return self.sems[name]

    def raw(self, eng, fn):
        self.queues[eng].append(fn)

    def ev(self, eng):
        return (self.cur[eng], self.cnt[self.cur[eng]])

    def op(self, eng, fn, n_inc=1, chain=True):
        sem_name = self.cur[eng]
        if chain and eng in ("v", "s") and self.cnt[sem_name] > 0:
            prev = (sem_name, self.cnt[sem_name])
            self.queues[eng].append(
                lambda h, s=prev[0], v=prev[1]: h.wait_ge(self._sem(s), v))
        self.cnt[sem_name] += n_inc
        val = self.cnt[sem_name]

        def wrapped(h, fn=fn, n_inc=n_inc, sem_name=sem_name):
            inst = fn(h)
            assert inst is not None, "op closure must return the last instruction"
            inst.then_inc(self._sem(sem_name), n_inc)

        self.queues[eng].append(wrapped)
        return (sem_name, val)

    def quiet(self, eng, fn):
        self.queues[eng].append(fn)

    def wait(self, eng, ev):
        if ev is None:
            return
        sem_name, val = ev
        if val <= 0:
            return
        self.queues[eng].append(
            lambda h, s=sem_name, v=val: h.wait_ge(self._sem(s), v)
        )

    def barrier(self, engs=("t", "v", "s", "q", "g")):
        evs = {e: (self.cur[e], self.cnt[self.cur[e]]) for e in engs}
        for e in engs:
            for o in engs:
                if (o != e or e in ("q", "g")) and evs[o][1] > 0:
                    self.wait(e, evs[o])

    def dma(self, qeng, out, in_, **kw):
        assert qeng in ("q", "g", "v", "s")
        sem_name = self.cur[qeng]
        self.cnt[sem_name] += 16
        val = self.cnt[sem_name]

        def wrapped(h, out=out, in_=in_, kw=kw, sem_name=sem_name):
            h.dma_start(out=out, in_=in_, **kw).then_inc(self._sem(sem_name), 16)

        self.queues[qeng].append(wrapped)
        return (sem_name, val)

    def collective(self, kind, op, replica_groups, ins, outs):
        name = "cc"
        self.cnt.setdefault(name, 0)
        self.cnt[name] += 1
        val = self.cnt[name]

        def wrapped(h):
            h.collective_compute(kind, op, replica_groups=replica_groups,
                                 ins=ins, outs=outs).then_inc(self._sem(name))

        self.queues["g"].append(wrapped)
        return (name, val)

    def sbuf_tmp(self, name, shape, dtype=F32):
        cm = self.nc.sbuf_tensor(name, list(shape), dtype)
        h = cm.__enter__()
        if not hasattr(self, "_tmp_cms"):
            self._tmp_cms = {}
        self._tmp_cms[id(h)] = cm
        return h

    def sfree(self, h):
        cm = self._tmp_cms.pop(id(h))
        cm.__exit__(None, None, None)

    def constf(self, value, dtype=F32):
        if not hasattr(self, "_consts"):
            self._consts = {}
            self._const_tile = self.sbuf("consts", [128, 32], F32)
            self._const_n = 0
        key = (float(value), dtype)
        if key not in self._consts:
            idx = self._const_n
            assert idx < 32
            self._const_n += 1
            ap = self._const_tile[:, idx:idx + 1]
            self.op("v", lambda v, ap=ap, value=value: v.memset(ap, float(value)))
            self._consts[key] = ap
        return self._consts[key]

    def finalize(self):
        assert not self._finalized
        self._finalized = True
        if hasattr(self, "_tmp_cms"):
            for cm in reversed(list(self._tmp_cms.values())):
                cm.__exit__(None, None, None)
            self._tmp_cms = {}
        nc = self.nc
        with self.st:
            with nc.Block() as block:
                decos = dict(t=block.tensor, v=block.vector, s=block.scalar,
                             q=block.sync, g=block.gpsimd)
                for e in ENGS:
                    if not self.queues[e]:
                        continue
                    def body(h, e=e):
                        for fn in self.queues[e]:
                            fn(h)
                    decos[e](body)
        return nc



def to2d(x):
    x = np.ascontiguousarray(x)
    return x.reshape(-1, x.shape[-1])


def emit_ln(p, x_t, sq, m_t, rstd_t, F, eps=1e-5):
    p.op("v", lambda v: v.tensor_reduce(m_t[:], x_t[:], axis=AX.X, op=OP.add))
    ev_sq = p.op("s", lambda s: s.activation(sq[:, 0:F], x_t[:], AF.Square,
                                             accum_out=rstd_t[:]))
    p.op("v", lambda v: v.tensor_scalar(m_t[:], m_t[:], 1.0 / F, None, OP.mult))
    p.wait("v", ev_sq)
    p.op("v", lambda v: v.tensor_scalar(rstd_t[:], rstd_t[:], 1.0 / F, None, OP.mult))
    p.op("v", lambda v: v.tensor_tensor(sq[:, 0:1], m_t[:], m_t[:], OP.mult))
    p.op("v", lambda v: v.tensor_tensor(rstd_t[:], rstd_t[:], sq[:, 0:1], OP.subtract))
    ev_v = p.op("v", lambda v: v.tensor_copy(sq[:, 1:2], rstd_t[:]))
    p.wait("s", ev_v)
    ev_s = p.op("s", lambda s: s.activation(rstd_t[:], sq[:, 1:2], AF.Sqrt,
                                            bias=p.constf(eps)))
    p.wait("v", ev_s)
    return p.op("v", lambda v: v.reciprocal(rstd_t[:], rstd_t[:]))



D = 1024
SA = 4096
NH = 8
TC = 2048
C1 = (3.0 * D * D) ** (-0.25)
C2 = (D * D) ** (-0.25)
HALO = 768
OFF = HALO
HID = 2730
GROUPS = [[0, 1], [2, 3], [4, 5], [6, 7]]
FF = 4096
NE = 8
R = 4
L = 4096
DI = 2048
DS = 16
DTR = 64


def hid_chunks():
    out = [(c * 128, 128) for c in range(21)]
    out.append((21 * 128, 42))
    return out

def attn_emit(p, io):
    x0b = p.dram("x0b", [SA, D])
    xown = p.dram("xown", [TC, D])
    wq = p.dram("wq", [D, 512])
    wk = p.dram("wk", [D, 512])
    wv = p.dram("wv", [D, 512])
    qb = p.dram("qb", [512])
    pt_d = p.dram("pt", [NH, 64, 256])
    pts_d = p.dram("pts", [NH, 64, 256])
    a1_d = p.dram("a1", [256, SA])
    a2_d = p.dram("a2", [256, SA])
    vtab_d = p.dram("vtab", [256, SA])
    wout_d = p.dram("wout", [D, D])
    outb_d = p.dram("outb", [1, D])
    masks_d = p.dram("masks", [128, 4 * 512], BF16)
    x1out = io['xres1']

    qfm_dr = p.dram("qfm_dr", [512, SA], BF16, kind="Internal")
    kfm_dr = p.dram("kfm_dr", [512, SA], BF16, kind="Internal")
    cin = p.dram("cc_in", [512, TC], BF16, kind="Internal")
    cout = p.dram("cc_out", [1024, TC], BF16, kind="Internal")

    ps0 = io["sps"][:, 0:512]
    ps1 = io["sps"][:, 512:1024]
    ptb0 = io["ptb0"]
    ptb1 = io["ptb1"]
    avp = io["avq"][0:64, 0:128]
    avpB = io["obx"][0:64, 0:128]
    tpb0 = io["tpb0"]
    tpb1 = io["tpb1"]
    ident_d = io["ident_d"]
    flg_d = io["flg_d"]

    ident = p.sbuf("ident_s", [128, 128], BF16)
    masks = p.sbuf("masks_s", [128, 4 * 512], BF16)
    ofm = p.sbuf("ofm", [128, 4 * SA], BF16)
    qbs = p.sbuf("qbs", [128, 4], F32)
    sq = p.sbuf("sq", [128, D], F32)
    mrow = p.sbuf("mrow", [128, 1], F32)
    rrow = p.sbuf("rrow", [128, 1], F32)
    mneg = p.sbuf("mneg", [128, 1], F32)
    rs = p.sbuf("rs", [128, 1], F32)
    rsp = p.sbuf("rsp", [128, 1], F32)
    rinv = p.sbuf("rinv", [128, 1], F32)
    Dg = p.sbuf("Dg", [128, 128], BF16)
    ptsb = p.sbuf("ptsb", [128, 512], BF16)
    p.constf(1e-5)

    vtm = p.sbuf_tmp("vtm", [128, 32 * 512], BF16)
    a1s = p.sbuf_tmp("a1s", [128, 2 * SA], BF16)
    a2s = p.sbuf_tmp("a2s", [128, 2 * SA], BF16)
    vts = p.sbuf_tmp("vts", [128, 2 * SA], BF16)
    prow = p.sbuf_tmp("prow", [128, SA], BF16)
    prowB = p.sbuf_tmp("prowB", [128, SA], BF16)
    ptsbB = p.sbuf_tmp("ptsbB", [128, 512], BF16)
    DgB = p.sbuf_tmp("DgB", [128, 128], BF16)
    mrowB = p.sbuf_tmp("mrowB", [128, 1], F32)
    mnegB = p.sbuf_tmp("mnegB", [128, 1], F32)
    rsB = p.sbuf_tmp("rsB", [128, 1], F32)
    rspB = p.sbuf_tmp("rspB", [128, 1], F32)
    rinvB = p.sbuf_tmp("rinvB", [128, 1], F32)

    wq_s = p.sbuf_tmp("wq_s", [128, 8 * 512], BF16)
    wk_s = p.sbuf_tmp("wk_s", [128, 8 * 512], BF16)
    wv_s = p.sbuf_tmp("wv_s", [128, 8 * 512], BF16)
    xfm = p.sbuf_tmp("xfm", [128, 8 * 512], BF16)
    xt = p.sbuf_tmp("xt", [128, D], F32)
    xn = p.sbuf_tmp("xn", [128, D], BF16)
    est = p.sbuf_tmp("est", [128, 512], BF16)

    for kc in range(8):
        p.dma("g", wq_s[:, kc * 512:(kc + 1) * 512], wq[kc * 128:(kc + 1) * 128, :])
        p.dma("g", wk_s[:, kc * 512:(kc + 1) * 512], wk[kc * 128:(kc + 1) * 128, :])
        p.dma("g", wv_s[:, kc * 512:(kc + 1) * 512], wv[kc * 128:(kc + 1) * 128, :])
    for ch in range(2):
        p.dma("g", a1s[:, ch * SA:(ch + 1) * SA], a1_d[ch * 128:(ch + 1) * 128, :])
        p.dma("g", a2s[:, ch * SA:(ch + 1) * SA], a2_d[ch * 128:(ch + 1) * 128, :])
        p.dma("g", vts[:, ch * SA:(ch + 1) * SA], vtab_d[ch * 128:(ch + 1) * 128, :])
    p.dma("q", ident[:], ident_d)
    p.dma("q", masks[:], masks_d)
    p.dma("q", qbs[:], qb.rearrange("(c p) -> p c", p=128), allow_slow_non_contiguous=True)
    p.op("v", lambda v: v.tensor_scalar(qbs[:], qbs[:], 0.125, None, OP.mult))
    p.barrier()

    for blk in range(8):
        for i in range(4):
            t0 = blk * 512 + i * 128
            p.wait("q", p.ev("v"))
            p.wait("q", p.ev("s"))
            ev_x = p.dma("q", xt[:], x0b[t0:t0 + 128, :])
            p.wait("v", ev_x)
            p.wait("s", ev_x)
            emit_ln(p, xt, sq, mrow, rrow, D)
            p.wait("v", p.ev("t"))
            ev_xn = p.op("v", lambda v: v.tensor_scalar(
                xn[:], xt[:], mrow[:, 0:1], rrow[:, 0:1], OP.subtract, OP.mult))
            ev_tp = {0: None, 1: None}
            for kc in range(8):
                tp = (tpb0, tpb1)[kc % 2]
                p.wait("t", ev_xn)
                p.wait("t", ev_tp[kc % 2])
                ev_t = p.op("t", lambda t, kc=kc, tp=tp: t.transpose(
                    tp[:], xn[:, kc * 128:(kc + 1) * 128], ident[:]))
                p.wait("v", ev_t)
                ev_e = p.op("v", lambda v, tp=tp, kc=kc, i=i: v.tensor_copy(
                    xfm[:, kc * 512 + i * 128: kc * 512 + (i + 1) * 128], tp[:]))
                ev_tp[kc % 2] = ev_e
        p.barrier(("t", "v", "s"))
        ev_est_dma = None
        for mc in range(4):
            for w_s, scale_qk, dst, use_bias in ((wq_s, 0.125 * C1, qfm_dr, True),
                                                 (wk_s, C1, kfm_dr, False)):
                psx = (ps0, ps1)[mc % 2]
                def pmm(t, w_s=w_s, mc=mc, psx=psx):
                    for kc in range(8):
                        inst = t.matmul(psx, w_s[:, kc * 512 + mc * 128: kc * 512 + (mc + 1) * 128],
                                        xfm[:, kc * 512:(kc + 1) * 512],
                                        start=(kc == 0), stop=(kc == 7))
                    return inst
                ev_mm = p.op("t", pmm)
                p.wait("s", ev_mm)
                p.wait("s", ev_est_dma)
                ev_ev = p.op("s", lambda s, psx=psx, scale_qk=scale_qk: s.activation(
                    est[:], psx, AF.Copy, scale=scale_qk))
                if use_bias:
                    p.wait("v", ev_ev)
                    p.wait("v", ev_est_dma)
                    ev_ev = p.op("v", lambda v, mc=mc: v.tensor_scalar(
                        est[:], est[:], qbs[:, mc:mc + 1], None, OP.add))
                    p.wait("q", ev_ev)
                else:
                    p.wait("q", ev_ev)
                ev_est_dma = p.dma("q", dst[mc * 128:(mc + 1) * 128, blk * 512:(blk + 1) * 512], est[:])
                p.wait("t", ev_ev)
        for i in range(4):
            psx = (ps0, ps1)[i % 2]
            def vmm(t, i=i, psx=psx):
                for kc in range(8):
                    inst = t.matmul(psx, xfm[:, kc * 512 + i * 128: kc * 512 + (i + 1) * 128],
                                    wv_s[:, kc * 512:(kc + 1) * 512],
                                    start=(kc == 0), stop=(kc == 7))
                return inst
            ev_mm = p.op("t", vmm)
            p.wait("s", ev_mm)
            ev_vv = p.op("s", lambda s, blk=blk, i=i, psx=psx: s.activation(
                vtm[:, (blk * 4 + i) * 512:(blk * 4 + i + 1) * 512], psx,
                AF.Copy, scale=C1))
            p.wait("t", ev_vv)
        p.barrier(("t", "v", "s"))
    p.barrier()
    p.sfree(est); p.sfree(xn); p.sfree(xt); p.sfree(xfm)
    p.sfree(wv_s); p.sfree(wk_s); p.sfree(wq_s)

    qh = p.sbuf_tmp("qh", [64, SA], BF16)
    kh = p.sbuf_tmp("kh", [64, SA], BF16)
    pth = p.sbuf_tmp("pth", [64, 256], BF16)
    ptsh = p.sbuf_tmp("ptsh", [64, 256], BF16)
    Us = p.sbuf_tmp("Us", [128, 2 * SA], BF16)
    t1s = p.sbuf_tmp("t1s", [128, 512], F32)

    for h in range(NH):
        p.dma("g", qh[:], qfm_dr[h * 64:(h + 1) * 64, :])
        p.dma("g", kh[:], kfm_dr[h * 64:(h + 1) * 64, :])
        p.dma("g", pth[:], pt_d[h])
        p.dma("g", ptsh[:], pts_d[h])
        p.barrier(("t", "v", "s", "g"))
        for ch in range(2):
            for tt in range(8):
                ev_r = p.op("t", lambda t, ch=ch, tt=tt: t.matmul(
                    ps0, pth[:, ch * 128:(ch + 1) * 128],
                    qh[:, tt * 512:(tt + 1) * 512], start=True, stop=True))
                ev_rs2 = p.op("t", lambda t, ch=ch, tt=tt: t.matmul(
                    ps1, ptsh[:, ch * 128:(ch + 1) * 128],
                    qh[:, tt * 512:(tt + 1) * 512], start=True, stop=True))
                p.wait("v", ev_r)
                p.op("v", lambda v, ch=ch, tt=tt: v.tensor_tensor(
                    t1s[:], ps0, a1s[:, ch * SA + tt * 512: ch * SA + (tt + 1) * 512], OP.mult))
                p.wait("v", ev_rs2)
                p.op("v", lambda v, ch=ch, tt=tt: v.tensor_tensor(
                    ptsb[:], ps1, a2s[:, ch * SA + tt * 512: ch * SA + (tt + 1) * 512], OP.mult))
                ev_u = p.op("v", lambda v, ch=ch, tt=tt: v.tensor_tensor(
                    Us[:, ch * SA + tt * 512: ch * SA + (tt + 1) * 512], t1s[:],
                    ptsb[:, 0:512], OP.add))
                p.wait("t", ev_u)
        p.barrier(("t", "v", "s"))
        for qi in range(32):
            dk = (qi * 128) // 512
            def smm(t, qi, kt, psx):
                t.matmul(psx, qh[:, qi * 128:(qi + 1) * 128],
                         kh[:, kt * 512:(kt + 1) * 512], start=True, stop=False)
                t.matmul(psx, Us[:, 0 * SA + qi * 128: 0 * SA + (qi + 1) * 128],
                         vts[:, 0 * SA + kt * 512: 0 * SA + (kt + 1) * 512], start=False, stop=False)
                return t.matmul(psx, Us[:, 1 * SA + qi * 128: 1 * SA + (qi + 1) * 128],
                                vts[:, 1 * SA + kt * 512: 1 * SA + (kt + 1) * 512], start=False, stop=True)
            ev_sd = p.op("t", lambda t, qi=qi, dk=dk: smm(t, qi, dk, ps0))
            p.wait("v", ev_sd)
            p.wait("v", p.ev("s"))
            p.op("v", lambda v: v.tensor_reduce(mrow[:], ps0, axis=AX.X, op=OP.max))
            ev_mn = p.op("v", lambda v: v.tensor_scalar(mneg[:], mrow[:], -1.0, None, OP.mult))
            p.wait("s", ev_mn)
            ev_pd = p.op("s", lambda s, dk=dk: s.activation(
                prow[:, dk * 512:(dk + 1) * 512], ps0, AF.Exp, bias=mneg[:, 0:1]))
            p.wait("v", ev_pd)
            mo = qi % 4
            p.op("v", lambda v, dk=dk, mo=mo: v.tensor_tensor(
                prow[:, dk * 512:(dk + 1) * 512], prow[:, dk * 512:(dk + 1) * 512],
                masks[:, mo * 512:(mo + 1) * 512], OP.mult))
            p.op("v", lambda v, dk=dk: v.tensor_reduce(
                rs[:], prow[:, dk * 512:(dk + 1) * 512], axis=AX.X, op=OP.add))
            ev_exp_prev = ev_pd
            for kt in range(dk):
                psx = (ps0, ps1)[(kt + 1) % 2]
                p.wait("t", ev_exp_prev)
                ev_sk = p.op("t", lambda t, qi=qi, kt=kt, psx=psx: smm(t, qi, kt, psx))
                p.wait("s", ev_sk)
                ev_exp_prev = p.op("s", lambda s, kt=kt, psx=psx: s.activation(
                    prow[:, kt * 512:(kt + 1) * 512], psx, AF.Exp, bias=mneg[:, 0:1],
                    accum_out=rsp[:]))
                p.wait("v", ev_exp_prev)
                p.op("v", lambda v: v.tensor_tensor(rs[:], rs[:], rsp[:], OP.add))
            p.op("v", lambda v: v.reciprocal(rinv[:], rs[:]))
            ev_d = p.op("v", lambda v: v.tensor_scalar(Dg[:], ident[:], rinv[:, 0:1],
                                                       None, OP.mult))
            p.wait("t", ev_d)
            nsub = (dk + 1) * 4
            ev_pt_evac = {0: None, 1: None}
            ev_av = None
            for g in range((nsub + 3) // 4):
                ptp = (ptb0, ptb1)[g % 2]
                p.wait("t", ev_pt_evac[g % 2])
                ngrp = min(4, nsub - g * 4)
                def ptmm(t, g=g, ngrp=ngrp, ptp=ptp):
                    for jj in range(ngrp):
                        inst = t.matmul(ptp[:, jj * 128:(jj + 1) * 128],
                                        prow[:, (g * 4 + jj) * 128:(g * 4 + jj + 1) * 128],
                                        Dg[:], start=True, stop=True)
                    return inst
                ev_ptm = p.op("t", ptmm)
                eng = ("v", "s")[g % 2]
                p.wait(eng, ev_ptm)
                p.wait(eng, ev_av)
                if eng == "v":
                    ev_ev = p.op("v", lambda v, ngrp=ngrp, ptp=ptp: v.tensor_copy(
                        ptsb[:, 0:ngrp * 128], ptp[:, 0:ngrp * 128]))
                else:
                    ev_ev = p.op("s", lambda s, ngrp=ngrp, ptp=ptp: s.copy(
                        ptsb[:, 0:ngrp * 128], ptp[:, 0:ngrp * 128]))
                ev_pt_evac[g % 2] = ev_ev
                p.wait("t", ev_ev)
                def avmm(t, g=g, ngrp=ngrp, h=h, nsub=nsub):
                    for jj in range(ngrp):
                        ks = g * 4 + jj
                        inst = t.matmul(avp, vtm[:, ks * 512 + h * 64: ks * 512 + (h + 1) * 64],
                                        ptsb[:, jj * 128:(jj + 1) * 128],
                                        start=(g == 0 and jj == 0),
                                        stop=(g * 4 + jj == nsub - 1))
                    return inst
                ev_av = p.op("t", avmm)
            pairc = (h // 2) * SA
            base = 64 * (h % 2)
            p.wait("s", ev_av)
            ev_o = p.op("s", lambda s, pairc=pairc, base=base, qi=qi: s.activation(
                ofm[base:base + 64, pairc + qi * 128: pairc + (qi + 1) * 128],
                avp, AF.Copy, scale=C2))
            p.wait("t", ev_o)
        p.barrier(("t", "v", "s", "g"))
    p.sfree(t1s); p.sfree(Us); p.sfree(ptsh); p.sfree(pth); p.sfree(kh); p.sfree(qh)
    p.sfree(rinvB); p.sfree(rspB); p.sfree(rsB); p.sfree(mnegB); p.sfree(mrowB)
    p.sfree(DgB); p.sfree(ptsbB); p.sfree(prowB)
    p.sfree(prow); p.sfree(vts); p.sfree(a2s); p.sfree(a1s); p.sfree(vtm)

    p.barrier()
    flagb = p.sbuf_tmp("flagb", [128, 1], F32)
    finv = p.sbuf_tmp("finv", [128, 1], F32)
    selst = p.sbuf_tmp("selst", [128, TC], BF16)
    sel2 = p.sbuf_tmp("sel2", [128, TC], BF16)
    ownloc = p.sbuf_tmp("ownloc", [128, 4 * TC], BF16)
    p.dma("q", flagb[:], flg_d[0].partition_broadcast(128))
    p.wait("v", p.ev("q"))
    p.op("v", lambda v: v.tensor_scalar(finv[:], flagb[:], -1.0, 1.0, OP.mult, OP.add))
    for pr in range(4):
        p.op("v", lambda v, pr=pr: v.tensor_scalar(
            selst[:], ofm[:, pr * SA: pr * SA + TC], flagb[:, 0:1], None, OP.mult))
        p.op("v", lambda v, pr=pr: v.tensor_scalar(
            sel2[:], ofm[:, pr * SA + TC: pr * SA + 2 * TC], finv[:, 0:1], None, OP.mult))
        ev_c = p.op("v", lambda v: v.tensor_tensor(selst[:], selst[:], sel2[:], OP.add))
        p.wait("q", ev_c)
        ev_cind = p.dma("q", cin[pr * 128:(pr + 1) * 128, :], selst[:])
        p.wait("v", ev_cind)
        p.op("v", lambda v, pr=pr: v.tensor_scalar(
            selst[:], ofm[:, pr * SA: pr * SA + TC], finv[:, 0:1], None, OP.mult))
        p.op("v", lambda v, pr=pr: v.tensor_scalar(
            sel2[:], ofm[:, pr * SA + TC: pr * SA + 2 * TC], flagb[:, 0:1], None, OP.mult))
        ev_o = p.op("v", lambda v, pr=pr: v.tensor_tensor(
            ownloc[:, pr * TC:(pr + 1) * TC], selst[:], sel2[:], OP.add))
    p.wait("g", p.ev("q"))
    p.wait("g", p.ev("v"))
    ev_cc = p.collective("AllGather", mybir.AluOpType.bypass,
                         [[0, 1], [2, 3], [4, 5], [6, 7]], [cin], [cout])
    p.wait("q", ev_cc)

    wout_s = p.sbuf_tmp("wout_s", [128, 8 * D], BF16)
    ofull = p.sbuf_tmp("ofull", [128, 8 * TC], BF16)
    obbc = p.sbuf_tmp("obbc", [128, D], F32)
    yt = p.sbuf_tmp("yt", [128, D], F32)
    xt2 = p.sbuf_tmp("xt2", [128, D], F32)
    for kc in range(8):
        p.dma("g", wout_s[:, kc * D:(kc + 1) * D], wout_d[kc * 128:(kc + 1) * 128, :])
    p.dma("g", obbc[:], outb_d[0].partition_broadcast(128))
    for c in range(8):
        p.wait("q", p.ev("v"))
        ev_l = p.dma("q", sel2[:], cout[c * 128:(c + 1) * 128, :])
        wl, wc = (finv, flagb) if c < 4 else (flagb, finv)
        p.op("v", lambda v, c=c, wl=wl: v.tensor_scalar(
            selst[:], ownloc[:, (c % 4) * TC:(c % 4 + 1) * TC], wl[:, 0:1], None, OP.mult))
        p.wait("v", ev_l)
        p.op("v", lambda v, c=c, wc=wc: v.tensor_scalar(
            sel2[:], sel2[:], wc[:, 0:1], None, OP.mult))
        p.op("v", lambda v, c=c: v.tensor_tensor(
            ofull[:, c * TC:(c + 1) * TC], selst[:], sel2[:], OP.add))
    p.barrier()
    for ti in range(16):
        ev_x = p.dma("q", xt2[:], xown[ti * 128:(ti + 1) * 128, :])
        for nh2 in range(2):
            psx = (ps0, ps1)[nh2]
            def omm(t, ti=ti, nh2=nh2, psx=psx):
                for kc in range(8):
                    inst = t.matmul(psx, ofull[:, kc * TC + ti * 128: kc * TC + (ti + 1) * 128],
                                    wout_s[:, kc * D + nh2 * 512: kc * D + (nh2 + 1) * 512],
                                    start=(kc == 0), stop=(kc == 7))
                return inst
            ev_mm = p.op("t", omm)
            p.wait("s", ev_mm)
            ev_y = p.op("s", lambda s, nh2=nh2, psx=psx: s.copy(
                yt[:, nh2 * 512:(nh2 + 1) * 512], psx))
            p.wait("t", ev_y)
        p.wait("v", p.ev("s"))
        p.op("v", lambda v: v.tensor_tensor(yt[:], yt[:], obbc[:], OP.add))
        p.wait("s", p.ev("v"))
        emit_ln(p, yt, sq, mrow, rrow, D)
        p.op("v", lambda v: v.tensor_scalar(yt[:], yt[:], mrow[:, 0:1], rrow[:, 0:1],
                                            OP.subtract, OP.mult))
        p.wait("v", ev_x)
        ev_res = p.op("v", lambda v: v.tensor_tensor(yt[:], yt[:], xt2[:], OP.add))
        p.wait("q", ev_res)
        ev_st = p.dma("q", x1out[ti * 128:(ti + 1) * 128, :], yt[:])
        p.wait("s", ev_st)
        p.wait("v", ev_st)
    p.wait("q", p.ev("q"))


def ffn_emit(p, io):
    x1own = io['xres1']
    x1halo = p.dram('x1halo_dr', [HALO, D], kind='Internal')
    wqkv_d = p.dram("wqkv", [2, D, 1536])
    qs_d = p.dram("qs", [2, 1, 512])
    ks_d = p.dram("ks", [2, 1, 512])
    wo_d = p.dram("wo", [2, 512, D])
    lg_d = p.dram("lg", [2, 1, D])
    lb_d = p.dram("lb", [2, 1, D])
    w1_d = p.dram("w1", [2, D, 2 * HID])
    b1_d = p.dram("b1", [2, 2 * HID])
    w2_d = p.dram("w2", [2, HID, D])
    b2_d = p.dram("b2", [2, 1, D])
    aq_d = p.dram("aq", [TC, 512])
    bq_d = p.dram("bq", [TC, 512])
    ak_d = p.dram("ak", [512 + TC, 512])
    bk_d = p.dram("bk", [512 + TC, 512])
    masks_d = p.dram("masksd", [128, 4 * 512], BF16)
    masksp_d = p.dram("masksp", [128, 4 * 512], BF16)
    xout = io['xres2']

    yfull = [p.dram(f"yfull{l}", [HALO + TC, D], kind="Internal") for l in range(2)]
    ymid = [p.dram(f"ymid{l}", [TC + 1, D], kind="Internal") for l in range(2)]
    cin_m = [p.dram(f"cinm{l}", [128, D], kind="Internal") for l in range(2)]
    cout_m = [p.dram(f"coutm{l}", [256, D], kind="Internal") for l in range(2)]
    cin_e = p.dram("cine", [HALO, D], kind="Internal")
    cout_e = p.dram("coute", [2 * HALO, D], kind="Internal")

    sps = io["sps"]
    ps0 = sps[:, 0:512]
    ps1 = sps[:, 512:1024]
    ptb0 = io["ptb0"]
    ptb1 = io["ptb1"]
    avp = io["avq"][0:64, 0:128]
    tpb0 = io["tpb0"]
    tpb1 = io["tpb1"]
    ident_d = io["ident_d"]
    flg_d = io["flg_d"]

    ident = p.sbuf("f_ident_s", [128, 128], BF16)
    flagb = p.sbuf("f_flagb", [128, 1], F32)
    sq = p.sbuf("f_sq", [128, D], F32)
    mrow = p.sbuf("f_mrow", [128, 1], F32)
    rrow = p.sbuf("f_rrow", [128, 1], F32)
    mneg = p.sbuf("f_mneg", [128, 1], F32)
    rs = p.sbuf("f_rs", [128, 1], F32)
    rsp = p.sbuf("f_rsp", [128, 8], F32)
    rinv = p.sbuf("f_rinv", [128, 1], F32)
    Dg = p.sbuf("f_Dg", [128, 128], BF16)
    prow = p.sbuf("f_prow", [128, 1024], BF16)
    ptsb = p.sbuf("f_ptsb", [128, 512], BF16)
    ht = p.sbuf("f_ht", [128, D], F32)
    htb = p.sbuf("f_htb", [128, D], BF16)
    hfm = p.sbuf("f_hfm", [128, 8 * 512], BF16)
    p.constf(1e-5)

    p.dma("q", ident[:], ident_d)
    p.dma("q", flagb[:], flg_d[0].partition_broadcast(128))
    p.barrier()

    cin_x1 = p.dram("cinx1", [HALO, D], kind="Internal")
    cout_x1 = p.dram("coutx1", [2 * HALO, D], kind="Internal")
    p.dma("q", cin_x1, x1own[TC - HALO:TC, :])
    p.wait("g", p.ev("q"))
    ev_cc0 = p.collective("AllGather", mybir.AluOpType.bypass, GROUPS,
                          [cin_x1], [cout_x1])
    p.wait("q", ev_cc0)
    for i in range(HALO // 128):
        ev_b0 = p.dma("q", ht[:], cout_x1[i * 128:(i + 1) * 128, :])
        p.wait("v", ev_b0)
        ev_bb0 = p.op("v", lambda v: v.tensor_scalar(ht[:], ht[:], flagb[:, 0:1],
                                                     None, OP.mult))
        p.wait("q", ev_bb0)
        ev_sx = p.dma("q", x1halo[i * 128:(i + 1) * 128, :], ht[:])
        p.wait("v", ev_sx)
    p.barrier()
    for r0, src, n in ((0, x1halo, HALO // 128), (OFF, x1own, TC // 128)):
        for i in range(n):
            p.wait("q", p.ev("v")); p.wait("q", p.ev("s"))
            ev_x = p.dma("q", ht[:], src[i * 128:(i + 1) * 128, :])
            p.wait("v", ev_x); p.wait("s", ev_x)
            emit_ln(p, ht, sq, mrow, rrow, D)
            ev_y = p.op("v", lambda v: v.tensor_scalar(
                ht[:], ht[:], mrow[:, 0:1], rrow[:, 0:1], OP.subtract, OP.mult))
            p.wait("q", ev_y)
            p.dma("q", yfull[0][r0 + i * 128: r0 + (i + 1) * 128, :], ht[:])
    p.barrier()

    for l in range(2):
        yf = yfull[l]
        masks = p.sbuf_tmp(f"masks_s{l}", [128, 4 * 512], BF16)
        masksp = p.sbuf_tmp(f"maskspg_s{l}", [128, 4 * 512], BF16)
        masksp0 = p.sbuf_tmp(f"masksp0_s{l}", [128, 4 * 512], BF16)
        p.dma("q", masks[:], masks_d)
        p.dma("q", masksp0[:], masksp_d)
        p.wait("v", p.ev("q"))
        for o in range(4):
            p.op("v", lambda v, o=o: v.tensor_scalar(
                masksp[:, o * 512:(o + 1) * 512], masksp0[:, o * 512:(o + 1) * 512],
                flagb[:, 0:1], None, OP.mult))
        wqkv_s = p.sbuf_tmp(f"wqkv_s{l}", [128, 8 * 1536], BF16)
        qtm = p.sbuf_tmp(f"qtm{l}", [128, 16 * 512], BF16)
        ktm = p.sbuf_tmp(f"ktm{l}", [128, 20 * 512], BF16)
        vtm = p.sbuf_tmp(f"vtm{l}", [128, 20 * 512], BF16)
        qfm = p.sbuf_tmp(f"qfm{l}", [128, 4 * TC], BF16)
        kfm = p.sbuf_tmp(f"kfm{l}", [128, 4 * 2560], BF16)
        tabs = p.sbuf_tmp(f"tabs{l}", [128, 2 * 512], BF16)
        qs_s = p.sbuf_tmp(f"qs_s{l}", [128, 512], F32)
        ks_s = p.sbuf_tmp(f"ks_s{l}", [128, 512], F32)
        nrm = p.sbuf_tmp(f"nrm{l}", [128, 8], F32)
        rot = p.sbuf_tmp(f"rot{l}", [128, 512], BF16)
        t3 = p.sbuf_tmp(f"t3{l}", [128, 512], BF16)

        for kc in range(8):
            p.dma("g", wqkv_s[:, kc * 1536:(kc + 1) * 1536],
                  wqkv_d[l, kc * 128:(kc + 1) * 128, :])
        p.dma("g", qs_s[:], qs_d[l, 0].partition_broadcast(128))
        p.dma("g", ks_s[:], ks_d[l, 0].partition_broadcast(128))
        p.barrier()

        for bi in range(5):
            r0 = OFF - 512 + bi * 512
            for i in range(4):
                rr = r0 + i * 128
                p.wait("q", p.ev("v"))
                ev1 = p.dma("q", ht[:, 0:512], yf[rr:rr + 128, 0:512])
                ev2 = p.dma("q", ht[:, 512:D], yf[rr - 1:rr + 127, 512:D])
                p.wait("v", ev2)
                p.wait("v", p.ev("t"))
                ev_hb = p.op("v", lambda v: v.tensor_copy(htb[:], ht[:]))
                ev_tp = {0: None, 1: None}
                for kc in range(8):
                    tp = (tpb0, tpb1)[kc % 2]
                    p.wait("t", ev_hb)
                    p.wait("t", ev_tp[kc % 2])
                    ev_t = p.op("t", lambda t, kc=kc, tp=tp: t.transpose(
                        tp[:], htb[:, kc * 128:(kc + 1) * 128], ident[:]))
                    p.wait("v", ev_t)
                    ev_e = p.op("v", lambda v, tp=tp, kc=kc, i=i: v.tensor_copy(
                        hfm[:, kc * 512 + i * 128: kc * 512 + (i + 1) * 128], tp[:]))
                    ev_tp[kc % 2] = ev_e
            p.barrier(("t", "v", "s"))
            parts = [("k", 512, ktm, bi), ("v", 1024, vtm, bi)]
            if bi >= 1:
                parts.append(("q", 0, qtm, bi - 1))
            for name, coff, dstm, bslot in parts:
                for i in range(4):
                    psx = (ps0, ps1)[i % 2]
                    def qmm(t, coff=coff, i=i, psx=psx):
                        for kc in range(8):
                            inst = t.matmul(psx, hfm[:, kc * 512 + i * 128: kc * 512 + (i + 1) * 128],
                                            wqkv_s[:, kc * 1536 + coff: kc * 1536 + coff + 512],
                                            start=(kc == 0), stop=(kc == 7))
                        return inst
                    ev_mm = p.op("t", qmm)
                    p.wait("s", ev_mm)
                    ev_ev = p.op("s", lambda s, dstm=dstm, bslot=bslot, i=i, psx=psx: s.activation(
                        dstm[:, (bslot * 4 + i) * 512:(bslot * 4 + i + 1) * 512], psx, AF.Copy))
                    p.wait("t", ev_ev)
            p.barrier(("t", "v", "s"))
        for which, tm, fm, ntile, tab_a, tab_b, sc_s, scale8, fmw in (
                ("q", qtm, qfm, 16, aq_d, bq_d, qs_s, 8.0, TC),
                ("k", ktm, kfm, 20, ak_d, bk_d, ks_s, 1.0, 2560)):
            for ti in range(ntile):
                sl = slice(ti * 512, (ti + 1) * 512)
                p.wait("g", p.ev("v"))
                p.dma("g", tabs[:, 0:512], tab_a[ti * 128:(ti + 1) * 128, :])
                ev_t2 = p.dma("g", tabs[:, 512:1024], tab_b[ti * 128:(ti + 1) * 128, :])
                ev_sqq = p.op("s", lambda s, tm=tm, sl=sl: s.activation(
                    sq[:, 0:512], tm[:, sl], AF.Square))
                p.wait("v", ev_sqq)
                for e in range(8):
                    p.op("v", lambda v, e=e: v.tensor_reduce(
                        nrm[:, e:e + 1], sq[:, e * 64:(e + 1) * 64],
                        axis=AX.X, op=OP.add, apply_absolute_value=None, negate=None))
                ev_n = p.op("v", lambda v: v.tensor_copy(rsp[:], nrm[:]))
                p.wait("s", ev_n)
                ev_sqn = p.op("s", lambda s: s.activation(nrm[:], rsp[:], AF.Sqrt))
                p.wait("v", ev_sqn)
                p.op("v", lambda v: v.tensor_scalar(nrm[:], nrm[:], 1e-12, None, OP.max))
                p.op("v", lambda v: v.reciprocal(nrm[:], nrm[:]))
                for e in range(8):
                    p.op("v", lambda v, tm=tm, sl=sl, e=e: v.tensor_scalar(
                        tm[:, sl][:, e * 64:(e + 1) * 64], tm[:, sl][:, e * 64:(e + 1) * 64],
                        nrm[:, e:e + 1], None, OP.mult))
                p.op("v", lambda v, tm=tm, sl=sl, sc_s=sc_s: v.tensor_tensor(
                    tm[:, sl], tm[:, sl], sc_s[:, 0:512], OP.mult))
                tv2 = tm[:, sl].rearrange("p (e two d) -> p e two d", two=2, d=32)
                rv = rot[:].rearrange("p (e two d) -> p e two d", two=2, d=32)
                p.op("v", lambda v, tv2=tv2, rv=rv: v.tensor_scalar(
                    rv[:, :, 0, :], tv2[:, :, 1, :], -1.0, None, OP.mult))
                p.op("v", lambda v, tv2=tv2, rv=rv: v.tensor_copy(rv[:, :, 1, :], tv2[:, :, 0, :]))
                p.wait("v", ev_t2)
                p.op("v", lambda v, tm=tm, sl=sl: v.tensor_tensor(
                    t3[:], tm[:, sl], tabs[:, 0:512], OP.mult))
                p.op("v", lambda v: v.tensor_tensor(rot[:], rot[:], tabs[:, 512:1024], OP.mult))
                ev_q = p.op("v", lambda v: v.tensor_tensor(t3[:], t3[:], rot[:], OP.add))
                if scale8 != 1.0:
                    ev_q = p.op("v", lambda v, scale8=scale8: v.tensor_scalar(
                        t3[:], t3[:], scale8, None, OP.mult))
                ev_tp = {0: None, 1: None}
                for c4 in range(4):
                    tp = (tpb0, tpb1)[c4 % 2]
                    p.wait("t", ev_q)
                    p.wait("t", ev_tp[c4 % 2])
                    ev_t = p.op("t", lambda t, c4=c4, tp=tp: t.transpose(
                        tp[:], t3[:, c4 * 128:(c4 + 1) * 128], ident[:]))
                    p.wait("v", ev_t)
                    ev_e = p.op("v", lambda v, c4=c4, tp=tp, fm=fm, ti=ti, fmw=fmw: v.tensor_copy(
                        fm[:, c4 * fmw + ti * 128: c4 * fmw + (ti + 1) * 128], tp[:]))
                    ev_tp[c4 % 2] = ev_e
        p.barrier(("t", "v", "s"))

        ofm = p.sbuf_tmp(f"ofm{l}", [128, 4 * TC], BF16)
        for h in range(NH):
            qbase = 64 * (h % 2)
            qcol = (h // 2)
            for qt in range(16):
                wl = qt // 4
                o4 = qt % 4
                def smm2(t, h=h, qt=qt, wl=wl, qbase=qbase, qcol=qcol):
                    for half in range(2):
                        inst = t.matmul(
                            sps[:, half * 512:(half + 1) * 512],
                            qfm[qbase:qbase + 64, qcol * TC + qt * 128: qcol * TC + (qt + 1) * 128],
                            kfm[qbase:qbase + 64,
                                qcol * 2560 + wl * 512 + half * 512:
                                qcol * 2560 + wl * 512 + (half + 1) * 512],
                            start=True, stop=True)
                    return inst
                p.wait("t", p.ev("s"))
                ev_s2 = p.op("t", smm2)
                p.wait("v", ev_s2)
                p.wait("v", p.ev("s"))
                p.op("v", lambda v: v.tensor_reduce(mrow[:], sps[:], axis=AX.X, op=OP.max))
                ev_mn = p.op("v", lambda v: v.tensor_scalar(mneg[:], mrow[:], -1.0, None, OP.mult))
                p.wait("s", ev_mn)
                ev_pd = p.op("s", lambda s: s.activation(prow[:], sps[:], AF.Exp, bias=mneg[:, 0:1]))
                p.wait("v", ev_pd)
                mpick = masksp if wl == 0 else masksp0
                p.op("v", lambda v, o4=o4, mpick=mpick: v.tensor_tensor(
                    prow[:, 0:512], prow[:, 0:512], mpick[:, o4 * 512:(o4 + 1) * 512], OP.mult))
                p.op("v", lambda v, o4=o4: v.tensor_tensor(
                    prow[:, 512:1024], prow[:, 512:1024], masks[:, o4 * 512:(o4 + 1) * 512], OP.mult))
                p.op("v", lambda v: v.tensor_reduce(rs[:], prow[:], axis=AX.X, op=OP.add))
                p.op("v", lambda v: v.reciprocal(rinv[:], rs[:]))
                ev_d = p.op("v", lambda v: v.tensor_scalar(Dg[:], ident[:], rinv[:, 0:1], None, OP.mult))
                p.wait("t", ev_d)
                ev_av = None
                ev_pt_evac = {0: None, 1: None}
                for g in range(2):
                    ptp = (ptb0, ptb1)[g % 2]
                    p.wait("t", ev_pt_evac[g % 2])
                    def ptmm(t, g=g, ptp=ptp):
                        for jj in range(4):
                            inst = t.matmul(ptp[:, jj * 128:(jj + 1) * 128],
                                            prow[:, (g * 4 + jj) * 128:(g * 4 + jj + 1) * 128],
                                            Dg[:], start=True, stop=True)
                        return inst
                    ev_ptm = p.op("t", ptmm)
                    eng = ("v", "s")[g % 2]
                    p.wait(eng, ev_ptm)
                    p.wait(eng, ev_av)
                    if eng == "v":
                        ev_ev = p.op("v", lambda v, ptp=ptp: v.tensor_copy(ptsb[:], ptp[:]))
                    else:
                        ev_ev = p.op("s", lambda s, ptp=ptp: s.copy(ptsb[:], ptp[:]))
                    ev_pt_evac[g % 2] = ev_ev
                    p.wait("t", ev_ev)
                    def avmm2(t, g=g, h=h, wl=wl):
                        for jj in range(4):
                            j = wl * 4 + g * 4 + jj
                            inst = t.matmul(avp,
                                            vtm[:, j * 512 + h * 64: j * 512 + (h + 1) * 64],
                                            ptsb[:, jj * 128:(jj + 1) * 128],
                                            start=(g == 0 and jj == 0),
                                            stop=(g == 1 and jj == 3))
                        return inst
                    ev_av = p.op("t", avmm2)
                p.wait("s", ev_av)
                ev_o = p.op("s", lambda s, h=h, qt=qt: s.activation(
                    ofm[64 * (h % 2):64 * (h % 2) + 64, (h // 2) * TC + qt * 128:(h // 2) * TC + (qt + 1) * 128],
                    avp, AF.Copy))
                p.wait("t", ev_o)
        p.barrier(("t", "v", "s"))

        wo_s = p.sbuf_tmp(f"wo_s{l}", [128, 4 * D], BF16)
        yt = p.sbuf_tmp(f"yt{l}", [128, D], F32)
        for kc in range(4):
            p.dma("g", wo_s[:, kc * D:(kc + 1) * D], wo_d[l, kc * 128:(kc + 1) * 128, :])
        p.barrier(("t", "v", "s", "g", "q"))
        ev_prev_store = None
        for ti in range(16):
            p.wait("q", p.ev("v"))
            ev_x = p.dma("q", ht[:], yf[OFF + ti * 128: OFF + (ti + 1) * 128, :])
            p.wait("s", ev_prev_store)
            for nh2 in range(2):
                psx = (ps0, ps1)[nh2]
                def omm(t, ti=ti, nh2=nh2, psx=psx):
                    for kc in range(4):
                        inst = t.matmul(psx, ofm[:, kc * TC + ti * 128: kc * TC + (ti + 1) * 128],
                                        wo_s[:, kc * D + nh2 * 512: kc * D + (nh2 + 1) * 512],
                                        start=(kc == 0), stop=(kc == 3))
                    return inst
                ev_mm = p.op("t", omm)
                p.wait("s", ev_mm)
                ev_y = p.op("s", lambda s, nh2=nh2, psx=psx: s.copy(
                    yt[:, nh2 * 512:(nh2 + 1) * 512], psx))
                p.wait("t", ev_y)
            p.wait("v", p.ev("s"))
            p.wait("v", ev_x)
            ev_r = p.op("v", lambda v: v.tensor_tensor(yt[:], yt[:], ht[:], OP.add))
            p.wait("q", ev_r)
            ev_prev_store = p.dma("q", ymid[l][1 + ti * 128: 1 + (ti + 1) * 128, :], yt[:])
        p.barrier()
        p.dma("q", cin_m[l], ymid[l][1 + 1920: 1 + 2048, :])
        p.wait("g", p.ev("q"))
        ev_cc = p.collective("AllGather", mybir.AluOpType.bypass, GROUPS,
                             [cin_m[l]], [cout_m[l]])
        p.wait("q", ev_cc)
        ev_b = p.dma("q", ht[0:1, :], cout_m[l][127:128, :])
        p.wait("v", ev_b)
        ev_bb = p.op("v", lambda v: v.tensor_scalar(ht[0:1, :], ht[0:1, :],
                                                    flagb[0:1, 0:1], None, OP.mult))
        p.wait("q", ev_bb)
        p.dma("q", ymid[l][0:1, :], ht[0:1, :])
        p.sfree(yt); p.sfree(wo_s); p.sfree(ofm)
        p.sfree(t3); p.sfree(rot); p.sfree(nrm); p.sfree(ks_s); p.sfree(qs_s)
        p.sfree(tabs); p.sfree(kfm); p.sfree(qfm)
        p.sfree(vtm); p.sfree(ktm); p.sfree(qtm); p.sfree(wqkv_s)
        p.sfree(masksp0); p.sfree(masksp); p.sfree(masks)
        p.barrier()

        w1_s = p.sbuf_tmp(f"w1_s{l}", [128, 8 * 2 * HID], BF16)
        b1c = p.sbuf_tmp(f"b1c{l}", [128, 44], F32)
        lgb = p.sbuf_tmp(f"lgb{l}", [128, D], F32)
        lbb = p.sbuf_tmp(f"lbb{l}", [128, D], F32)
        b2b = p.sbuf_tmp(f"b2b{l}", [128, D], F32)
        w2_s = p.sbuf_tmp(f"w2_s{l}", [128, 22 * D], BF16)
        xfm2 = p.sbuf_tmp(f"xfm2{l}", [128, 8 * 512], BF16)
        prodfm = p.sbuf_tmp(f"prodfm{l}", [128, 22 * 512], BF16)
        yt2 = p.sbuf_tmp(f"yt2{l}", [128, D], F32)
        for kc in range(8):
            p.dma("g", w1_s[:, kc * 2 * HID:(kc + 1) * 2 * HID],
                  w1_d[l, kc * 128:(kc + 1) * 128, :])
        for ci, (f0, fsz) in enumerate(hid_chunks()):
            p.dma("g", w2_s[0:fsz, ci * D:(ci + 1) * D], w2_d[l, f0:f0 + fsz, :])
        p.dma("g", lgb[:], lg_d[l, 0].partition_broadcast(128))
        p.dma("g", lbb[:], lb_d[l, 0].partition_broadcast(128))
        p.dma("g", b2b[:], b2_d[l, 0].partition_broadcast(128))
        for ci, (f0, fsz) in enumerate(hid_chunks()):
            p.dma("q", b1c[0:fsz, ci:ci + 1], b1_d[l, f0:f0 + fsz].rearrange("(f o) -> f o", o=1),
                  allow_slow_non_contiguous=True)
            p.dma("q", b1c[0:fsz, ci + 22:ci + 23],
                  b1_d[l, HID + f0: HID + f0 + fsz].rearrange("(f o) -> f o", o=1),
                  allow_slow_non_contiguous=True)
        p.barrier()
        for blk in range(4):
            for i in range(4):
                rr = blk * 512 + i * 128
                p.wait("q", p.ev("v")); p.wait("q", p.ev("s"))
                ev1 = p.dma("q", ht[:, 0:512], ymid[l][1 + rr: 1 + rr + 128, 0:512])
                ev2 = p.dma("q", ht[:, 512:D], ymid[l][rr: rr + 128, 512:D])
                p.wait("v", ev2); p.wait("s", ev2)
                emit_ln(p, ht, sq, mrow, rrow, D)
                p.op("v", lambda v: v.tensor_scalar(
                    ht[:], ht[:], mrow[:, 0:1], rrow[:, 0:1], OP.subtract, OP.mult))
                p.op("v", lambda v: v.tensor_tensor(ht[:], ht[:], lgb[:], OP.mult))
                p.wait("v", p.ev("t"))
                ev_xn = p.op("v", lambda v: v.tensor_tensor(htb[:], ht[:], lbb[:], OP.add))
                ev_tp = {0: None, 1: None}
                for kc in range(8):
                    tp = (tpb0, tpb1)[kc % 2]
                    p.wait("t", ev_xn)
                    p.wait("t", ev_tp[kc % 2])
                    ev_t = p.op("t", lambda t, kc=kc, tp=tp: t.transpose(
                        tp[:], htb[:, kc * 128:(kc + 1) * 128], ident[:]))
                    p.wait("v", ev_t)
                    ev_e = p.op("v", lambda v, tp=tp, kc=kc, i=i: v.tensor_copy(
                        xfm2[:, kc * 512 + i * 128: kc * 512 + (i + 1) * 128], tp[:]))
                    ev_tp[kc % 2] = ev_e
            p.barrier(("t", "v", "s"))
            for ci, (f0, fsz) in enumerate(hid_chunks()):
                def gmm(t, f0=f0, fsz=fsz):
                    for kc in range(8):
                        inst = t.matmul(sps[0:fsz, 0:512],
                                        w1_s[:, kc * 2 * HID + HID + f0: kc * 2 * HID + HID + f0 + fsz],
                                        xfm2[:, kc * 512:(kc + 1) * 512],
                                        start=(kc == 0), stop=(kc == 7))
                    return inst
                p.wait("t", p.ev("s"))
                p.wait("t", p.ev("v"))
                ev_g = p.op("t", gmm)
                p.wait("s", ev_g)
                ev_gel = p.op("s", lambda s, fsz=fsz, ci=ci: s.activation(
                    ptsb[0:fsz, :], sps[0:fsz, 0:512], AF.Gelu, bias=b1c[0:fsz, ci + 22:ci + 23]))
                def amm(t, f0=f0, fsz=fsz):
                    for kc in range(8):
                        inst = t.matmul(sps[0:fsz, 512:1024],
                                        w1_s[:, kc * 2 * HID + f0: kc * 2 * HID + f0 + fsz],
                                        xfm2[:, kc * 512:(kc + 1) * 512],
                                        start=(kc == 0), stop=(kc == 7))
                    return inst
                ev_a = p.op("t", amm)
                p.wait("v", ev_a)
                p.wait("v", ev_gel)
                p.op("v", lambda v, fsz=fsz, ci=ci: v.scalar_tensor_tensor(
                    prodfm[0:fsz, ci * 512:(ci + 1) * 512], sps[0:fsz, 512:1024],
                    b1c[0:fsz, ci:ci + 1], ptsb[0:fsz, :], OP.add, OP.mult))
            p.barrier(("t", "v", "s"))
            for i in range(4):
                rr = blk * 512 + i * 128
                p.wait("q", p.ev("v"))
                ev_x = p.dma("q", ht[:], ymid[l][1 + rr: 1 + rr + 128, :])
                p.wait("s", p.ev("q"))
                for nh2 in range(2):
                    psx = (ps0, ps1)[nh2]
                    ev_w2l = None
                    def w2mm(t, i=i, nh2=nh2, psx=psx):
                        for ci, (f0, fsz) in enumerate(hid_chunks()):
                            inst = t.matmul(psx,
                                            prodfm[0:fsz, ci * 512 + i * 128: ci * 512 + (i + 1) * 128],
                                            w2_s[0:fsz, ci * D + nh2 * 512: ci * D + (nh2 + 1) * 512],
                                            start=(ci == 0), stop=(ci == 21))
                        return inst
                    ev_mm = p.op("t", w2mm)
                    p.wait("s", ev_mm)
                    ev_y = p.op("s", lambda s, nh2=nh2, psx=psx: s.copy(
                        yt2[:, nh2 * 512:(nh2 + 1) * 512], psx))
                    p.wait("t", ev_y)
                p.wait("v", p.ev("s"))
                p.wait("v", ev_x)
                p.op("v", lambda v: v.tensor_tensor(yt2[:], yt2[:], b2b[:], OP.add))
                ev_r = p.op("v", lambda v: v.tensor_tensor(yt2[:], yt2[:], ht[:], OP.add))
                if l == 0:
                    p.wait("q", ev_r)
                    p.dma("q", yfull[1][OFF + rr: OFF + rr + 128, :], yt2[:])
                else:
                    p.wait("s", ev_r)
                    emit_ln(p, yt2, sq, mrow, rrow, D)
                    p.op("v", lambda v: v.tensor_scalar(
                        yt2[:], yt2[:], mrow[:, 0:1], rrow[:, 0:1], OP.subtract, OP.mult))
                    p.wait("q", p.ev("v"))
                    ev_x1 = p.dma("q", ht[:], x1own[rr:rr + 128, :])
                    p.wait("v", ev_x1)
                    ev_f = p.op("v", lambda v: v.tensor_tensor(yt2[:], yt2[:], ht[:], OP.add))
                    p.wait("q", ev_f)
                    p.dma("q", xout[rr:rr + 128, :], yt2[:])
        p.barrier()
        p.sfree(yt2); p.sfree(prodfm); p.sfree(xfm2)
        p.sfree(w2_s); p.sfree(b2b); p.sfree(lbb); p.sfree(lgb); p.sfree(b1c); p.sfree(w1_s)
        if l == 0:
            p.dma("q", cin_e, yfull[1][OFF + TC - HALO: OFF + TC, :])
            p.wait("g", p.ev("q"))
            ev_cc = p.collective("AllGather", mybir.AluOpType.bypass, GROUPS,
                                 [cin_e], [cout_e])
            p.wait("q", ev_cc)
            for i in range(HALO // 128):
                ev_b = p.dma("q", ht[:], cout_e[i * 128:(i + 1) * 128, :])
                p.wait("v", ev_b)
                ev_bb = p.op("v", lambda v: v.tensor_scalar(
                    ht[:], ht[:], flagb[:, 0:1], None, OP.mult))
                p.wait("q", ev_bb)
                ev_sst = p.dma("q", yfull[1][i * 128:(i + 1) * 128, :], ht[:])
                p.wait("v", ev_sst)
            p.barrier()
    p.wait("q", p.ev("q"))


def moe_emit(p, io):
    NBLK = TC // 512
    xin = io['xres3']
    wiT = p.dram("wiT", [D, FF])
    woT = p.dram("woT", [FF, D])
    gateT = p.dram("gateT", [D, NE])
    gateb = p.dram("gateb", [1, NE])
    wib = p.dram("wib", [FF])
    wob = p.dram("wob", [D])
    AcatT = p.dram("AcatT", [D, NE * R])
    Bcat = p.dram("Bcat", [NE * R, FF])
    ident_d = io["ident_d"]
    ident32_d = io["ident32_d"]
    xout = io['xout']

    wiT_s = p.sbuf("mo_wiT_s", [128, 8 * FF], BF16)
    woT_s = p.sbuf("mo_woT_s", [128, 32 * D], BF16)
    gateT_s = p.sbuf("mo_gateT_s", [128, 8 * NE], F32)
    xfm32 = p.sbuf("mo_xfm32", [128, 8 * 128], F32)
    xn32 = p.sbuf("mo_xn32", [128, D], F32)
    gb_s = p.sbuf("mo_gb_s", [128, NE], F32)
    wib_s = p.sbuf("mo_wib_s", [128, 32], F32)
    wob_s = p.sbuf("mo_wob_s", [128, 8], F32)
    AcatT_s = p.sbuf("mo_AcatT_s", [128, 8 * NE * R], BF16)
    Bcat_s = p.sbuf("mo_Bcat_s", [32, FF], BF16)
    ident = p.sbuf("mo_ident_s", [128, 128], BF16)
    ident32 = p.sbuf("mo_ident32_s", [128, 128], F32)
    xt = [p.sbuf(f"mo_xt{i}", [128, D], F32) for i in range(4)]
    xn = p.sbuf("mo_xn", [128, D], BF16)
    xfm = p.sbuf("mo_xfm", [128, 8 * 512], BF16)
    sq = p.sbuf("mo_sq", [128, D], F32)
    mrow = p.sbuf("mo_mrow", [128, 1], F32)
    rrow = p.sbuf("mo_rrow", [128, 1], F32)
    gl = p.sbuf("mo_gl", [128, NE], F32)
    gmax = p.sbuf("mo_gmax", [128, 1], F32)
    eq = p.sbuf("mo_eq", [128, NE], F32)
    cum = p.sbuf("mo_cum", [128, NE], F32)
    oh = p.sbuf("mo_oh", [128, NE], F32)
    zer8 = p.sbuf("mo_zer8", [128, NE], F32)
    oh32 = p.sbuf("mo_oh32", [128, 32], F32)
    u_bf = p.sbuf("mo_u_bf", [128, 32], BF16)
    ufm = p.sbuf("mo_ufm", [32, 512], BF16)
    inter = p.sbuf("mo_inter", [128, 8 * 512], BF16)
    oacc = p.sbuf("mo_oacc", [128, 8 * 512], F32)
    yt = p.sbuf("mo_yt", [128, D], F32)

    tp0 = io["tpb0"]
    tp1 = io["tpb1"]
    gp = io["ptb0"][:, 0:NE]
    sp = io["ptb1"][:, 0:32]
    big0 = io["sps"][:, 0:512]
    big1 = io["sps"][:, 512:1024]
    ob0 = io["avq"][:, 0:512]
    ob1 = io["obx"][:, 0:512]

    for kc in range(8):
        p.dma("g", wiT_s[:, kc * FF:(kc + 1) * FF], wiT[kc * 128:(kc + 1) * 128, :])
    for kc2 in range(32):
        p.dma("g", woT_s[:, kc2 * D:(kc2 + 1) * D], woT[kc2 * 128:(kc2 + 1) * 128, :])
    for kc in range(8):
        p.dma("g", gateT_s[:, kc * NE:(kc + 1) * NE], gateT[kc * 128:(kc + 1) * 128, :])
        p.dma("g", AcatT_s[:, kc * 32:(kc + 1) * 32], AcatT[kc * 128:(kc + 1) * 128, :])
    p.dma("g", Bcat_s[:], Bcat)
    p.dma("g", gb_s[:], gateb[0].partition_broadcast(128))
    p.dma("q", wib_s[:], wib.rearrange("(c p) -> p c", p=128), allow_slow_non_contiguous=True)
    p.dma("q", wob_s[:], wob.rearrange("(c p) -> p c", p=128), allow_slow_non_contiguous=True)
    p.dma("q", ident[:], ident_d)
    p.dma("q", ident32[:], ident32_d)
    p.op("v", lambda v: v.memset(zer8[:], 0.0))
    p.constf(1e-5)
    p.barrier()

    for blk in range(NBLK):
        for i in range(4):
            t0 = blk * 512 + i * 128
            ev_x = p.dma("q", xt[i][:], xin[t0:t0 + 128, :])
            p.wait("v", ev_x)
            p.wait("s", ev_x)
            emit_ln(p, xt[i], sq, mrow, rrow, D)
            ev_xn32 = p.op("v", lambda v, i=i: v.tensor_scalar(
                xn32[:], xt[i][:], mrow[:, 0:1], rrow[:, 0:1], OP.subtract, OP.mult))
            ev_xn = p.op("v", lambda v: v.tensor_copy(xn[:], xn32[:]))
            ev_tp_evac = {0: None, 1: None}
            for kc in range(8):
                tp = (tp0, tp1)[kc % 2]
                p.wait("t", ev_xn)
                p.wait("t", ev_tp_evac[kc % 2])
                ev_t = p.op("t", lambda t, kc=kc, tp=tp: t.transpose(
                    tp[:], xn[:, kc * 128:(kc + 1) * 128], ident[:]))
                p.wait("v", ev_t)
                ev_e = p.op("v", lambda v, tp=tp, kc=kc, i=i: v.tensor_copy(
                    xfm[:, kc * 512 + i * 128: kc * 512 + (i + 1) * 128], tp[:]))
                ev_tp_evac[kc % 2] = ev_e
            for kc in range(8):
                p.wait("t", ev_xn32)
                ev_t = p.op("t", lambda t, kc=kc: t.transpose(
                    ob0[:, 0:128], xn32[:, kc * 128:(kc + 1) * 128], ident32[:]))
                p.wait("v", ev_t)
                ev_e32 = p.op("v", lambda v, kc=kc: v.tensor_copy(
                    xfm32[:, kc * 128:(kc + 1) * 128], ob0[:, 0:128]))
                p.wait("t", ev_e32)
            p.barrier(("t", "v"))
            def gmm(t, i=i):
                for kc in range(8):
                    inst = t.matmul(gp, xfm32[:, kc * 128:(kc + 1) * 128],
                                    gateT_s[:, kc * NE:(kc + 1) * NE],
                                    start=(kc == 0), stop=(kc == 7))
                return inst
            ev_g = p.op("t", gmm)
            def smm(t, i=i):
                for kc in range(8):
                    inst = t.matmul(sp, xfm[:, kc * 512 + i * 128: kc * 512 + (i + 1) * 128],
                                    AcatT_s[:, kc * 32:(kc + 1) * 32],
                                    start=(kc == 0), stop=(kc == 7))
                return inst
            ev_s = p.op("t", smm)
            p.wait("v", ev_g)
            p.op("v", lambda v: v.tensor_tensor(gl[:], gp, gb_s[:], OP.add))
            p.op("v", lambda v: v.tensor_reduce(gmax[:], gl[:], axis=AX.X, op=OP.max))
            p.op("v", lambda v: v.tensor_scalar(eq[:], gl[:], gmax[:, 0:1], None, OP.is_ge))
            p.op("v", lambda v: v.tensor_tensor_scan(cum[:], eq[:], zer8[:], 0.0,
                                                     OP.add, OP.add))
            p.op("v", lambda v: v.tensor_scalar(cum[:], cum[:], 1.0, None, OP.is_equal))
            p.op("v", lambda v: v.tensor_tensor(oh[:], cum[:], eq[:], OP.mult))
            oh32v = oh32.rearrange("p (e r) -> p e r", r=4)
            for r in range(4):
                p.op("v", lambda v, r=r: v.tensor_copy(oh32v[:, :, r], oh[:]))
            p.wait("v", ev_s)
            p.op("v", lambda v: v.tensor_tensor(u_bf[:], sp, oh32[:], OP.mult))
            p.barrier(("t", "v"))
            p.op("t", lambda t: t.transpose(tp0[0:32, 0:128], u_bf[:], ident[:]))
            p.barrier(("t", "v"))
            p.op("v", lambda v, i=i: v.tensor_copy(ufm[:, i * 128:(i + 1) * 128],
                                                   tp0[0:32, 0:128]))
            p.barrier(("t", "v", "s"))
        p.barrier(("t", "v", "s"))
        ev_big_evac = {0: None, 1: None}
        ev_ob_read = {0: None, 1: None}
        for rnd in range(4):
            ev_inter = None
            for j in range(8):
                mc = rnd * 8 + j
                bigp = (big0, big1)[j % 2]
                p.wait("t", ev_big_evac[j % 2])
                def bmm(t, mc=mc, bigp=bigp):
                    for kc in range(8):
                        t.matmul(bigp, wiT_s[:, kc * FF + mc * 128: kc * FF + (mc + 1) * 128],
                                 xfm[:, kc * 512:(kc + 1) * 512],
                                 start=(kc == 0), stop=False)
                    return t.matmul(bigp, Bcat_s[:, mc * 128:(mc + 1) * 128], ufm[:],
                                    start=False, stop=True)
                ev_mm = p.op("t", bmm)
                p.wait("s", ev_mm)
                ev_r = p.op("s", lambda s, j=j, mc=mc, bigp=bigp: s.activation(
                    inter[:, j * 512:(j + 1) * 512], bigp, AF.Relu,
                    bias=wib_s[:, mc:mc + 1]))
                ev_big_evac[j % 2] = ev_r
                ev_inter = ev_r
            for oc in range(8):
                obp = (ob0, ob1)[oc % 2]
                p.wait("t", ev_inter)
                p.wait("t", ev_ob_read[oc % 2])
                def omm(t, rnd=rnd, oc=oc, obp=obp):
                    for j in range(8):
                        kc2 = rnd * 8 + j
                        inst = t.matmul(obp, woT_s[:, kc2 * D + oc * 128: kc2 * D + (oc + 1) * 128],
                                        inter[:, j * 512:(j + 1) * 512],
                                        start=(j == 0), stop=(j == 7))
                    return inst
                ev_mm = p.op("t", omm)
                p.wait("v", ev_mm)
                if rnd == 0:
                    ev_a = p.op("v", lambda v, oc=oc, obp=obp: v.tensor_copy(
                        oacc[:, oc * 512:(oc + 1) * 512], obp))
                else:
                    ev_a = p.op("v", lambda v, oc=oc, obp=obp: v.tensor_tensor(
                        oacc[:, oc * 512:(oc + 1) * 512],
                        oacc[:, oc * 512:(oc + 1) * 512], obp, OP.add))
                ev_ob_read[oc % 2] = ev_a
            p.barrier(("t", "s", "v"))
            ev_big_evac = {0: None, 1: None}
            ev_ob_read = {0: None, 1: None}
        for oc in range(8):
            p.op("v", lambda v, oc=oc: v.tensor_scalar(
                inter[:, oc * 512:(oc + 1) * 512], oacc[:, oc * 512:(oc + 1) * 512],
                wob_s[:, oc:oc + 1], None, OP.add))
        p.barrier(("t", "v"))
        for i in range(4):
            t0 = blk * 512 + i * 128
            ev_tp_evac = {0: None, 1: None}
            for oc in range(8):
                tp = (tp0, tp1)[oc % 2]
                p.wait("t", ev_tp_evac[oc % 2])
                ev_t = p.op("t", lambda t, i=i, oc=oc, tp=tp: t.transpose(
                    tp[:], inter[:, oc * 512 + i * 128: oc * 512 + (i + 1) * 128], ident[:]))
                p.wait("v", ev_t)
                ev_e = p.op("v", lambda v, oc=oc, tp=tp: v.tensor_copy(
                    yt[:, oc * 128:(oc + 1) * 128], tp[:]))
                ev_tp_evac[oc % 2] = ev_e
            p.barrier(("v", "s"))
            emit_ln(p, yt, sq, mrow, rrow, D)
            p.op("v", lambda v: v.tensor_scalar(yt[:], yt[:], mrow[:, 0:1], rrow[:, 0:1],
                                                OP.subtract, OP.mult))
            ev_res = p.op("v", lambda v, i=i: v.tensor_tensor(yt[:], yt[:], xt[i][:], OP.add))
            p.wait("q", ev_res)
            ev_st = p.dma("q", xout[t0:t0 + 128, :], yt[:])
            p.wait("v", ev_st)
            p.wait("t", ev_res)
            p.barrier(("t", "v", "s"))
        p.barrier()
    p.wait("q", p.ev("q"))




D = 1024
T = 2048
DI = 2048
DS = 16
TB = 512
NTB = T // TB
GROUPS = [[0, 1], [2, 3], [4, 5], [6, 7]]


def mamba_emit(p, io):
    xres2 = io["xres2"]
    xres3 = io["xres3"]
    sps = io["sps"]
    ps0 = sps[:, 0:512]
    ps1 = sps[:, 512:1024]
    tpb0 = io["tpb0"]
    tpb1 = io["tpb1"]
    xpp = io["ptb0"][0:96, 0:256]
    ident_d = io["ident_d"]
    flg_d = io["flg_d"]

    inpT_d = p.dram("m_inpT", [D, 2 * DI])
    xpwT_d = [p.dram(f"m_xpwT{s}", [DI, 96]) for s in "fb"]
    dtpwT_d = [p.dram(f"m_dtpwT{s}", [64, DI]) for s in "fb"]
    dtb_d = [p.dram(f"m_dtb{s}", [DI]) for s in "fb"]
    alog_d = [p.dram(f"m_alog{s}", [DI, DS]) for s in "fb"]
    dvec_d = [p.dram(f"m_dvec{s}", [DI]) for s in "fb"]
    convw_d = [p.dram(f"m_convw{s}", [DI, 1, 4]) for s in "fb"]
    convb_d = [p.dram(f"m_convb{s}", [DI]) for s in "fb"]
    outT_d = p.dram("m_outT", [DI, D])

    xz_dr = p.dram("m_xz", [2 * DI, T], BF16, kind="Internal")
    xbr_dr = [p.dram(f"m_xbr{s}", [DI, T], kind="Internal") for s in "fb"]
    bcrow_dr = [p.dram(f"m_bcrow{s}", [32, T], kind="Internal") for s in "fb"]
    dlt_dr = [p.dram(f"m_dlt{s}", [DI, T], kind="Internal") for s in "fb"]
    dlu_dr = [p.dram(f"m_dlu{s}", [DI, T], kind="Internal") for s in "fb"]
    ybr_dr = [p.dram(f"m_ybr{s}", [DI, T], BF16, kind="Internal") for s in "fb"]
    cin_h = p.dram("m_cinh", [128, 96], kind="Internal")
    cout_h = p.dram("m_couth", [256, 96], kind="Internal")
    cin_st = p.dram("m_cinst", [128, 512], kind="Internal")
    cout_st = p.dram("m_coutst", [256, 512], kind="Internal")

    ident = p.sbuf("m_ident", [128, 128], BF16)
    flagb = p.sbuf("m_flagb", [128, 1], F32)
    finv = p.sbuf("m_finv", [128, 1], F32)
    sq = p.sbuf("m_sq", [128, D], F32)
    mrow = p.sbuf("m_mrow", [128, 1], F32)
    rrow = p.sbuf("m_rrow", [128, 1], F32)
    ht = p.sbuf("m_ht", [128, D], F32)
    htb = p.sbuf("m_htb", [128, D], BF16)
    hfm = p.sbuf("m_hfm", [128, 8 * 512], BF16)
    est = p.sbuf("m_est", [128, 512], BF16)
    p.dma("q", ident[:], ident_d)
    p.dma("q", flagb[:], flg_d[0].partition_broadcast(128))
    p.wait("v", p.ev("q"))
    p.op("v", lambda v: v.tensor_scalar(finv[:], flagb[:], -1.0, 1.0, OP.mult, OP.add))

    inpT_s = p.sbuf_tmp("m_inpT_s", [128, 8 * 2 * DI], BF16)
    for kc in range(8):
        p.dma("g", inpT_s[:, kc * 2 * DI:(kc + 1) * 2 * DI],
              inpT_d[kc * 128:(kc + 1) * 128, :])
    p.barrier()
    for blk in range(T // 512):
        for i in range(4):
            t0 = blk * 512 + i * 128
            p.wait("q", p.ev("v")); p.wait("q", p.ev("s"))
            ev_x = p.dma("q", ht[:], xres2[t0:t0 + 128, :])
            p.wait("v", ev_x); p.wait("s", ev_x)
            emit_ln(p, ht, sq, mrow, rrow, D)
            p.wait("v", p.ev("t"))
            ev_xn = p.op("v", lambda v: v.tensor_scalar(
                htb[:], ht[:], mrow[:, 0:1], rrow[:, 0:1], OP.subtract, OP.mult))
            ev_tp = {0: None, 1: None}
            for kc in range(8):
                tp = (tpb0, tpb1)[kc % 2]
                p.wait("t", ev_xn)
                p.wait("t", ev_tp[kc % 2])
                ev_t = p.op("t", lambda t, kc=kc, tp=tp: t.transpose(
                    tp[:], htb[:, kc * 128:(kc + 1) * 128], ident[:]))
                p.wait("v", ev_t)
                ev_e = p.op("v", lambda v, tp=tp, kc=kc, i=i: v.tensor_copy(
                    hfm[:, kc * 512 + i * 128: kc * 512 + (i + 1) * 128], tp[:]))
                ev_tp[kc % 2] = ev_e
        p.barrier(("t", "v", "s"))
        ev_dma_prev = None
        for mc in range(32):
            psx = (ps0, ps1)[mc % 2]
            def pmm(t, mc=mc, psx=psx):
                for kc in range(8):
                    inst = t.matmul(psx, inpT_s[:, kc * 2 * DI + mc * 128: kc * 2 * DI + (mc + 1) * 128],
                                    hfm[:, kc * 512:(kc + 1) * 512],
                                    start=(kc == 0), stop=(kc == 7))
                return inst
            ev_mm = p.op("t", pmm)
            p.wait("s", ev_mm)
            p.wait("s", ev_dma_prev)
            ev_ev = p.op("s", lambda s, psx=psx: s.activation(est[:], psx, AF.Copy))
            p.wait("q", ev_ev)
            ev_dma_prev = p.dma("q", xz_dr[mc * 128:(mc + 1) * 128, blk * 512:(blk + 1) * 512], est[:])
            p.wait("t", ev_ev)
        p.barrier(("t", "v", "s", "q"))
    p.sfree(inpT_s)
    p.barrier()

    hbuf = p.sbuf_tmp("m_hbuf", [128, 96], F32)
    hbf = p.sbuf_tmp("m_hbf", [128, 96], F32)
    hbb = p.sbuf_tmp("m_hbb", [128, 96], F32)
    for ch in range(16):
        p.dma("g", hbuf[:, ch * 6:ch * 6 + 3], xz_dr[ch * 128:(ch + 1) * 128, 0:3])
        p.dma("g", hbuf[:, ch * 6 + 3:ch * 6 + 6], xz_dr[ch * 128:(ch + 1) * 128, T - 3:T])
    p.barrier(("q", "g", "v"))
    p.dma("q", cin_h, hbuf[:])
    p.wait("g", p.ev("q"))
    ev_cc = p.collective("AllGather", mybir.AluOpType.bypass, GROUPS, [cin_h], [cout_h])
    p.wait("q", ev_cc)
    ev_f = p.dma("q", hbf[:], cout_h[0:128, :])
    ev_b = p.dma("q", hbb[:], cout_h[128:256, :])
    p.wait("v", ev_b)
    p.op("v", lambda v: v.tensor_scalar(hbf[:], hbf[:], flagb[:, 0:1], None, OP.mult))
    p.op("v", lambda v: v.tensor_scalar(hbb[:], hbb[:], finv[:, 0:1], None, OP.mult))
    p.barrier()

    cw_s = p.sbuf_tmp("m_cw", [128, 2 * 64], F32)
    cb_s = p.sbuf_tmp("m_cb", [128, 2 * 16], F32)
    xzc = p.sbuf_tmp("m_xzc", [128, T + 6], F32)
    cva = p.sbuf_tmp("m_cva", [128, T], F32)
    for di, s_ in enumerate("fb"):
        p.dma("q", cw_s[:, di * 64:(di + 1) * 64].rearrange("p (c k) -> p c k", k=4),
              convw_d[di].rearrange("(c p) o k -> p c (o k)", p=128),
              allow_slow_non_contiguous=True)
        p.dma("q", cb_s[:, di * 16:(di + 1) * 16], convb_d[di].rearrange("(c p) -> p c", p=128),
              allow_slow_non_contiguous=True)
    p.barrier(("q", "v", "s", "g"))
    for di in range(2):
        for ch in range(16):
            p.wait("g", p.ev("v"))
            ev_x = p.dma("g", xzc[:, 3:T + 3], xz_dr[ch * 128:(ch + 1) * 128, :])
            p.wait("v", ev_x)
            if di == 0:
                p.op("v", lambda v, ch=ch: v.tensor_copy(xzc[:, 0:3], hbf[:, ch * 6 + 3:ch * 6 + 6]))
                sl = [3, 2, 1, 0]
                offs = [3, 2, 1, 0]
                wj = [3, 2, 1, 0]
            else:
                p.op("v", lambda v, ch=ch: v.tensor_copy(xzc[:, T + 3:T + 6], hbb[:, ch * 6:ch * 6 + 3]))
                offs = [3, 4, 5, 6]
                wj = [3, 2, 1, 0]
            p.op("v", lambda v, di=di, ch=ch, o=offs[0], j=wj[0]: v.tensor_scalar(
                cva[:], xzc[:, o:o + T], cw_s[:, di * 64 + ch * 4 + j: di * 64 + ch * 4 + j + 1],
                None, OP.mult))
            for k in range(1, 4):
                p.op("v", lambda v, di=di, ch=ch, o=offs[k], j=wj[k]: v.scalar_tensor_tensor(
                    cva[:], xzc[:, o:o + T], cw_s[:, di * 64 + ch * 4 + j: di * 64 + ch * 4 + j + 1],
                    cva[:], OP.mult, OP.add))
            ev_c = p.ev("v")
            p.wait("s", ev_c)
            ev_si = p.op("s", lambda s, di=di, ch=ch: s.activation(
                cva[:], cva[:], AF.Silu, bias=cb_s[:, di * 16 + ch:di * 16 + ch + 1]))
            p.wait("q", ev_si)
            ev_st = p.dma("q", xbr_dr[di][ch * 128:(ch + 1) * 128, :], cva[:])
            p.wait("v", ev_st)
    p.sfree(cva); p.sfree(xzc); p.sfree(cb_s); p.sfree(cw_s)
    p.sfree(hbb); p.sfree(hbf); p.sfree(hbuf)
    p.barrier()

    xdbl = [p.sbuf(f"m_xdbl{s}", [96, T], F32) for s in "fb"]
    xpwT_s = p.sbuf_tmp("m_xpwT_s", [128, 2 * 16 * 96], F32)
    xbrc = p.sbuf_tmp("m_xbrc", [128, 16 * 512], F32)
    for di in range(2):
        for kc in range(16):
            p.dma("q", xpwT_s[:, di * 1536 + kc * 96: di * 1536 + (kc + 1) * 96],
                  xpwT_d[di][kc * 128:(kc + 1) * 128, :])
    p.barrier()
    for di in range(2):
        for blk in range(NTB):
            for kc in range(16):
                p.dma(("q", "g")[kc % 2], xbrc[:, kc * 512:(kc + 1) * 512],
                      xbr_dr[di][kc * 128:(kc + 1) * 128, blk * 512:(blk + 1) * 512])
            p.barrier(("t", "q", "g"))
            for half in range(2):
                def xpm(t, di=di, half=half):
                    for kc in range(16):
                        inst = t.matmul(xpp,
                                        xpwT_s[:, di * 1536 + kc * 96: di * 1536 + (kc + 1) * 96],
                                        xbrc[:, kc * 512 + half * 256: kc * 512 + (half + 1) * 256],
                                        start=(kc == 0), stop=(kc == 15))
                    return inst
                ev_mm = p.op("t", xpm)
                p.wait("s", ev_mm)
                ev_ev = p.op("s", lambda s, di=di, blk=blk, half=half: s.copy(
                    xdbl[di][:, blk * 512 + half * 256: blk * 512 + (half + 1) * 256], xpp))
                p.wait("t", ev_ev)
            p.barrier(("t", "s", "q", "g"))
    p.sfree(xbrc); p.sfree(xpwT_s)
    p.wait("q", p.ev("s")); p.wait("q", p.ev("v"))
    for di in range(2):
        p.dma("q", bcrow_dr[di], xdbl[di][64:96, :])
    p.barrier()

    dtpwT_s = p.sbuf("m_dtpwT_s", [64, 2 * DI], F32)
    dtb_s = p.sbuf("m_dtb_s", [128, 2 * 16], F32)
    aexp = p.sbuf("m_aexp", [128, 2 * 16 * DS], F32)
    dvec_s = p.sbuf("m_dvec_s", [128, 2 * 16], F32)
    states = p.sbuf("m_states", [128, 2 * 16 * DS], F32)
    init_s = p.sbuf("m_init", [128, 2 * 16 * DS], F32)
    e1col = p.sbuf("m_e1col", [128, 1], F32)
    for di in range(2):
        p.dma("q", dtpwT_s[:, di * DI:(di + 1) * DI], dtpwT_d[di])
        p.dma("q", dtb_s[:, di * 16:(di + 1) * 16], dtb_d[di].rearrange("(c p) -> p c", p=128),
              allow_slow_non_contiguous=True)
        p.dma("q", dvec_s[:, di * 16:(di + 1) * 16], dvec_d[di].rearrange("(c p) -> p c", p=128),
              allow_slow_non_contiguous=True)
        ev_a = p.dma("q", aexp[:, di * 256:(di + 1) * 256].rearrange("p (c n) -> p c n", n=DS),
                     alog_d[di].rearrange("(c p) n -> p c n", p=128))
    p.wait("s", p.ev("q"))
    ev_a2 = p.op("s", lambda s: s.activation(aexp[:], aexp[:], AF.Exp))
    p.wait("v", ev_a2)
    p.op("v", lambda v: v.tensor_scalar(aexp[:], aexp[:], -1.0, None, OP.mult))
    p.op("v", lambda v: v.memset(states[:], 0.0))
    p.barrier()

    bball = p.sbuf_tmp("m_bball", [128, 16 * T], BF16)
    dltf = p.sbuf_tmp("m_p1dlt", [128, T], F32)
    dluf = p.sbuf_tmp("m_p1dlu", [128, T], F32)
    dlub = p.sbuf_tmp("m_p1dlub", [128, T], BF16)
    lamf2 = [p.sbuf_tmp("m_p1lam", [128, T], BF16),
             p.sbuf_tmp("m_p1lamB", [128, T], BF16)]
    bscf2 = [p.sbuf_tmp("m_p1bsc", [128, T], BF16),
             p.sbuf_tmp("m_p1bscB", [128, T], BF16)]
    hscf = p.sbuf_tmp("m_p1hsc", [128, T], BF16)
    uch = p.sbuf_tmp("m_uch", [128, T], F32)
    spt = p.sbuf_tmp("m_spt", [128, T], F32)
    ev_scan_par = [None, None]
    ev_bsc_par = [None, None]
    ev_bsc_last = None
    evv_ps = [None, None]
    evs_ps = [None, None]
    for di in range(2):
        for n in range(DS):
            p.wait("g", p.ev("v"))
            p.dma("g", bball[:, n * T:(n + 1) * T], bcrow_dr[di][n, :].partition_broadcast(128))
        p.barrier(("g", "v", "s", "t", "q"))
        for ch in range(16):
            for sb in range(4):
                psx = (ps0, ps1)[sb % 2]
                p.wait("t", evv_ps[sb % 2]); p.wait("t", evs_ps[sb % 2])
                ev_mm = p.op("t", lambda t, di=di, ch=ch, sb=sb, psx=psx: t.matmul(
                    psx, dtpwT_s[:, di * DI + ch * 128:(di * DI) + (ch + 1) * 128],
                    xdbl[di][0:64, sb * 512:(sb + 1) * 512], start=True, stop=True))
                p.wait("s", ev_mm)
                evs_ps[sb % 2] = p.op("s", lambda s, di=di, ch=ch, sb=sb, psx=psx: s.activation(
                    spt[:, sb * 512:(sb + 1) * 512], psx, AF.Abs,
                    bias=dtb_s[:, di * 16 + ch: di * 16 + ch + 1]))
                ev_sq2 = p.op("s", lambda s, sb=sb: s.activation(
                    spt[:, sb * 512:(sb + 1) * 512], spt[:, sb * 512:(sb + 1) * 512],
                    AF.Exp, scale=-1.0))
                ev_sq3 = p.op("s", lambda s, sb=sb: s.activation(
                    spt[:, sb * 512:(sb + 1) * 512], spt[:, sb * 512:(sb + 1) * 512],
                    AF.Ln, bias=p.constf(1.0)))
                p.wait("v", ev_sq3)
                evv_ps[sb % 2] = p.op("v", lambda v, di=di, ch=ch, sb=sb, psx=psx: v.tensor_scalar(
                    dltf[:, sb * 512:(sb + 1) * 512], psx,
                    dtb_s[:, di * 16 + ch: di * 16 + ch + 1], 0.0, OP.add, OP.max))
                ev_dadd = p.op("v", lambda v, sb=sb: v.tensor_tensor(
                    dltf[:, sb * 512:(sb + 1) * 512], dltf[:, sb * 512:(sb + 1) * 512],
                    spt[:, sb * 512:(sb + 1) * 512], OP.add))
            p.wait("q", p.ev("v"))
            ev_u = p.dma("q", uch[:], xbr_dr[di][ch * 128:(ch + 1) * 128, :])
            p.wait("v", ev_u)
            p.op("v", lambda v: v.tensor_tensor(dluf[:], dltf[:], uch[:], OP.mult))
            p.wait("v", ev_bsc_last)
            ev_dub = p.op("v", lambda v: v.tensor_copy(dlub[:], dluf[:]))
            p.wait("q", ev_dub)
            p.dma("q", dlt_dr[di][ch * 128:(ch + 1) * 128, :], dltf[:])
            ev_sdl = p.dma("q", dlu_dr[di][ch * 128:(ch + 1) * 128, :], dluf[:])
            p.wait("g", ev_dub)
            for n in range(DS):
                pa = n % 2
                lamf = lamf2[pa]
                bscf = bscf2[pa]
                p.wait("s", ev_scan_par[pa])
                if n == 0:
                    p.wait("s", ev_dadd)
                ev_l = p.op("s", lambda s, di=di, ch=ch, n=n, lamf=lamf: s.activation(
                    lamf[:], dltf[:], AF.Exp,
                    scale=aexp[:, di * 256 + ch * DS + n: di * 256 + ch * DS + n + 1]))
                p.wait("g", ev_scan_par[pa])
                ev_bsc = p.op("g", lambda g, n=n, bscf=bscf: g.tensor_tensor(
                    bscf[:], dlub[:], bball[:, n * T:(n + 1) * T], OP.mult))
                ev_bsc_last = ev_bsc
                p.wait("v", ev_l)
                p.wait("v", ev_bsc)
                if di == 0:
                    p.op("v", lambda v, lamf=lamf, bscf=bscf: v.tensor_tensor_scan(
                        hscf[:], lamf[:], bscf[:], 0.0, OP.mult, OP.add))
                else:
                    p.op("v", lambda v, lamf=lamf, bscf=bscf: v.tensor_tensor_scan(
                        hscf[:], lamf[:, ::-1], bscf[:, ::-1], 0.0, OP.mult, OP.add))
                ev_scan_par[pa] = p.op("v", lambda v, di=di, ch=ch, n=n: v.tensor_copy(
                    states[:, di * 256 + ch * DS + n: di * 256 + ch * DS + n + 1],
                    hscf[:, T - 1:T]))
            p.wait("v", ev_sdl)
            p.wait("s", ev_sdl)
        p.barrier(("t", "v", "s", "q", "g"))
        ev_scan_par = [None, None]
        evv_ps = [None, None]
        evs_ps = [None, None]
        ev_bsc_last = None
    p.sfree(spt); p.sfree(uch); p.sfree(hscf)
    p.sfree(bscf2[1]); p.sfree(bscf2[0]); p.sfree(lamf2[1]); p.sfree(lamf2[0])
    p.sfree(dlub); p.sfree(dluf); p.sfree(dltf); p.sfree(bball)
    p.barrier()

    stg = p.sbuf_tmp("m_stg", [128, 512], F32)
    p.wait("q", p.ev("v"))
    p.dma("q", cin_st, states[:])
    p.wait("g", p.ev("q"))
    ev_cc2 = p.collective("AllGather", mybir.AluOpType.bypass, GROUPS, [cin_st], [cout_st])
    p.wait("q", ev_cc2)
    ev_l1 = p.dma("q", stg[:, 0:256], cout_st[0:128, 0:256])
    ev_l2 = p.dma("q", stg[:, 256:512], cout_st[128:256, 256:512])
    p.wait("v", ev_l2)
    p.op("v", lambda v: v.tensor_scalar(init_s[:, 0:256], stg[:, 0:256],
                                        flagb[:, 0:1], None, OP.mult))
    p.op("v", lambda v: v.tensor_scalar(init_s[:, 256:512], stg[:, 256:512],
                                        finv[:, 0:1], None, OP.mult))
    p.sfree(stg)
    p.barrier()

    TB2 = 1024
    bcb = p.sbuf_tmp("m_bcb", [128, DS * TB2], BF16)
    ccb = p.sbuf_tmp("m_ccb", [128, DS * TB2], BF16)
    dlt2 = p.sbuf_tmp("m_dlt2", [128, TB2], F32)
    dlu2 = p.sbuf_tmp("m_dlu2", [128, TB2], BF16)
    lam2d = [p.sbuf_tmp("m_lam2", [128, TB2], BF16),
             p.sbuf_tmp("m_lam2B", [128, TB2], BF16)]
    bsc2 = p.sbuf_tmp("m_bsc2", [128, TB2], BF16)
    hsc2 = p.sbuf_tmp("m_hsc2", [128, TB2], BF16)
    tmp2d = [p.sbuf_tmp("m_tmp2", [128, TB2], BF16),
             p.sbuf_tmp("m_tmp2B", [128, TB2], BF16)]
    tmp2 = tmp2d[0]
    yac2 = p.sbuf_tmp("m_yac2", [128, TB2], F32)
    uch2 = p.sbuf_tmp("m_uch2", [128, TB2], F32)
    zall = p.sbuf_tmp("m_zall", [128, 16 * TB2], BF16)
    st2 = p.sbuf_tmp("m_st2", [128, 2 * 16 * DS], F32)
    NB2 = T // TB2
    for di in range(2):
        for tau in range(NB2):
            tb = tau if di == 0 else (NB2 - 1 - tau)
            tsl = slice(tb * TB2, (tb + 1) * TB2)
            for n in range(DS):
                p.wait("g", p.ev("v"))
                p.dma("g", bcb[:, n * TB2:(n + 1) * TB2], bcrow_dr[di][n, tsl].partition_broadcast(128))
                p.dma("g", ccb[:, n * TB2:(n + 1) * TB2], bcrow_dr[di][DS + n, tsl].partition_broadcast(128))
            for ch in range(16):
                p.dma("g", zall[:, ch * TB2:(ch + 1) * TB2],
                      xz_dr[DI + ch * 128: DI + (ch + 1) * 128, tsl])
            p.barrier(("g", "v", "s", "t", "q"))
            for ch in range(16):
                p.op("s", lambda s, ch=ch: s.activation(
                    zall[:, ch * TB2:(ch + 1) * TB2], zall[:, ch * TB2:(ch + 1) * TB2], AF.Silu))
            p.barrier(("s", "v", "g"))
            ev_scan2 = [None, None]
            ev_tmp2 = [None, None]
            ev_add = None
            ev_st2 = None
            for ch in range(16):
                p.wait("q", p.ev("v")); p.wait("q", p.ev("s"))
                ev_dlt = p.dma("q", dlt2[:], dlt_dr[di][ch * 128:(ch + 1) * 128, tsl])
                ev_d2 = p.dma("g", dlu2[:], dlu_dr[di][ch * 128:(ch + 1) * 128, tsl])
                ev_u2 = p.dma("q", uch2[:], xbr_dr[di][ch * 128:(ch + 1) * 128, tsl])
                p.wait("s", ev_dlt)
                p.wait("v", p.ev("q"))
                p.wait("v", ev_d2)
                p.wait("g", ev_st2)
                ev_ms = p.op("g", lambda g: g.memset(yac2[:], 0.0))
                for n in range(DS):
                    pa = n % 2
                    lam2 = lam2d[pa]
                    tmp2n = tmp2d[pa]
                    scol = st2[:, di * 256 + ch * DS + n: di * 256 + ch * DS + n + 1]
                    icol = init_s[:, di * 256 + ch * DS + n: di * 256 + ch * DS + n + 1]
                    p.wait("s", ev_scan2[pa])
                    ev_l = p.op("s", lambda s, di=di, ch=ch, n=n, lam2=lam2: s.activation(
                        lam2[:], dlt2[:], AF.Exp,
                        scale=aexp[:, di * 256 + ch * DS + n: di * 256 + ch * DS + n + 1]))
                    p.op("v", lambda v, n=n: v.tensor_tensor(
                        bsc2[:], dlu2[:], bcb[:, n * TB2:(n + 1) * TB2], OP.mult))
                    p.wait("v", ev_l)
                    init_ap = icol if tau == 0 else scol
                    if di == 0:
                        ev_scan2[pa] = p.op("v", lambda v, init_ap=init_ap, lam2=lam2: v.tensor_tensor_scan(
                            hsc2[:], lam2[:], bsc2[:], init_ap, OP.mult, OP.add))
                    else:
                        ev_scan2[pa] = p.op("v", lambda v, init_ap=init_ap, lam2=lam2: v.tensor_tensor_scan(
                            hsc2[:], lam2[:, ::-1], bsc2[:, ::-1], init_ap, OP.mult, OP.add))
                    p.op("v", lambda v, scol=scol: v.tensor_copy(scol, hsc2[:, TB2 - 1:TB2]))
                    p.wait("v", ev_tmp2[pa])
                    if di == 0:
                        ev_t = p.op("v", lambda v, n=n, tmp2n=tmp2n: v.tensor_tensor(
                            tmp2n[:], hsc2[:], ccb[:, n * TB2:(n + 1) * TB2], OP.mult))
                    else:
                        ev_t = p.op("v", lambda v, n=n, tmp2n=tmp2n: v.tensor_tensor(
                            tmp2n[:], hsc2[:], ccb[:, n * TB2:(n + 1) * TB2][:, ::-1], OP.mult))
                    p.wait("g", ev_t)
                    ev_add = p.op("g", lambda g, tmp2n=tmp2n: g.tensor_tensor(
                        yac2[:], yac2[:], tmp2n[:], OP.add))
                    ev_tmp2[pa] = ev_add
                p.wait("v", ev_u2)
                p.wait("v", ev_add)
                if di == 0:
                    p.op("v", lambda v, di=di, ch=ch: v.scalar_tensor_tensor(
                        yac2[:], uch2[:], dvec_s[:, di * 16 + ch: di * 16 + ch + 1],
                        yac2[:], OP.mult, OP.add))
                else:
                    p.op("v", lambda v: v.tensor_copy(tmp2[:], yac2[:, ::-1]))
                    p.op("v", lambda v, di=di, ch=ch: v.scalar_tensor_tensor(
                        yac2[:], uch2[:], dvec_s[:, di * 16 + ch: di * 16 + ch + 1],
                        tmp2[:], OP.mult, OP.add))
                ev_y = p.op("v", lambda v, ch=ch: v.tensor_tensor(
                    yac2[:], yac2[:], zall[:, ch * TB2:(ch + 1) * TB2], OP.mult))
                p.wait("g", ev_y)
                ev_st2 = p.dma("g", ybr_dr[di][ch * 128:(ch + 1) * 128, tsl], yac2[:])
        p.barrier(("t", "v", "s", "q", "g"))
    p.sfree(st2); p.sfree(zall); p.sfree(uch2); p.sfree(yac2)
    p.sfree(tmp2d[1]); p.sfree(tmp2d[0]); p.sfree(hsc2); p.sfree(bsc2)
    p.sfree(lam2d[1]); p.sfree(lam2d[0]); p.sfree(dlu2); p.sfree(dlt2)
    p.sfree(ccb); p.sfree(bcb)
    p.barrier()

    ysum = p.sbuf_tmp("m_ysum", [128, 16 * T], BF16)
    outT_s = p.sbuf_tmp("m_outT_s", [128, 16 * D], BF16)
    ya = p.sbuf_tmp("m_ya", [128, T], BF16)
    ybb2 = p.sbuf_tmp("m_ybb2", [128, T], BF16)
    yt = p.sbuf_tmp("m_yt", [128, D], F32)
    for kc in range(16):
        p.dma("g", outT_s[:, kc * D:(kc + 1) * D], outT_d[kc * 128:(kc + 1) * 128, :])
    p.barrier()
    for kc in range(16):
        p.wait("q", p.ev("v"))
        ev1 = p.dma("q", ya[:], ybr_dr[0][kc * 128:(kc + 1) * 128, :])
        ev2 = p.dma("q", ybb2[:], ybr_dr[1][kc * 128:(kc + 1) * 128, :])
        p.wait("v", ev2)
        p.op("v", lambda v, kc=kc: v.tensor_tensor(
            ysum[:, kc * T:(kc + 1) * T], ya[:], ybb2[:], OP.add))
    p.barrier(("t", "v", "q"))
    ev_pst = None
    for ti in range(16):
        p.wait("q", p.ev("v"))
        ev_x = p.dma("q", ht[:], xres2[ti * 128:(ti + 1) * 128, :])
        p.wait("s", ev_pst)
        for nh2 in range(2):
            psx = (ps0, ps1)[nh2]
            def omm(t, ti=ti, nh2=nh2, psx=psx):
                for kc in range(16):
                    inst = t.matmul(psx, ysum[:, kc * T + ti * 128: kc * T + (ti + 1) * 128],
                                    outT_s[:, kc * D + nh2 * 512: kc * D + (nh2 + 1) * 512],
                                    start=(kc == 0), stop=(kc == 15))
                return inst
            ev_mm = p.op("t", omm)
            p.wait("s", ev_mm)
            ev_y = p.op("s", lambda s, nh2=nh2, psx=psx: s.copy(
                yt[:, nh2 * 512:(nh2 + 1) * 512], psx))
            p.wait("t", ev_y)
        p.wait("v", p.ev("s"))
        p.wait("s", p.ev("v"))
        emit_ln(p, yt, sq, mrow, rrow, D)
        p.op("v", lambda v: v.tensor_scalar(yt[:], yt[:], mrow[:, 0:1], rrow[:, 0:1],
                                            OP.subtract, OP.mult))
        p.wait("v", ev_x)
        ev_r = p.op("v", lambda v: v.tensor_tensor(yt[:], yt[:], ht[:], OP.add))
        p.wait("q", ev_r)
        ev_pst = p.dma("q", xres3[ti * 128:(ti + 1) * 128, :], yt[:])
        p.wait("v", ev_pst)
    p.sfree(yt); p.sfree(ybb2); p.sfree(ya); p.sfree(outT_s); p.sfree(ysum)
    p.barrier()


def mamba_host(z, core):
    out = dict(
        m_inpT=np.ascontiguousarray(z["L1_attn_in_proj_w"].T),
        m_outT=np.ascontiguousarray(z["L1_attn_out_proj_w"].T),
    )
    for di, sfx in enumerate(["", "_b"]):
        s_ = "fb"[di]
        out[f"m_xpwT{s_}"] = np.ascontiguousarray(z["L1_attn_x_proj_w" + sfx].T)
        out[f"m_dtpwT{s_}"] = np.ascontiguousarray(z["L1_attn_dt_proj_w" + sfx].T)
        out[f"m_dtb{s_}"] = z["L1_attn_dt_proj_b" + sfx]
        out[f"m_alog{s_}"] = z["L1_attn_A_log" if di == 0 else "L1_attn_A_b_log"]
        out[f"m_dvec{s_}"] = z["L1_attn_D" if di == 0 else "L1_attn_D_b"]
        out[f"m_convw{s_}"] = z["L1_attn_conv_w" + sfx]
        out[f"m_convb{s_}"] = z["L1_attn_conv_b" + sfx]
    return out


def attn_tables():
    pvec = np.arange(128)
    omega = 10000.0 ** (-(2.0 * pvec) / 256.0)
    i = np.arange(S, dtype=np.float64)
    ang = np.outer(omega, i)
    CP = np.sqrt(2.0) / 16.0
    a1 = np.zeros((256, S)); a2 = np.zeros((256, S))
    vtab = np.zeros((256, S))
    a1[0::2] = CP * np.sin(ang); a1[1::2] = CP * np.sin(ang)
    a2[0::2] = CP * np.cos(ang); a2[1::2] = -CP * np.cos(ang)
    vtab[0::2] = np.cos(ang); vtab[1::2] = np.sin(ang)
    masks = np.zeros((128, 4 * 512), np.float32)
    ii = np.arange(128)[:, None]; jj = np.arange(512)[None, :]
    for o in range(4):
        masks[:, o * 512:(o + 1) * 512] = (jj <= ii + o * 128).astype(np.float32)
    return a1.astype(np.float32), a2.astype(np.float32), vtab.astype(np.float32), \
        masks.astype(ml_dtypes.bfloat16)



def attn_host(z, core, x0):
    b, hg = core // 2, core % 2
    heads = slice(hg * 8, hg * 8 + 8)
    qkv = z["L0_attn_qkv"]
    wq_ = np.ascontiguousarray(qkv[:, 0, heads, :].reshape(D, 512))
    wk_ = np.ascontiguousarray(qkv[:, 1, heads, :].reshape(D, 512))
    wv_ = np.ascontiguousarray(qkv[:, 2, heads, :].reshape(D, 512))
    qb_ = np.ascontiguousarray(z["L0_attn_q_bias"][heads].reshape(512))
    pos = z["L0_attn_positional"]
    pt_ = np.ascontiguousarray(pos[:, heads, :].transpose(1, 2, 0))
    swap = np.arange(256) ^ 1
    pts_ = np.ascontiguousarray(pos[swap][:, heads, :].transpose(1, 2, 0))
    a1, a2, vtab, masks = attn_tables()
    return dict(
        x0b=np.ascontiguousarray(x0[b]),
        xown=np.ascontiguousarray(x0[b, hg * TC:(hg + 1) * TC]),
        wq=wq_, wk=wk_, wv=wv_, qb=qb_, pt=pt_, pts=pts_,
        a1=a1, a2=a2, vtab=vtab,
        wout=np.ascontiguousarray(z["L0_attn_out_kernel"]),
        outb=z["L0_attn_out_bias"].reshape(1, D),
        ident=np.eye(128).astype(ml_dtypes.bfloat16),
        flg=np.array([[float(hg)]], np.float32),
        masks=masks,
    )



def ffn_tables(hg):
    g0 = hg * 2048
    half_idx = np.arange(0, 64, 2, dtype=np.float64)
    inv_freq = 1.0 / 10000.0 ** (half_idx / 64.0)
    base = (half_idx + 0.4 * 64.0) / (1.4 * 64.0)
    def mk(trange):
        t = np.asarray(trange, dtype=np.float64)[:, None]
        fb = t * inv_freq[None, :]
        sc = base[None, :] ** ((t - 2048.0) / 256.0)
        fr = np.concatenate([fb, fb], 1)
        scl = np.concatenate([sc, sc], 1)
        return np.cos(fr), np.sin(fr), scl
    cq, sq_, scq = mk(range(g0, g0 + 2048))
    ck, sk_, sck = mk(range(g0 - 512, g0 + 2048))
    t8 = lambda a: np.tile(a.astype(np.float32), (1, 8))
    return (t8(cq * scq), t8(sq_ * scq), t8(ck / sck), t8(sk_ / sck))



def host_masks():
    ii = np.arange(128)[:, None]; jj = np.arange(512)[None, :]
    md = np.zeros((128, 4 * 512), np.float32)
    mp = np.zeros((128, 4 * 512), np.float32)
    for o in range(4):
        md[:, o * 512:(o + 1) * 512] = (jj <= ii + o * 128)
        mp[:, o * 512:(o + 1) * 512] = (jj >= ii + o * 128)
    return md.astype(ml_dtypes.bfloat16), mp.astype(ml_dtypes.bfloat16)



def ffn_host(z, core, x1):
    b, hg = core // 2, core % 2
    g0 = hg * TC
    if hg == 0:
        halo = np.zeros((HALO, D), np.float32)
    else:
        halo = np.ascontiguousarray(x1[b, g0 - HALO:g0])
    aq, bq, ak, bk = ffn_tables(hg)
    md, mp = host_masks()
    qs = np.stack([np.tile(z["L0_ffn_q_scale"][l], 8)[None] for l in range(2)])
    ks = np.stack([np.tile(z["L0_ffn_k_scale"][l], 8)[None] for l in range(2)])
    return dict(
        x1own=np.ascontiguousarray(x1[b, g0:g0 + TC]),
        x1halo=halo,
        wqkv=z["L0_ffn_wqkv"], qs=qs, ks=ks,
        wo=z["L0_ffn_wo"],
        lg=z["L0_ffn_ln_g"].reshape(2, 1, D), lb=z["L0_ffn_ln_b"].reshape(2, 1, D),
        w1=z["L0_ffn_w1"], b1=z["L0_ffn_b1"], w2=z["L0_ffn_w2"],
        b2=z["L0_ffn_b2"].reshape(2, 1, D),
        aq=aq, bq=bq, ak=ak, bk=bk,
        ident=np.eye(128).astype(ml_dtypes.bfloat16),
        masksd=md, masksp=mp,
        flg=np.array([[float(hg)]], np.float32),
    )



def moe_host(z, core):
    b, h = core // 2, core % 2
    NE_, R_, D_, FF_ = 8, 4, 1024, 4096
    return dict(
        wiT=np.ascontiguousarray(z["L1_ffn_wi_W"].T),
        woT=np.ascontiguousarray(z["L1_ffn_wo_W"].T),
        gateT=np.ascontiguousarray(z["L1_ffn_gate_W"].T),
        gateb=z["L1_ffn_gate_b"].reshape(1, NE_),
        wib=z["L1_ffn_wi_b"],
        wob=z["L1_ffn_wo_b"],
        AcatT=np.ascontiguousarray(z["L1_ffn_lora_A"].reshape(NE_ * R_, D_).T),
        Bcat=np.ascontiguousarray(z["L1_ffn_lora_B"].transpose(0, 2, 1).reshape(NE_ * R_, FF_)),
    )


_NC = None



TC = 2048
D = 1024


def fused_build():
    p = Prog()
    io = {}
    io["ident_d"] = p.dram("ident", [128, 128], BF16)
    io["ident32_d"] = p.dram("ident32", [128, 128], F32)
    io["flg_d"] = p.dram("flg", [1, 1])
    io["xres1"] = p.dram("xres1", [TC, D], kind="Internal")
    io["xres2"] = p.dram("xres2", [TC, D], kind="Internal")
    io["xres3"] = p.dram("xres3", [TC, D], kind="Internal")
    io["xout"] = p.dram("xout", [TC, D], kind="ExternalOutput")
    io["sps"] = p.psum("sps", [128, 1024], F32)
    io["ptb0"] = p.psum("ptb0", [128, 512], F32)
    io["ptb1"] = p.psum("ptb1", [128, 512], F32)
    io["avq"] = p.psum("avq", [128, 512], F32)
    io["obx"] = p.psum("obx", [128, 512], F32)
    io["tpb0"] = p.psum("tpb0", [128, 128], BF16)
    io["tpb1"] = p.psum("tpb1", [128, 128], BF16)
    p.constf(1e-5)
    p.constf(1.0)

    for emitfn in (attn_emit, ffn_emit, mamba_emit, moe_emit):
        p.push_scope()
        emitfn(p, io)
        p.barrier()
        p.pop_scope()
        p.barrier()
        p.new_epoch()
    p.wait("q", p.ev("q"))
    p.wait("g", p.ev("g"))
    return p


def fused_host(z, core, x0):
    b, hg = core // 2, core % 2
    im = {}
    a = attn_host(z, core, x0)
    im.update(a)
    f = ffn_host(z, core, x0)
    for k in ("x1own", "x1halo"):
        f.pop(k, None)
    im.update(f)
    im.update(mamba_host(z, core))
    mo = moe_host(z, core)
    mo.pop("xin", None)
    im.update(mo)
    im["ident"] = np.eye(128).astype(ml_dtypes.bfloat16)
    im["ident32"] = np.eye(128).astype(np.float32)
    im["flg"] = np.array([[float(hg)]], np.float32)
    return im


def kernel(**inputs):
    z = {k: np.asarray(v) for k, v in inputs.items()}
    x0 = z["L0_attn_input"]
    global _NC
    if _NC is None:
        _NC = fused_build().finalize()
    nc = _NC
    cores = list(range(8))
    in_maps = [fused_host(z, c, x0) for c in cores]
    res = run_bass_kernel_spmd(nc, in_maps, core_ids=cores)
    x4 = np.stack([
        np.concatenate([res.results[2 * b]["xout"], res.results[2 * b + 1]["xout"]], 0)
        for b in range(4)])
    return x4.astype(np.float32)




# version 12
# speedup vs baseline: 1.0548x; 1.0548x over previous
import numpy as np
import contextlib
import ml_dtypes
import concourse.bass as bass
import concourse.mybir as mybir
from concourse.bass_utils import run_bass_kernel_spmd
from concourse.alu_op_type import AluOpType as OP

S = 4096


F32 = mybir.dt.float32
BF16 = mybir.dt.bfloat16
AF = mybir.ActivationFunctionType
AX = mybir.AxisListType

ENGS = ("t", "v", "s", "q", "g")


class Prog:
    def __init__(self, name="prog"):
        self.nc = bass.Bass("TRN2", target_bir_lowering=False)
        self.st = contextlib.ExitStack()
        self.queues = {e: [] for e in ENGS}
        self._epoch = 0
        self.cur = {e: e + "0" for e in ENGS}
        self.cnt = {}
        for e in ENGS:
            self.cnt[e + "0"] = 0
        self.sems = {}
        self._finalized = False

    def new_epoch(self):
        self._epoch += 1
        for e in ENGS:
            name = e + str(self._epoch)
            self.cur[e] = name
            self.cnt[name] = 0

    def dram(self, name, shape, dtype=F32, kind="ExternalInput", **kw):
        return self.nc.dram_tensor(name, list(shape), dtype, kind=kind, **kw).ap()

    def sbuf(self, name, shape, dtype=F32):
        if getattr(self, "_scope", None) is not None:
            h = self.sbuf_tmp(name, shape, dtype)
            self._scope.append(h)
            return h
        return self.st.enter_context(self.nc.sbuf_tensor(name, list(shape), dtype))

    def push_scope(self):
        assert getattr(self, "_scope", None) is None
        self._scope = []
        if not hasattr(self, "_tmp_cms"):
            self._tmp_cms = {}
        self._scope_base = set(self._tmp_cms.keys())

    def pop_scope(self):
        newids = [k for k in self._tmp_cms.keys() if k not in self._scope_base]
        for k in reversed(newids):
            cm = self._tmp_cms.pop(k)
            cm.__exit__(None, None, None)
        self._scope = None

    def psum(self, name, shape, dtype=F32):
        return self.st.enter_context(self.nc.psum_tensor(name, list(shape), dtype))

    def _sem(self, name):
        if name not in self.sems:
            self.sems[name] = self.st.enter_context(self.nc.semaphore(name))
        return self.sems[name]

    def raw(self, eng, fn):
        self.queues[eng].append(fn)

    def ev(self, eng):
        return (self.cur[eng], self.cnt[self.cur[eng]])

    def op(self, eng, fn, n_inc=1, chain=True):
        sem_name = self.cur[eng]
        if chain and eng in ("v", "s") and self.cnt[sem_name] > 0:
            prev = (sem_name, self.cnt[sem_name])
            self.queues[eng].append(
                lambda h, s=prev[0], v=prev[1]: h.wait_ge(self._sem(s), v))
        self.cnt[sem_name] += n_inc
        val = self.cnt[sem_name]

        def wrapped(h, fn=fn, n_inc=n_inc, sem_name=sem_name):
            inst = fn(h)
            assert inst is not None, "op closure must return the last instruction"
            inst.then_inc(self._sem(sem_name), n_inc)

        self.queues[eng].append(wrapped)
        return (sem_name, val)

    def quiet(self, eng, fn):
        self.queues[eng].append(fn)

    def wait(self, eng, ev):
        if ev is None:
            return
        sem_name, val = ev
        if val <= 0:
            return
        self.queues[eng].append(
            lambda h, s=sem_name, v=val: h.wait_ge(self._sem(s), v)
        )

    def barrier(self, engs=("t", "v", "s", "q", "g")):
        evs = {e: (self.cur[e], self.cnt[self.cur[e]]) for e in engs}
        for e in engs:
            for o in engs:
                if (o != e or e in ("q", "g")) and evs[o][1] > 0:
                    self.wait(e, evs[o])

    def dma(self, qeng, out, in_, **kw):
        assert qeng in ("q", "g", "v", "s")
        sem_name = self.cur[qeng]
        self.cnt[sem_name] += 16
        val = self.cnt[sem_name]

        def wrapped(h, out=out, in_=in_, kw=kw, sem_name=sem_name):
            h.dma_start(out=out, in_=in_, **kw).then_inc(self._sem(sem_name), 16)

        self.queues[qeng].append(wrapped)
        return (sem_name, val)

    def collective(self, kind, op, replica_groups, ins, outs):
        name = "cc"
        self.cnt.setdefault(name, 0)
        self.cnt[name] += 1
        val = self.cnt[name]

        def wrapped(h):
            h.collective_compute(kind, op, replica_groups=replica_groups,
                                 ins=ins, outs=outs).then_inc(self._sem(name))

        self.queues["g"].append(wrapped)
        return (name, val)

    def sbuf_tmp(self, name, shape, dtype=F32):
        cm = self.nc.sbuf_tensor(name, list(shape), dtype)
        h = cm.__enter__()
        if not hasattr(self, "_tmp_cms"):
            self._tmp_cms = {}
        self._tmp_cms[id(h)] = cm
        return h

    def sfree(self, h):
        cm = self._tmp_cms.pop(id(h))
        cm.__exit__(None, None, None)

    def constf(self, value, dtype=F32):
        if not hasattr(self, "_consts"):
            self._consts = {}
            self._const_tile = self.sbuf("consts", [128, 32], F32)
            self._const_n = 0
        key = (float(value), dtype)
        if key not in self._consts:
            idx = self._const_n
            assert idx < 32
            self._const_n += 1
            ap = self._const_tile[:, idx:idx + 1]
            self.op("v", lambda v, ap=ap, value=value: v.memset(ap, float(value)))
            self._consts[key] = ap
        return self._consts[key]

    def finalize(self):
        assert not self._finalized
        self._finalized = True
        if hasattr(self, "_tmp_cms"):
            for cm in reversed(list(self._tmp_cms.values())):
                cm.__exit__(None, None, None)
            self._tmp_cms = {}
        nc = self.nc
        with self.st:
            with nc.Block() as block:
                decos = dict(t=block.tensor, v=block.vector, s=block.scalar,
                             q=block.sync, g=block.gpsimd)
                for e in ENGS:
                    if not self.queues[e]:
                        continue
                    def body(h, e=e):
                        for fn in self.queues[e]:
                            fn(h)
                    decos[e](body)
        return nc



def to2d(x):
    x = np.ascontiguousarray(x)
    return x.reshape(-1, x.shape[-1])


def emit_ln(p, x_t, sq, m_t, rstd_t, F, eps=1e-5):
    p.op("v", lambda v: v.tensor_reduce(m_t[:], x_t[:], axis=AX.X, op=OP.add))
    ev_sq = p.op("s", lambda s: s.activation(sq[:, 0:F], x_t[:], AF.Square,
                                             accum_out=rstd_t[:]))
    p.op("v", lambda v: v.tensor_scalar(m_t[:], m_t[:], 1.0 / F, None, OP.mult))
    p.wait("v", ev_sq)
    p.op("v", lambda v: v.tensor_scalar(rstd_t[:], rstd_t[:], 1.0 / F, None, OP.mult))
    p.op("v", lambda v: v.tensor_tensor(sq[:, 0:1], m_t[:], m_t[:], OP.mult))
    p.op("v", lambda v: v.tensor_tensor(rstd_t[:], rstd_t[:], sq[:, 0:1], OP.subtract))
    ev_v = p.op("v", lambda v: v.tensor_copy(sq[:, 1:2], rstd_t[:]))
    p.wait("s", ev_v)
    ev_s = p.op("s", lambda s: s.activation(rstd_t[:], sq[:, 1:2], AF.Sqrt,
                                            bias=p.constf(eps)))
    p.wait("v", ev_s)
    return p.op("v", lambda v: v.reciprocal(rstd_t[:], rstd_t[:]))



D = 1024
SA = 4096
NH = 8
TC = 2048
C1 = (3.0 * D * D) ** (-0.25)
C2 = (D * D) ** (-0.25)
HALO = 768
OFF = HALO
HID = 2730
GROUPS = [[0, 1], [2, 3], [4, 5], [6, 7]]
FF = 4096
NE = 8
R = 4
L = 4096
DI = 2048
DS = 16
DTR = 64


def hid_chunks():
    out = [(c * 128, 128) for c in range(21)]
    out.append((21 * 128, 42))
    return out

def attn_emit(p, io):
    x0b = p.dram("x0b", [SA, D])
    xown = p.dram("xown", [TC, D])
    wq = p.dram("wq", [D, 512])
    wk = p.dram("wk", [D, 512])
    wv = p.dram("wv", [D, 512])
    qb = p.dram("qb", [512])
    pt_d = p.dram("pt", [NH, 64, 256])
    pts_d = p.dram("pts", [NH, 64, 256])
    a1_d = p.dram("a1", [256, SA])
    a2_d = p.dram("a2", [256, SA])
    vtab_d = p.dram("vtab", [256, SA])
    wout_d = p.dram("wout", [D, D])
    outb_d = p.dram("outb", [1, D])
    masks_d = p.dram("masks", [128, 4 * 512], BF16)
    x1out = io['xres1']

    qfm_dr = p.dram("qfm_dr", [512, SA], BF16, kind="Internal")
    kfm_dr = p.dram("kfm_dr", [512, SA], BF16, kind="Internal")
    cin = p.dram("cc_in", [512, TC], BF16, kind="Internal")
    cout = p.dram("cc_out", [1024, TC], BF16, kind="Internal")

    ps0 = io["sps"][:, 0:512]
    ps1 = io["sps"][:, 512:1024]
    ptb0 = io["ptb0"]
    ptb1 = io["ptb1"]
    avp = io["avq"][0:64, 0:128]
    avpB = io["obx"][0:64, 0:128]
    tpb0 = io["tpb0"]
    tpb1 = io["tpb1"]
    ident_d = io["ident_d"]
    flg_d = io["flg_d"]

    ident = p.sbuf("ident_s", [128, 128], BF16)
    masks = p.sbuf("masks_s", [128, 4 * 512], BF16)
    ofm = p.sbuf("ofm", [128, 4 * SA], BF16)
    qbs = p.sbuf("qbs", [128, 4], F32)
    sq = p.sbuf("sq", [128, D], F32)
    mrow = p.sbuf("mrow", [128, 1], F32)
    rrow = p.sbuf("rrow", [128, 1], F32)
    mneg = p.sbuf("mneg", [128, 1], F32)
    rs = p.sbuf("rs", [128, 1], F32)
    rsp = p.sbuf("rsp", [128, 1], F32)
    rinv = p.sbuf("rinv", [128, 1], F32)
    Dg = p.sbuf("Dg", [128, 128], BF16)
    ptsb = p.sbuf("ptsb", [128, 512], BF16)
    p.constf(1e-5)

    vtm = p.sbuf_tmp("vtm", [128, 32 * 512], BF16)
    a1s = p.sbuf_tmp("a1s", [128, 2 * SA], BF16)
    a2s = p.sbuf_tmp("a2s", [128, 2 * SA], BF16)
    vts = p.sbuf_tmp("vts", [128, 2 * SA], BF16)
    prow = p.sbuf_tmp("prow", [128, SA], BF16)
    prowB = p.sbuf_tmp("prowB", [128, SA], BF16)
    ptsbB = p.sbuf_tmp("ptsbB", [128, 512], BF16)
    DgB = p.sbuf_tmp("DgB", [128, 128], BF16)
    mrowB = p.sbuf_tmp("mrowB", [128, 1], F32)
    mnegB = p.sbuf_tmp("mnegB", [128, 1], F32)
    rsB = p.sbuf_tmp("rsB", [128, 1], F32)
    rspB = p.sbuf_tmp("rspB", [128, 1], F32)
    rinvB = p.sbuf_tmp("rinvB", [128, 1], F32)

    wq_s = p.sbuf_tmp("wq_s", [128, 8 * 512], BF16)
    wk_s = p.sbuf_tmp("wk_s", [128, 8 * 512], BF16)
    wv_s = p.sbuf_tmp("wv_s", [128, 8 * 512], BF16)
    xfm = p.sbuf_tmp("xfm", [128, 8 * 512], BF16)
    xt = p.sbuf_tmp("xt", [128, D], F32)
    xn = p.sbuf_tmp("xn", [128, D], BF16)
    est = p.sbuf_tmp("est", [128, 512], BF16)

    for kc in range(8):
        p.dma("g", wq_s[:, kc * 512:(kc + 1) * 512], wq[kc * 128:(kc + 1) * 128, :])
        p.dma("g", wk_s[:, kc * 512:(kc + 1) * 512], wk[kc * 128:(kc + 1) * 128, :])
        p.dma("g", wv_s[:, kc * 512:(kc + 1) * 512], wv[kc * 128:(kc + 1) * 128, :])
    for ch in range(2):
        p.dma("g", a1s[:, ch * SA:(ch + 1) * SA], a1_d[ch * 128:(ch + 1) * 128, :])
        p.dma("g", a2s[:, ch * SA:(ch + 1) * SA], a2_d[ch * 128:(ch + 1) * 128, :])
        p.dma("g", vts[:, ch * SA:(ch + 1) * SA], vtab_d[ch * 128:(ch + 1) * 128, :])
    p.dma("q", ident[:], ident_d)
    p.dma("q", masks[:], masks_d)
    p.dma("q", qbs[:], qb.rearrange("(c p) -> p c", p=128), allow_slow_non_contiguous=True)
    p.op("v", lambda v: v.tensor_scalar(qbs[:], qbs[:], 0.125, None, OP.mult))
    p.barrier()

    for blk in range(8):
        for i in range(4):
            t0 = blk * 512 + i * 128
            p.wait("q", p.ev("v"))
            p.wait("q", p.ev("s"))
            ev_x = p.dma("q", xt[:], x0b[t0:t0 + 128, :])
            p.wait("v", ev_x)
            p.wait("s", ev_x)
            emit_ln(p, xt, sq, mrow, rrow, D)
            p.wait("v", p.ev("t"))
            ev_xn = p.op("v", lambda v: v.tensor_scalar(
                xn[:], xt[:], mrow[:, 0:1], rrow[:, 0:1], OP.subtract, OP.mult))
            ev_tp = {0: None, 1: None}
            for kc in range(8):
                tp = (tpb0, tpb1)[kc % 2]
                p.wait("t", ev_xn)
                p.wait("t", ev_tp[kc % 2])
                ev_t = p.op("t", lambda t, kc=kc, tp=tp: t.transpose(
                    tp[:], xn[:, kc * 128:(kc + 1) * 128], ident[:]))
                p.wait("v", ev_t)
                ev_e = p.op("v", lambda v, tp=tp, kc=kc, i=i: v.tensor_copy(
                    xfm[:, kc * 512 + i * 128: kc * 512 + (i + 1) * 128], tp[:]))
                ev_tp[kc % 2] = ev_e
        p.barrier(("t", "v", "s"))
        ev_est_dma = None
        for mc in range(4):
            for w_s, scale_qk, dst, use_bias in ((wq_s, 0.125 * C1, qfm_dr, True),
                                                 (wk_s, C1, kfm_dr, False)):
                psx = (ps0, ps1)[mc % 2]
                def pmm(t, w_s=w_s, mc=mc, psx=psx):
                    for kc in range(8):
                        inst = t.matmul(psx, w_s[:, kc * 512 + mc * 128: kc * 512 + (mc + 1) * 128],
                                        xfm[:, kc * 512:(kc + 1) * 512],
                                        start=(kc == 0), stop=(kc == 7))
                    return inst
                ev_mm = p.op("t", pmm)
                p.wait("s", ev_mm)
                p.wait("s", ev_est_dma)
                ev_ev = p.op("s", lambda s, psx=psx, scale_qk=scale_qk: s.activation(
                    est[:], psx, AF.Copy, scale=scale_qk))
                if use_bias:
                    p.wait("v", ev_ev)
                    p.wait("v", ev_est_dma)
                    ev_ev = p.op("v", lambda v, mc=mc: v.tensor_scalar(
                        est[:], est[:], qbs[:, mc:mc + 1], None, OP.add))
                    p.wait("q", ev_ev)
                else:
                    p.wait("q", ev_ev)
                ev_est_dma = p.dma("q", dst[mc * 128:(mc + 1) * 128, blk * 512:(blk + 1) * 512], est[:])
                p.wait("t", ev_ev)
        for i in range(4):
            psx = (ps0, ps1)[i % 2]
            def vmm(t, i=i, psx=psx):
                for kc in range(8):
                    inst = t.matmul(psx, xfm[:, kc * 512 + i * 128: kc * 512 + (i + 1) * 128],
                                    wv_s[:, kc * 512:(kc + 1) * 512],
                                    start=(kc == 0), stop=(kc == 7))
                return inst
            ev_mm = p.op("t", vmm)
            p.wait("s", ev_mm)
            ev_vv = p.op("s", lambda s, blk=blk, i=i, psx=psx: s.activation(
                vtm[:, (blk * 4 + i) * 512:(blk * 4 + i + 1) * 512], psx,
                AF.Copy, scale=C1))
            p.wait("t", ev_vv)
        p.barrier(("t", "v", "s"))
    p.barrier()
    p.sfree(est); p.sfree(xn); p.sfree(xt); p.sfree(xfm)
    p.sfree(wv_s); p.sfree(wk_s); p.sfree(wq_s)

    qh = p.sbuf_tmp("qh", [64, SA], BF16)
    kh = p.sbuf_tmp("kh", [64, SA], BF16)
    pth = p.sbuf_tmp("pth", [64, 256], BF16)
    ptsh = p.sbuf_tmp("ptsh", [64, 256], BF16)
    Us = p.sbuf_tmp("Us", [128, 2 * SA], BF16)
    t1s = p.sbuf_tmp("t1s", [128, 512], F32)

    for h in range(NH):
        p.dma("g", qh[:], qfm_dr[h * 64:(h + 1) * 64, :])
        p.dma("g", kh[:], kfm_dr[h * 64:(h + 1) * 64, :])
        p.dma("g", pth[:], pt_d[h])
        p.dma("g", ptsh[:], pts_d[h])
        p.barrier(("t", "v", "s", "g"))
        for ch in range(2):
            for tt in range(8):
                ev_r = p.op("t", lambda t, ch=ch, tt=tt: t.matmul(
                    ps0, pth[:, ch * 128:(ch + 1) * 128],
                    qh[:, tt * 512:(tt + 1) * 512], start=True, stop=True))
                ev_rs2 = p.op("t", lambda t, ch=ch, tt=tt: t.matmul(
                    ps1, ptsh[:, ch * 128:(ch + 1) * 128],
                    qh[:, tt * 512:(tt + 1) * 512], start=True, stop=True))
                p.wait("v", ev_r)
                p.op("v", lambda v, ch=ch, tt=tt: v.tensor_tensor(
                    t1s[:], ps0, a1s[:, ch * SA + tt * 512: ch * SA + (tt + 1) * 512], OP.mult))
                p.wait("v", ev_rs2)
                p.op("v", lambda v, ch=ch, tt=tt: v.tensor_tensor(
                    ptsb[:], ps1, a2s[:, ch * SA + tt * 512: ch * SA + (tt + 1) * 512], OP.mult))
                ev_u = p.op("v", lambda v, ch=ch, tt=tt: v.tensor_tensor(
                    Us[:, ch * SA + tt * 512: ch * SA + (tt + 1) * 512], t1s[:],
                    ptsb[:, 0:512], OP.add))
                p.wait("t", ev_u)
        p.barrier(("t", "v", "s"))
        for qi in range(32):
            dk = (qi * 128) // 512
            def smm(t, qi, kt, psx):
                t.matmul(psx, qh[:, qi * 128:(qi + 1) * 128],
                         kh[:, kt * 512:(kt + 1) * 512], start=True, stop=False)
                t.matmul(psx, Us[:, 0 * SA + qi * 128: 0 * SA + (qi + 1) * 128],
                         vts[:, 0 * SA + kt * 512: 0 * SA + (kt + 1) * 512], start=False, stop=False)
                return t.matmul(psx, Us[:, 1 * SA + qi * 128: 1 * SA + (qi + 1) * 128],
                                vts[:, 1 * SA + kt * 512: 1 * SA + (kt + 1) * 512], start=False, stop=True)
            ev_sd = p.op("t", lambda t, qi=qi, dk=dk: smm(t, qi, dk, ps0))
            p.wait("v", ev_sd)
            p.wait("v", p.ev("s"))
            p.op("v", lambda v: v.tensor_reduce(mrow[:], ps0, axis=AX.X, op=OP.max))
            ev_mn = p.op("v", lambda v: v.tensor_scalar(mneg[:], mrow[:], -1.0, None, OP.mult))
            p.wait("s", ev_mn)
            ev_pd = p.op("s", lambda s, dk=dk: s.activation(
                prow[:, dk * 512:(dk + 1) * 512], ps0, AF.Exp, bias=mneg[:, 0:1]))
            p.wait("v", ev_pd)
            mo = qi % 4
            p.op("v", lambda v, dk=dk, mo=mo: v.tensor_tensor(
                prow[:, dk * 512:(dk + 1) * 512], prow[:, dk * 512:(dk + 1) * 512],
                masks[:, mo * 512:(mo + 1) * 512], OP.mult))
            p.op("v", lambda v, dk=dk: v.tensor_reduce(
                rs[:], prow[:, dk * 512:(dk + 1) * 512], axis=AX.X, op=OP.add))
            ev_exp_prev = ev_pd
            for kt in range(dk):
                psx = (ps0, ps1)[(kt + 1) % 2]
                p.wait("t", ev_exp_prev)
                ev_sk = p.op("t", lambda t, qi=qi, kt=kt, psx=psx: smm(t, qi, kt, psx))
                p.wait("s", ev_sk)
                ev_exp_prev = p.op("s", lambda s, kt=kt, psx=psx: s.activation(
                    prow[:, kt * 512:(kt + 1) * 512], psx, AF.Exp, bias=mneg[:, 0:1],
                    accum_out=rsp[:]))
                p.wait("v", ev_exp_prev)
                p.op("v", lambda v: v.tensor_tensor(rs[:], rs[:], rsp[:], OP.add))
            p.op("v", lambda v: v.reciprocal(rinv[:], rs[:]))
            ev_d = p.op("v", lambda v: v.tensor_scalar(Dg[:], ident[:], rinv[:, 0:1],
                                                       None, OP.mult))
            p.wait("t", ev_d)
            nsub = (dk + 1) * 4
            ev_pt_evac = {0: None, 1: None}
            ev_av = None
            for g in range((nsub + 3) // 4):
                ptp = (ptb0, ptb1)[g % 2]
                p.wait("t", ev_pt_evac[g % 2])
                ngrp = min(4, nsub - g * 4)
                def ptmm(t, g=g, ngrp=ngrp, ptp=ptp):
                    for jj in range(ngrp):
                        inst = t.matmul(ptp[:, jj * 128:(jj + 1) * 128],
                                        prow[:, (g * 4 + jj) * 128:(g * 4 + jj + 1) * 128],
                                        Dg[:], start=True, stop=True)
                    return inst
                ev_ptm = p.op("t", ptmm)
                eng = ("v", "s")[g % 2]
                p.wait(eng, ev_ptm)
                p.wait(eng, ev_av)
                if eng == "v":
                    ev_ev = p.op("v", lambda v, ngrp=ngrp, ptp=ptp: v.tensor_copy(
                        ptsb[:, 0:ngrp * 128], ptp[:, 0:ngrp * 128]))
                else:
                    ev_ev = p.op("s", lambda s, ngrp=ngrp, ptp=ptp: s.copy(
                        ptsb[:, 0:ngrp * 128], ptp[:, 0:ngrp * 128]))
                ev_pt_evac[g % 2] = ev_ev
                p.wait("t", ev_ev)
                def avmm(t, g=g, ngrp=ngrp, h=h, nsub=nsub):
                    for jj in range(ngrp):
                        ks = g * 4 + jj
                        inst = t.matmul(avp, vtm[:, ks * 512 + h * 64: ks * 512 + (h + 1) * 64],
                                        ptsb[:, jj * 128:(jj + 1) * 128],
                                        start=(g == 0 and jj == 0),
                                        stop=(g * 4 + jj == nsub - 1))
                    return inst
                ev_av = p.op("t", avmm)
            pairc = (h // 2) * SA
            base = 64 * (h % 2)
            p.wait("s", ev_av)
            ev_o = p.op("s", lambda s, pairc=pairc, base=base, qi=qi: s.activation(
                ofm[base:base + 64, pairc + qi * 128: pairc + (qi + 1) * 128],
                avp, AF.Copy, scale=C2))
            p.wait("t", ev_o)
        p.barrier(("t", "v", "s", "g"))
    p.sfree(t1s); p.sfree(Us); p.sfree(ptsh); p.sfree(pth); p.sfree(kh); p.sfree(qh)
    p.sfree(rinvB); p.sfree(rspB); p.sfree(rsB); p.sfree(mnegB); p.sfree(mrowB)
    p.sfree(DgB); p.sfree(ptsbB); p.sfree(prowB)
    p.sfree(prow); p.sfree(vts); p.sfree(a2s); p.sfree(a1s); p.sfree(vtm)

    p.barrier()
    flagb = p.sbuf_tmp("flagb", [128, 1], F32)
    finv = p.sbuf_tmp("finv", [128, 1], F32)
    selst = p.sbuf_tmp("selst", [128, TC], BF16)
    sel2 = p.sbuf_tmp("sel2", [128, TC], BF16)
    ownloc = p.sbuf_tmp("ownloc", [128, 4 * TC], BF16)
    p.dma("q", flagb[:], flg_d[0].partition_broadcast(128))
    p.wait("v", p.ev("q"))
    p.op("v", lambda v: v.tensor_scalar(finv[:], flagb[:], -1.0, 1.0, OP.mult, OP.add))
    for pr in range(4):
        p.op("v", lambda v, pr=pr: v.tensor_scalar(
            selst[:], ofm[:, pr * SA: pr * SA + TC], flagb[:, 0:1], None, OP.mult))
        p.op("v", lambda v, pr=pr: v.tensor_scalar(
            sel2[:], ofm[:, pr * SA + TC: pr * SA + 2 * TC], finv[:, 0:1], None, OP.mult))
        ev_c = p.op("v", lambda v: v.tensor_tensor(selst[:], selst[:], sel2[:], OP.add))
        p.wait("q", ev_c)
        ev_cind = p.dma("q", cin[pr * 128:(pr + 1) * 128, :], selst[:])
        p.wait("v", ev_cind)
        p.op("v", lambda v, pr=pr: v.tensor_scalar(
            selst[:], ofm[:, pr * SA: pr * SA + TC], finv[:, 0:1], None, OP.mult))
        p.op("v", lambda v, pr=pr: v.tensor_scalar(
            sel2[:], ofm[:, pr * SA + TC: pr * SA + 2 * TC], flagb[:, 0:1], None, OP.mult))
        ev_o = p.op("v", lambda v, pr=pr: v.tensor_tensor(
            ownloc[:, pr * TC:(pr + 1) * TC], selst[:], sel2[:], OP.add))
    p.wait("g", p.ev("q"))
    p.wait("g", p.ev("v"))
    ev_cc = p.collective("AllGather", mybir.AluOpType.bypass,
                         [[0, 1], [2, 3], [4, 5], [6, 7]], [cin], [cout])
    p.wait("q", ev_cc)

    wout_s = p.sbuf_tmp("wout_s", [128, 8 * D], BF16)
    ofull = p.sbuf_tmp("ofull", [128, 8 * TC], BF16)
    obbc = p.sbuf_tmp("obbc", [128, D], F32)
    yt = p.sbuf_tmp("yt", [128, D], F32)
    xt2 = p.sbuf_tmp("xt2", [128, D], F32)
    for kc in range(8):
        p.dma("g", wout_s[:, kc * D:(kc + 1) * D], wout_d[kc * 128:(kc + 1) * 128, :])
    p.dma("g", obbc[:], outb_d[0].partition_broadcast(128))
    for c in range(8):
        p.wait("q", p.ev("v"))
        ev_l = p.dma("q", sel2[:], cout[c * 128:(c + 1) * 128, :])
        wl, wc = (finv, flagb) if c < 4 else (flagb, finv)
        p.op("v", lambda v, c=c, wl=wl: v.tensor_scalar(
            selst[:], ownloc[:, (c % 4) * TC:(c % 4 + 1) * TC], wl[:, 0:1], None, OP.mult))
        p.wait("v", ev_l)
        p.op("v", lambda v, c=c, wc=wc: v.tensor_scalar(
            sel2[:], sel2[:], wc[:, 0:1], None, OP.mult))
        p.op("v", lambda v, c=c: v.tensor_tensor(
            ofull[:, c * TC:(c + 1) * TC], selst[:], sel2[:], OP.add))
    p.barrier()
    for ti in range(16):
        ev_x = p.dma("q", xt2[:], xown[ti * 128:(ti + 1) * 128, :])
        for nh2 in range(2):
            psx = (ps0, ps1)[nh2]
            def omm(t, ti=ti, nh2=nh2, psx=psx):
                for kc in range(8):
                    inst = t.matmul(psx, ofull[:, kc * TC + ti * 128: kc * TC + (ti + 1) * 128],
                                    wout_s[:, kc * D + nh2 * 512: kc * D + (nh2 + 1) * 512],
                                    start=(kc == 0), stop=(kc == 7))
                return inst
            ev_mm = p.op("t", omm)
            p.wait("s", ev_mm)
            ev_y = p.op("s", lambda s, nh2=nh2, psx=psx: s.copy(
                yt[:, nh2 * 512:(nh2 + 1) * 512], psx))
            p.wait("t", ev_y)
        p.wait("v", p.ev("s"))
        p.op("v", lambda v: v.tensor_tensor(yt[:], yt[:], obbc[:], OP.add))
        p.wait("s", p.ev("v"))
        emit_ln(p, yt, sq, mrow, rrow, D)
        p.op("v", lambda v: v.tensor_scalar(yt[:], yt[:], mrow[:, 0:1], rrow[:, 0:1],
                                            OP.subtract, OP.mult))
        p.wait("v", ev_x)
        ev_res = p.op("v", lambda v: v.tensor_tensor(yt[:], yt[:], xt2[:], OP.add))
        p.wait("q", ev_res)
        ev_st = p.dma("q", x1out[ti * 128:(ti + 1) * 128, :], yt[:])
        p.wait("s", ev_st)
        p.wait("v", ev_st)
    p.wait("q", p.ev("q"))


def ffn_emit(p, io):
    x1own = io['xres1']
    x1halo = p.dram('x1halo_dr', [HALO, D], kind='Internal')
    wqkv_d = p.dram("wqkv", [2, D, 1536])
    qs_d = p.dram("qs", [2, 1, 512])
    ks_d = p.dram("ks", [2, 1, 512])
    wo_d = p.dram("wo", [2, 512, D])
    lg_d = p.dram("lg", [2, 1, D])
    lb_d = p.dram("lb", [2, 1, D])
    w1_d = p.dram("w1", [2, D, 2 * HID])
    b1_d = p.dram("b1", [2, 2 * HID])
    w2_d = p.dram("w2", [2, HID, D])
    b2_d = p.dram("b2", [2, 1, D])
    aq_d = p.dram("aq", [TC, 512])
    bq_d = p.dram("bq", [TC, 512])
    ak_d = p.dram("ak", [512 + TC, 512])
    bk_d = p.dram("bk", [512 + TC, 512])
    masks_d = p.dram("masksd", [128, 4 * 512], BF16)
    masksp_d = p.dram("masksp", [128, 4 * 512], BF16)
    xout = io['xres2']

    yfull = [p.dram(f"yfull{l}", [HALO + TC, D], kind="Internal") for l in range(2)]
    ymid = [p.dram(f"ymid{l}", [TC + 1, D], kind="Internal") for l in range(2)]
    cin_m = [p.dram(f"cinm{l}", [128, D], kind="Internal") for l in range(2)]
    cout_m = [p.dram(f"coutm{l}", [256, D], kind="Internal") for l in range(2)]
    cin_e = p.dram("cine", [HALO, D], kind="Internal")
    cout_e = p.dram("coute", [2 * HALO, D], kind="Internal")

    sps = io["sps"]
    ps0 = sps[:, 0:512]
    ps1 = sps[:, 512:1024]
    ptb0 = io["ptb0"]
    ptb1 = io["ptb1"]
    avp = io["avq"][0:64, 0:128]
    tpb0 = io["tpb0"]
    tpb1 = io["tpb1"]
    ident_d = io["ident_d"]
    flg_d = io["flg_d"]

    ident = p.sbuf("f_ident_s", [128, 128], BF16)
    flagb = p.sbuf("f_flagb", [128, 1], F32)
    sq = p.sbuf("f_sq", [128, D], F32)
    mrow = p.sbuf("f_mrow", [128, 1], F32)
    rrow = p.sbuf("f_rrow", [128, 1], F32)
    mneg = p.sbuf("f_mneg", [128, 1], F32)
    rs = p.sbuf("f_rs", [128, 1], F32)
    rsp = p.sbuf("f_rsp", [128, 8], F32)
    rinv = p.sbuf("f_rinv", [128, 1], F32)
    Dg = p.sbuf("f_Dg", [128, 128], BF16)
    prow = p.sbuf("f_prow", [128, 1024], BF16)
    ptsb = p.sbuf("f_ptsb", [128, 512], BF16)
    ht = p.sbuf("f_ht", [128, D], F32)
    htb = p.sbuf("f_htb", [128, D], BF16)
    hfm = p.sbuf("f_hfm", [128, 8 * 512], BF16)
    p.constf(1e-5)

    p.dma("q", ident[:], ident_d)
    p.dma("q", flagb[:], flg_d[0].partition_broadcast(128))
    p.barrier()

    cin_x1 = p.dram("cinx1", [HALO, D], kind="Internal")
    cout_x1 = p.dram("coutx1", [2 * HALO, D], kind="Internal")
    p.dma("q", cin_x1, x1own[TC - HALO:TC, :])
    p.wait("g", p.ev("q"))
    ev_cc0 = p.collective("AllGather", mybir.AluOpType.bypass, GROUPS,
                          [cin_x1], [cout_x1])
    p.wait("q", ev_cc0)
    for i in range(HALO // 128):
        ev_b0 = p.dma("q", ht[:], cout_x1[i * 128:(i + 1) * 128, :])
        p.wait("v", ev_b0)
        ev_bb0 = p.op("v", lambda v: v.tensor_scalar(ht[:], ht[:], flagb[:, 0:1],
                                                     None, OP.mult))
        p.wait("q", ev_bb0)
        ev_sx = p.dma("q", x1halo[i * 128:(i + 1) * 128, :], ht[:])
        p.wait("v", ev_sx)
    p.barrier()
    for r0, src, n in ((0, x1halo, HALO // 128), (OFF, x1own, TC // 128)):
        for i in range(n):
            p.wait("q", p.ev("v")); p.wait("q", p.ev("s"))
            ev_x = p.dma("q", ht[:], src[i * 128:(i + 1) * 128, :])
            p.wait("v", ev_x); p.wait("s", ev_x)
            emit_ln(p, ht, sq, mrow, rrow, D)
            ev_y = p.op("v", lambda v: v.tensor_scalar(
                ht[:], ht[:], mrow[:, 0:1], rrow[:, 0:1], OP.subtract, OP.mult))
            p.wait("q", ev_y)
            p.dma("q", yfull[0][r0 + i * 128: r0 + (i + 1) * 128, :], ht[:])
    p.barrier()

    for l in range(2):
        yf = yfull[l]
        masks = p.sbuf_tmp(f"masks_s{l}", [128, 4 * 512], BF16)
        masksp = p.sbuf_tmp(f"maskspg_s{l}", [128, 4 * 512], BF16)
        masksp0 = p.sbuf_tmp(f"masksp0_s{l}", [128, 4 * 512], BF16)
        p.dma("q", masks[:], masks_d)
        p.dma("q", masksp0[:], masksp_d)
        p.wait("v", p.ev("q"))
        for o in range(4):
            p.op("v", lambda v, o=o: v.tensor_scalar(
                masksp[:, o * 512:(o + 1) * 512], masksp0[:, o * 512:(o + 1) * 512],
                flagb[:, 0:1], None, OP.mult))
        wqkv_s = p.sbuf_tmp(f"wqkv_s{l}", [128, 8 * 1536], BF16)
        qtm = p.sbuf_tmp(f"qtm{l}", [128, 16 * 512], BF16)
        ktm = p.sbuf_tmp(f"ktm{l}", [128, 20 * 512], BF16)
        vtm = p.sbuf_tmp(f"vtm{l}", [128, 20 * 512], BF16)
        qfm = p.sbuf_tmp(f"qfm{l}", [128, 4 * TC], BF16)
        kfm = p.sbuf_tmp(f"kfm{l}", [128, 4 * 2560], BF16)
        tabs = p.sbuf_tmp(f"tabs{l}", [128, 2 * 512], BF16)
        qs_s = p.sbuf_tmp(f"qs_s{l}", [128, 512], F32)
        ks_s = p.sbuf_tmp(f"ks_s{l}", [128, 512], F32)
        nrm = p.sbuf_tmp(f"nrm{l}", [128, 8], F32)
        rot = p.sbuf_tmp(f"rot{l}", [128, 512], BF16)
        t3 = p.sbuf_tmp(f"t3{l}", [128, 512], BF16)

        for kc in range(8):
            p.dma("g", wqkv_s[:, kc * 1536:(kc + 1) * 1536],
                  wqkv_d[l, kc * 128:(kc + 1) * 128, :])
        p.dma("g", qs_s[:], qs_d[l, 0].partition_broadcast(128))
        p.dma("g", ks_s[:], ks_d[l, 0].partition_broadcast(128))
        p.barrier()

        for bi in range(5):
            r0 = OFF - 512 + bi * 512
            for i in range(4):
                rr = r0 + i * 128
                p.wait("q", p.ev("v"))
                ev1 = p.dma("q", ht[:, 0:512], yf[rr:rr + 128, 0:512])
                ev2 = p.dma("q", ht[:, 512:D], yf[rr - 1:rr + 127, 512:D])
                p.wait("v", ev2)
                p.wait("v", p.ev("t"))
                ev_hb = p.op("v", lambda v: v.tensor_copy(htb[:], ht[:]))
                ev_tp = {0: None, 1: None}
                for kc in range(8):
                    tp = (tpb0, tpb1)[kc % 2]
                    p.wait("t", ev_hb)
                    p.wait("t", ev_tp[kc % 2])
                    ev_t = p.op("t", lambda t, kc=kc, tp=tp: t.transpose(
                        tp[:], htb[:, kc * 128:(kc + 1) * 128], ident[:]))
                    p.wait("v", ev_t)
                    ev_e = p.op("v", lambda v, tp=tp, kc=kc, i=i: v.tensor_copy(
                        hfm[:, kc * 512 + i * 128: kc * 512 + (i + 1) * 128], tp[:]))
                    ev_tp[kc % 2] = ev_e
            p.barrier(("t", "v", "s"))
            parts = [("k", 512, ktm, bi), ("v", 1024, vtm, bi)]
            if bi >= 1:
                parts.append(("q", 0, qtm, bi - 1))
            for name, coff, dstm, bslot in parts:
                for i in range(4):
                    psx = (ps0, ps1)[i % 2]
                    def qmm(t, coff=coff, i=i, psx=psx):
                        for kc in range(8):
                            inst = t.matmul(psx, hfm[:, kc * 512 + i * 128: kc * 512 + (i + 1) * 128],
                                            wqkv_s[:, kc * 1536 + coff: kc * 1536 + coff + 512],
                                            start=(kc == 0), stop=(kc == 7))
                        return inst
                    ev_mm = p.op("t", qmm)
                    p.wait("s", ev_mm)
                    ev_ev = p.op("s", lambda s, dstm=dstm, bslot=bslot, i=i, psx=psx: s.activation(
                        dstm[:, (bslot * 4 + i) * 512:(bslot * 4 + i + 1) * 512], psx, AF.Copy))
                    p.wait("t", ev_ev)
            p.barrier(("t", "v", "s"))
        for which, tm, fm, ntile, tab_a, tab_b, sc_s, scale8, fmw in (
                ("q", qtm, qfm, 16, aq_d, bq_d, qs_s, 8.0, TC),
                ("k", ktm, kfm, 20, ak_d, bk_d, ks_s, 1.0, 2560)):
            for ti in range(ntile):
                sl = slice(ti * 512, (ti + 1) * 512)
                p.wait("g", p.ev("v"))
                p.dma("g", tabs[:, 0:512], tab_a[ti * 128:(ti + 1) * 128, :])
                ev_t2 = p.dma("g", tabs[:, 512:1024], tab_b[ti * 128:(ti + 1) * 128, :])
                ev_sqq = p.op("s", lambda s, tm=tm, sl=sl: s.activation(
                    sq[:, 0:512], tm[:, sl], AF.Square))
                p.wait("v", ev_sqq)
                for e in range(8):
                    p.op("v", lambda v, e=e: v.tensor_reduce(
                        nrm[:, e:e + 1], sq[:, e * 64:(e + 1) * 64],
                        axis=AX.X, op=OP.add, apply_absolute_value=None, negate=None))
                ev_n = p.op("v", lambda v: v.tensor_copy(rsp[:], nrm[:]))
                p.wait("s", ev_n)
                ev_sqn = p.op("s", lambda s: s.activation(nrm[:], rsp[:], AF.Sqrt))
                p.wait("v", ev_sqn)
                p.op("v", lambda v: v.tensor_scalar(nrm[:], nrm[:], 1e-12, None, OP.max))
                p.op("v", lambda v: v.reciprocal(nrm[:], nrm[:]))
                for e in range(8):
                    p.op("v", lambda v, tm=tm, sl=sl, e=e: v.tensor_scalar(
                        tm[:, sl][:, e * 64:(e + 1) * 64], tm[:, sl][:, e * 64:(e + 1) * 64],
                        nrm[:, e:e + 1], None, OP.mult))
                p.op("v", lambda v, tm=tm, sl=sl, sc_s=sc_s: v.tensor_tensor(
                    tm[:, sl], tm[:, sl], sc_s[:, 0:512], OP.mult))
                tv2 = tm[:, sl].rearrange("p (e two d) -> p e two d", two=2, d=32)
                rv = rot[:].rearrange("p (e two d) -> p e two d", two=2, d=32)
                p.op("v", lambda v, tv2=tv2, rv=rv: v.tensor_scalar(
                    rv[:, :, 0, :], tv2[:, :, 1, :], -1.0, None, OP.mult))
                p.op("v", lambda v, tv2=tv2, rv=rv: v.tensor_copy(rv[:, :, 1, :], tv2[:, :, 0, :]))
                p.wait("v", ev_t2)
                p.op("v", lambda v, tm=tm, sl=sl: v.tensor_tensor(
                    t3[:], tm[:, sl], tabs[:, 0:512], OP.mult))
                p.op("v", lambda v: v.tensor_tensor(rot[:], rot[:], tabs[:, 512:1024], OP.mult))
                ev_q = p.op("v", lambda v: v.tensor_tensor(t3[:], t3[:], rot[:], OP.add))
                if scale8 != 1.0:
                    ev_q = p.op("v", lambda v, scale8=scale8: v.tensor_scalar(
                        t3[:], t3[:], scale8, None, OP.mult))
                ev_tp = {0: None, 1: None}
                for c4 in range(4):
                    tp = (tpb0, tpb1)[c4 % 2]
                    p.wait("t", ev_q)
                    p.wait("t", ev_tp[c4 % 2])
                    ev_t = p.op("t", lambda t, c4=c4, tp=tp: t.transpose(
                        tp[:], t3[:, c4 * 128:(c4 + 1) * 128], ident[:]))
                    p.wait("v", ev_t)
                    ev_e = p.op("v", lambda v, c4=c4, tp=tp, fm=fm, ti=ti, fmw=fmw: v.tensor_copy(
                        fm[:, c4 * fmw + ti * 128: c4 * fmw + (ti + 1) * 128], tp[:]))
                    ev_tp[c4 % 2] = ev_e
        p.barrier(("t", "v", "s"))

        ofm = p.sbuf_tmp(f"ofm{l}", [128, 4 * TC], BF16)
        for h in range(NH):
            qbase = 64 * (h % 2)
            qcol = (h // 2)
            for qt in range(16):
                wl = qt // 4
                o4 = qt % 4
                def smm2(t, h=h, qt=qt, wl=wl, qbase=qbase, qcol=qcol):
                    for half in range(2):
                        inst = t.matmul(
                            sps[:, half * 512:(half + 1) * 512],
                            qfm[qbase:qbase + 64, qcol * TC + qt * 128: qcol * TC + (qt + 1) * 128],
                            kfm[qbase:qbase + 64,
                                qcol * 2560 + wl * 512 + half * 512:
                                qcol * 2560 + wl * 512 + (half + 1) * 512],
                            start=True, stop=True)
                    return inst
                p.wait("t", p.ev("s"))
                ev_s2 = p.op("t", smm2)
                p.wait("v", ev_s2)
                p.wait("v", p.ev("s"))
                p.op("v", lambda v: v.tensor_reduce(mrow[:], sps[:], axis=AX.X, op=OP.max))
                ev_mn = p.op("v", lambda v: v.tensor_scalar(mneg[:], mrow[:], -1.0, None, OP.mult))
                p.wait("s", ev_mn)
                ev_pd = p.op("s", lambda s: s.activation(prow[:], sps[:], AF.Exp, bias=mneg[:, 0:1]))
                p.wait("v", ev_pd)
                mpick = masksp if wl == 0 else masksp0
                p.op("v", lambda v, o4=o4, mpick=mpick: v.tensor_tensor(
                    prow[:, 0:512], prow[:, 0:512], mpick[:, o4 * 512:(o4 + 1) * 512], OP.mult))
                p.op("v", lambda v, o4=o4: v.tensor_tensor(
                    prow[:, 512:1024], prow[:, 512:1024], masks[:, o4 * 512:(o4 + 1) * 512], OP.mult))
                p.op("v", lambda v: v.tensor_reduce(rs[:], prow[:], axis=AX.X, op=OP.add))
                p.op("v", lambda v: v.reciprocal(rinv[:], rs[:]))
                ev_d = p.op("v", lambda v: v.tensor_scalar(Dg[:], ident[:], rinv[:, 0:1], None, OP.mult))
                p.wait("t", ev_d)
                ev_av = None
                ev_pt_evac = {0: None, 1: None}
                for g in range(2):
                    ptp = (ptb0, ptb1)[g % 2]
                    p.wait("t", ev_pt_evac[g % 2])
                    def ptmm(t, g=g, ptp=ptp):
                        for jj in range(4):
                            inst = t.matmul(ptp[:, jj * 128:(jj + 1) * 128],
                                            prow[:, (g * 4 + jj) * 128:(g * 4 + jj + 1) * 128],
                                            Dg[:], start=True, stop=True)
                        return inst
                    ev_ptm = p.op("t", ptmm)
                    eng = ("v", "s")[g % 2]
                    p.wait(eng, ev_ptm)
                    p.wait(eng, ev_av)
                    if eng == "v":
                        ev_ev = p.op("v", lambda v, ptp=ptp: v.tensor_copy(ptsb[:], ptp[:]))
                    else:
                        ev_ev = p.op("s", lambda s, ptp=ptp: s.copy(ptsb[:], ptp[:]))
                    ev_pt_evac[g % 2] = ev_ev
                    p.wait("t", ev_ev)
                    def avmm2(t, g=g, h=h, wl=wl):
                        for jj in range(4):
                            j = wl * 4 + g * 4 + jj
                            inst = t.matmul(avp,
                                            vtm[:, j * 512 + h * 64: j * 512 + (h + 1) * 64],
                                            ptsb[:, jj * 128:(jj + 1) * 128],
                                            start=(g == 0 and jj == 0),
                                            stop=(g == 1 and jj == 3))
                        return inst
                    ev_av = p.op("t", avmm2)
                p.wait("s", ev_av)
                ev_o = p.op("s", lambda s, h=h, qt=qt: s.activation(
                    ofm[64 * (h % 2):64 * (h % 2) + 64, (h // 2) * TC + qt * 128:(h // 2) * TC + (qt + 1) * 128],
                    avp, AF.Copy))
                p.wait("t", ev_o)
        p.barrier(("t", "v", "s"))

        wo_s = p.sbuf_tmp(f"wo_s{l}", [128, 4 * D], BF16)
        yt = p.sbuf_tmp(f"yt{l}", [128, D], F32)
        for kc in range(4):
            p.dma("g", wo_s[:, kc * D:(kc + 1) * D], wo_d[l, kc * 128:(kc + 1) * 128, :])
        p.barrier(("t", "v", "s", "g", "q"))
        ev_prev_store = None
        for ti in range(16):
            p.wait("q", p.ev("v"))
            ev_x = p.dma("q", ht[:], yf[OFF + ti * 128: OFF + (ti + 1) * 128, :])
            p.wait("s", ev_prev_store)
            for nh2 in range(2):
                psx = (ps0, ps1)[nh2]
                def omm(t, ti=ti, nh2=nh2, psx=psx):
                    for kc in range(4):
                        inst = t.matmul(psx, ofm[:, kc * TC + ti * 128: kc * TC + (ti + 1) * 128],
                                        wo_s[:, kc * D + nh2 * 512: kc * D + (nh2 + 1) * 512],
                                        start=(kc == 0), stop=(kc == 3))
                    return inst
                ev_mm = p.op("t", omm)
                p.wait("s", ev_mm)
                ev_y = p.op("s", lambda s, nh2=nh2, psx=psx: s.copy(
                    yt[:, nh2 * 512:(nh2 + 1) * 512], psx))
                p.wait("t", ev_y)
            p.wait("v", p.ev("s"))
            p.wait("v", ev_x)
            ev_r = p.op("v", lambda v: v.tensor_tensor(yt[:], yt[:], ht[:], OP.add))
            p.wait("q", ev_r)
            ev_prev_store = p.dma("q", ymid[l][1 + ti * 128: 1 + (ti + 1) * 128, :], yt[:])
        p.barrier()
        p.dma("q", cin_m[l], ymid[l][1 + 1920: 1 + 2048, :])
        p.wait("g", p.ev("q"))
        ev_cc = p.collective("AllGather", mybir.AluOpType.bypass, GROUPS,
                             [cin_m[l]], [cout_m[l]])
        p.wait("q", ev_cc)
        ev_b = p.dma("q", ht[0:1, :], cout_m[l][127:128, :])
        p.wait("v", ev_b)
        ev_bb = p.op("v", lambda v: v.tensor_scalar(ht[0:1, :], ht[0:1, :],
                                                    flagb[0:1, 0:1], None, OP.mult))
        p.wait("q", ev_bb)
        p.dma("q", ymid[l][0:1, :], ht[0:1, :])
        p.sfree(yt); p.sfree(wo_s); p.sfree(ofm)
        p.sfree(t3); p.sfree(rot); p.sfree(nrm); p.sfree(ks_s); p.sfree(qs_s)
        p.sfree(tabs); p.sfree(kfm); p.sfree(qfm)
        p.sfree(vtm); p.sfree(ktm); p.sfree(qtm); p.sfree(wqkv_s)
        p.sfree(masksp0); p.sfree(masksp); p.sfree(masks)
        p.barrier()

        w1_s = p.sbuf_tmp(f"w1_s{l}", [128, 8 * 2 * HID], BF16)
        b1c = p.sbuf_tmp(f"b1c{l}", [128, 44], F32)
        lgb = p.sbuf_tmp(f"lgb{l}", [128, D], F32)
        lbb = p.sbuf_tmp(f"lbb{l}", [128, D], F32)
        b2b = p.sbuf_tmp(f"b2b{l}", [128, D], F32)
        w2_s = p.sbuf_tmp(f"w2_s{l}", [128, 22 * D], BF16)
        xfm2 = p.sbuf_tmp(f"xfm2{l}", [128, 8 * 512], BF16)
        prodfm = p.sbuf_tmp(f"prodfm{l}", [128, 22 * 512], BF16)
        yt2 = p.sbuf_tmp(f"yt2{l}", [128, D], F32)
        for kc in range(8):
            p.dma("g", w1_s[:, kc * 2 * HID:(kc + 1) * 2 * HID],
                  w1_d[l, kc * 128:(kc + 1) * 128, :])
        for ci, (f0, fsz) in enumerate(hid_chunks()):
            p.dma("g", w2_s[0:fsz, ci * D:(ci + 1) * D], w2_d[l, f0:f0 + fsz, :])
        p.dma("g", lgb[:], lg_d[l, 0].partition_broadcast(128))
        p.dma("g", lbb[:], lb_d[l, 0].partition_broadcast(128))
        p.dma("g", b2b[:], b2_d[l, 0].partition_broadcast(128))
        for ci, (f0, fsz) in enumerate(hid_chunks()):
            p.dma("q", b1c[0:fsz, ci:ci + 1], b1_d[l, f0:f0 + fsz].rearrange("(f o) -> f o", o=1),
                  allow_slow_non_contiguous=True)
            p.dma("q", b1c[0:fsz, ci + 22:ci + 23],
                  b1_d[l, HID + f0: HID + f0 + fsz].rearrange("(f o) -> f o", o=1),
                  allow_slow_non_contiguous=True)
        p.barrier()
        for blk in range(4):
            for i in range(4):
                rr = blk * 512 + i * 128
                p.wait("q", p.ev("v")); p.wait("q", p.ev("s"))
                ev1 = p.dma("q", ht[:, 0:512], ymid[l][1 + rr: 1 + rr + 128, 0:512])
                ev2 = p.dma("q", ht[:, 512:D], ymid[l][rr: rr + 128, 512:D])
                p.wait("v", ev2); p.wait("s", ev2)
                emit_ln(p, ht, sq, mrow, rrow, D)
                p.op("v", lambda v: v.tensor_scalar(
                    ht[:], ht[:], mrow[:, 0:1], rrow[:, 0:1], OP.subtract, OP.mult))
                p.op("v", lambda v: v.tensor_tensor(ht[:], ht[:], lgb[:], OP.mult))
                p.wait("v", p.ev("t"))
                ev_xn = p.op("v", lambda v: v.tensor_tensor(htb[:], ht[:], lbb[:], OP.add))
                ev_tp = {0: None, 1: None}
                for kc in range(8):
                    tp = (tpb0, tpb1)[kc % 2]
                    p.wait("t", ev_xn)
                    p.wait("t", ev_tp[kc % 2])
                    ev_t = p.op("t", lambda t, kc=kc, tp=tp: t.transpose(
                        tp[:], htb[:, kc * 128:(kc + 1) * 128], ident[:]))
                    p.wait("v", ev_t)
                    ev_e = p.op("v", lambda v, tp=tp, kc=kc, i=i: v.tensor_copy(
                        xfm2[:, kc * 512 + i * 128: kc * 512 + (i + 1) * 128], tp[:]))
                    ev_tp[kc % 2] = ev_e
            p.barrier(("t", "v", "s"))
            for ci, (f0, fsz) in enumerate(hid_chunks()):
                def gmm(t, f0=f0, fsz=fsz):
                    for kc in range(8):
                        inst = t.matmul(sps[0:fsz, 0:512],
                                        w1_s[:, kc * 2 * HID + HID + f0: kc * 2 * HID + HID + f0 + fsz],
                                        xfm2[:, kc * 512:(kc + 1) * 512],
                                        start=(kc == 0), stop=(kc == 7))
                    return inst
                p.wait("t", p.ev("s"))
                p.wait("t", p.ev("v"))
                ev_g = p.op("t", gmm)
                p.wait("s", ev_g)
                ev_gel = p.op("s", lambda s, fsz=fsz, ci=ci: s.activation(
                    ptsb[0:fsz, :], sps[0:fsz, 0:512], AF.Gelu, bias=b1c[0:fsz, ci + 22:ci + 23]))
                def amm(t, f0=f0, fsz=fsz):
                    for kc in range(8):
                        inst = t.matmul(sps[0:fsz, 512:1024],
                                        w1_s[:, kc * 2 * HID + f0: kc * 2 * HID + f0 + fsz],
                                        xfm2[:, kc * 512:(kc + 1) * 512],
                                        start=(kc == 0), stop=(kc == 7))
                    return inst
                ev_a = p.op("t", amm)
                p.wait("v", ev_a)
                p.wait("v", ev_gel)
                p.op("v", lambda v, fsz=fsz, ci=ci: v.scalar_tensor_tensor(
                    prodfm[0:fsz, ci * 512:(ci + 1) * 512], sps[0:fsz, 512:1024],
                    b1c[0:fsz, ci:ci + 1], ptsb[0:fsz, :], OP.add, OP.mult))
            p.barrier(("t", "v", "s"))
            for i in range(4):
                rr = blk * 512 + i * 128
                p.wait("q", p.ev("v"))
                ev_x = p.dma("q", ht[:], ymid[l][1 + rr: 1 + rr + 128, :])
                p.wait("s", p.ev("q"))
                for nh2 in range(2):
                    psx = (ps0, ps1)[nh2]
                    ev_w2l = None
                    def w2mm(t, i=i, nh2=nh2, psx=psx):
                        for ci, (f0, fsz) in enumerate(hid_chunks()):
                            inst = t.matmul(psx,
                                            prodfm[0:fsz, ci * 512 + i * 128: ci * 512 + (i + 1) * 128],
                                            w2_s[0:fsz, ci * D + nh2 * 512: ci * D + (nh2 + 1) * 512],
                                            start=(ci == 0), stop=(ci == 21))
                        return inst
                    ev_mm = p.op("t", w2mm)
                    p.wait("s", ev_mm)
                    ev_y = p.op("s", lambda s, nh2=nh2, psx=psx: s.copy(
                        yt2[:, nh2 * 512:(nh2 + 1) * 512], psx))
                    p.wait("t", ev_y)
                p.wait("v", p.ev("s"))
                p.wait("v", ev_x)
                p.op("v", lambda v: v.tensor_tensor(yt2[:], yt2[:], b2b[:], OP.add))
                ev_r = p.op("v", lambda v: v.tensor_tensor(yt2[:], yt2[:], ht[:], OP.add))
                if l == 0:
                    p.wait("q", ev_r)
                    p.dma("q", yfull[1][OFF + rr: OFF + rr + 128, :], yt2[:])
                else:
                    p.wait("s", ev_r)
                    emit_ln(p, yt2, sq, mrow, rrow, D)
                    p.op("v", lambda v: v.tensor_scalar(
                        yt2[:], yt2[:], mrow[:, 0:1], rrow[:, 0:1], OP.subtract, OP.mult))
                    p.wait("q", p.ev("v"))
                    ev_x1 = p.dma("q", ht[:], x1own[rr:rr + 128, :])
                    p.wait("v", ev_x1)
                    ev_f = p.op("v", lambda v: v.tensor_tensor(yt2[:], yt2[:], ht[:], OP.add))
                    p.wait("q", ev_f)
                    p.dma("q", xout[rr:rr + 128, :], yt2[:])
        p.barrier()
        p.sfree(yt2); p.sfree(prodfm); p.sfree(xfm2)
        p.sfree(w2_s); p.sfree(b2b); p.sfree(lbb); p.sfree(lgb); p.sfree(b1c); p.sfree(w1_s)
        if l == 0:
            p.dma("q", cin_e, yfull[1][OFF + TC - HALO: OFF + TC, :])
            p.wait("g", p.ev("q"))
            ev_cc = p.collective("AllGather", mybir.AluOpType.bypass, GROUPS,
                                 [cin_e], [cout_e])
            p.wait("q", ev_cc)
            for i in range(HALO // 128):
                ev_b = p.dma("q", ht[:], cout_e[i * 128:(i + 1) * 128, :])
                p.wait("v", ev_b)
                ev_bb = p.op("v", lambda v: v.tensor_scalar(
                    ht[:], ht[:], flagb[:, 0:1], None, OP.mult))
                p.wait("q", ev_bb)
                ev_sst = p.dma("q", yfull[1][i * 128:(i + 1) * 128, :], ht[:])
                p.wait("v", ev_sst)
            p.barrier()
    p.wait("q", p.ev("q"))


def moe_emit(p, io):
    NBLK = TC // 512
    xin = io['xres3']
    wiT = p.dram("wiT", [D, FF])
    woT = p.dram("woT", [FF, D])
    gateT = p.dram("gateT", [D, NE])
    gateb = p.dram("gateb", [1, NE])
    wib = p.dram("wib", [FF])
    wob = p.dram("wob", [D])
    AcatT = p.dram("AcatT", [D, NE * R])
    Bcat = p.dram("Bcat", [NE * R, FF])
    ident_d = io["ident_d"]
    ident32_d = io["ident32_d"]
    xout = io['xout']

    wiT_s = p.sbuf("mo_wiT_s", [128, 8 * FF], BF16)
    woT_s = p.sbuf("mo_woT_s", [128, 32 * D], BF16)
    gateT_s = p.sbuf("mo_gateT_s", [128, 8 * NE], F32)
    xfm32 = p.sbuf("mo_xfm32", [128, 8 * 128], F32)
    xn32 = p.sbuf("mo_xn32", [128, D], F32)
    gb_s = p.sbuf("mo_gb_s", [128, NE], F32)
    wib_s = p.sbuf("mo_wib_s", [128, 32], F32)
    wob_s = p.sbuf("mo_wob_s", [128, 8], F32)
    AcatT_s = p.sbuf("mo_AcatT_s", [128, 8 * NE * R], BF16)
    Bcat_s = p.sbuf("mo_Bcat_s", [32, FF], BF16)
    ident = p.sbuf("mo_ident_s", [128, 128], BF16)
    ident32 = p.sbuf("mo_ident32_s", [128, 128], F32)
    xt = [p.sbuf(f"mo_xt{i}", [128, D], F32) for i in range(4)]
    xn = p.sbuf("mo_xn", [128, D], BF16)
    xfm = p.sbuf("mo_xfm", [128, 8 * 512], BF16)
    sq = p.sbuf("mo_sq", [128, D], F32)
    mrow = p.sbuf("mo_mrow", [128, 1], F32)
    rrow = p.sbuf("mo_rrow", [128, 1], F32)
    gl = p.sbuf("mo_gl", [128, NE], F32)
    gmax = p.sbuf("mo_gmax", [128, 1], F32)
    eq = p.sbuf("mo_eq", [128, NE], F32)
    cum = p.sbuf("mo_cum", [128, NE], F32)
    oh = p.sbuf("mo_oh", [128, NE], F32)
    zer8 = p.sbuf("mo_zer8", [128, NE], F32)
    oh32 = p.sbuf("mo_oh32", [128, 32], F32)
    u_bf = p.sbuf("mo_u_bf", [128, 32], BF16)
    ufm = p.sbuf("mo_ufm", [32, 512], BF16)
    inter = p.sbuf("mo_inter", [128, 8 * 512], BF16)
    oacc = p.sbuf("mo_oacc", [128, 8 * 512], F32)
    yt = p.sbuf("mo_yt", [128, D], F32)

    tp0 = io["tpb0"]
    tp1 = io["tpb1"]
    gp = io["ptb0"][:, 0:NE]
    sp = io["ptb1"][:, 0:32]
    big0 = io["sps"][:, 0:512]
    big1 = io["sps"][:, 512:1024]
    ob0 = io["avq"][:, 0:512]
    ob1 = io["obx"][:, 0:512]

    for kc in range(8):
        p.dma("g", wiT_s[:, kc * FF:(kc + 1) * FF], wiT[kc * 128:(kc + 1) * 128, :])
    for kc2 in range(32):
        p.dma("g", woT_s[:, kc2 * D:(kc2 + 1) * D], woT[kc2 * 128:(kc2 + 1) * 128, :])
    for kc in range(8):
        p.dma("g", gateT_s[:, kc * NE:(kc + 1) * NE], gateT[kc * 128:(kc + 1) * 128, :])
        p.dma("g", AcatT_s[:, kc * 32:(kc + 1) * 32], AcatT[kc * 128:(kc + 1) * 128, :])
    p.dma("g", Bcat_s[:], Bcat)
    p.dma("g", gb_s[:], gateb[0].partition_broadcast(128))
    p.dma("q", wib_s[:], wib.rearrange("(c p) -> p c", p=128), allow_slow_non_contiguous=True)
    p.dma("q", wob_s[:], wob.rearrange("(c p) -> p c", p=128), allow_slow_non_contiguous=True)
    p.dma("q", ident[:], ident_d)
    p.dma("q", ident32[:], ident32_d)
    p.op("v", lambda v: v.memset(zer8[:], 0.0))
    p.constf(1e-5)
    p.barrier()

    for blk in range(NBLK):
        for i in range(4):
            t0 = blk * 512 + i * 128
            ev_x = p.dma("q", xt[i][:], xin[t0:t0 + 128, :])
            p.wait("v", ev_x)
            p.wait("s", ev_x)
            emit_ln(p, xt[i], sq, mrow, rrow, D)
            ev_xn32 = p.op("v", lambda v, i=i: v.tensor_scalar(
                xn32[:], xt[i][:], mrow[:, 0:1], rrow[:, 0:1], OP.subtract, OP.mult))
            ev_xn = p.op("v", lambda v: v.tensor_copy(xn[:], xn32[:]))
            ev_tp_evac = {0: None, 1: None}
            for kc in range(8):
                tp = (tp0, tp1)[kc % 2]
                p.wait("t", ev_xn)
                p.wait("t", ev_tp_evac[kc % 2])
                ev_t = p.op("t", lambda t, kc=kc, tp=tp: t.transpose(
                    tp[:], xn[:, kc * 128:(kc + 1) * 128], ident[:]))
                p.wait("v", ev_t)
                ev_e = p.op("v", lambda v, tp=tp, kc=kc, i=i: v.tensor_copy(
                    xfm[:, kc * 512 + i * 128: kc * 512 + (i + 1) * 128], tp[:]))
                ev_tp_evac[kc % 2] = ev_e
            for kc in range(8):
                p.wait("t", ev_xn32)
                ev_t = p.op("t", lambda t, kc=kc: t.transpose(
                    ob0[:, 0:128], xn32[:, kc * 128:(kc + 1) * 128], ident32[:]))
                p.wait("v", ev_t)
                ev_e32 = p.op("v", lambda v, kc=kc: v.tensor_copy(
                    xfm32[:, kc * 128:(kc + 1) * 128], ob0[:, 0:128]))
                p.wait("t", ev_e32)
            p.barrier(("t", "v"))
            def gmm(t, i=i):
                for kc in range(8):
                    inst = t.matmul(gp, xfm32[:, kc * 128:(kc + 1) * 128],
                                    gateT_s[:, kc * NE:(kc + 1) * NE],
                                    start=(kc == 0), stop=(kc == 7))
                return inst
            ev_g = p.op("t", gmm)
            def smm(t, i=i):
                for kc in range(8):
                    inst = t.matmul(sp, xfm[:, kc * 512 + i * 128: kc * 512 + (i + 1) * 128],
                                    AcatT_s[:, kc * 32:(kc + 1) * 32],
                                    start=(kc == 0), stop=(kc == 7))
                return inst
            ev_s = p.op("t", smm)
            p.wait("v", ev_g)
            p.op("v", lambda v: v.tensor_tensor(gl[:], gp, gb_s[:], OP.add))
            p.op("v", lambda v: v.tensor_reduce(gmax[:], gl[:], axis=AX.X, op=OP.max))
            p.op("v", lambda v: v.tensor_scalar(eq[:], gl[:], gmax[:, 0:1], None, OP.is_ge))
            p.op("v", lambda v: v.tensor_tensor_scan(cum[:], eq[:], zer8[:], 0.0,
                                                     OP.add, OP.add))
            p.op("v", lambda v: v.tensor_scalar(cum[:], cum[:], 1.0, None, OP.is_equal))
            p.op("v", lambda v: v.tensor_tensor(oh[:], cum[:], eq[:], OP.mult))
            oh32v = oh32.rearrange("p (e r) -> p e r", r=4)
            for r in range(4):
                p.op("v", lambda v, r=r: v.tensor_copy(oh32v[:, :, r], oh[:]))
            p.wait("v", ev_s)
            p.op("v", lambda v: v.tensor_tensor(u_bf[:], sp, oh32[:], OP.mult))
            p.barrier(("t", "v"))
            p.op("t", lambda t: t.transpose(tp0[0:32, 0:128], u_bf[:], ident[:]))
            p.barrier(("t", "v"))
            p.op("v", lambda v, i=i: v.tensor_copy(ufm[:, i * 128:(i + 1) * 128],
                                                   tp0[0:32, 0:128]))
            p.barrier(("t", "v", "s"))
        p.barrier(("t", "v", "s"))
        ev_big_evac = {0: None, 1: None}
        ev_ob_read = {0: None, 1: None}
        for rnd in range(4):
            ev_inter = None
            for j in range(8):
                mc = rnd * 8 + j
                bigp = (big0, big1)[j % 2]
                p.wait("t", ev_big_evac[j % 2])
                def bmm(t, mc=mc, bigp=bigp):
                    for kc in range(8):
                        t.matmul(bigp, wiT_s[:, kc * FF + mc * 128: kc * FF + (mc + 1) * 128],
                                 xfm[:, kc * 512:(kc + 1) * 512],
                                 start=(kc == 0), stop=False)
                    return t.matmul(bigp, Bcat_s[:, mc * 128:(mc + 1) * 128], ufm[:],
                                    start=False, stop=True)
                ev_mm = p.op("t", bmm)
                p.wait("s", ev_mm)
                ev_r = p.op("s", lambda s, j=j, mc=mc, bigp=bigp: s.activation(
                    inter[:, j * 512:(j + 1) * 512], bigp, AF.Relu,
                    bias=wib_s[:, mc:mc + 1]))
                ev_big_evac[j % 2] = ev_r
                ev_inter = ev_r
            for oc in range(8):
                obp = (ob0, ob1)[oc % 2]
                p.wait("t", ev_inter)
                p.wait("t", ev_ob_read[oc % 2])
                def omm(t, rnd=rnd, oc=oc, obp=obp):
                    for j in range(8):
                        kc2 = rnd * 8 + j
                        inst = t.matmul(obp, woT_s[:, kc2 * D + oc * 128: kc2 * D + (oc + 1) * 128],
                                        inter[:, j * 512:(j + 1) * 512],
                                        start=(j == 0), stop=(j == 7))
                    return inst
                ev_mm = p.op("t", omm)
                p.wait("v", ev_mm)
                if rnd == 0:
                    ev_a = p.op("v", lambda v, oc=oc, obp=obp: v.tensor_copy(
                        oacc[:, oc * 512:(oc + 1) * 512], obp))
                else:
                    ev_a = p.op("v", lambda v, oc=oc, obp=obp: v.tensor_tensor(
                        oacc[:, oc * 512:(oc + 1) * 512],
                        oacc[:, oc * 512:(oc + 1) * 512], obp, OP.add))
                ev_ob_read[oc % 2] = ev_a
            p.barrier(("t", "s", "v"))
            ev_big_evac = {0: None, 1: None}
            ev_ob_read = {0: None, 1: None}
        for oc in range(8):
            p.op("v", lambda v, oc=oc: v.tensor_scalar(
                inter[:, oc * 512:(oc + 1) * 512], oacc[:, oc * 512:(oc + 1) * 512],
                wob_s[:, oc:oc + 1], None, OP.add))
        p.barrier(("t", "v"))
        for i in range(4):
            t0 = blk * 512 + i * 128
            ev_tp_evac = {0: None, 1: None}
            for oc in range(8):
                tp = (tp0, tp1)[oc % 2]
                p.wait("t", ev_tp_evac[oc % 2])
                ev_t = p.op("t", lambda t, i=i, oc=oc, tp=tp: t.transpose(
                    tp[:], inter[:, oc * 512 + i * 128: oc * 512 + (i + 1) * 128], ident[:]))
                p.wait("v", ev_t)
                ev_e = p.op("v", lambda v, oc=oc, tp=tp: v.tensor_copy(
                    yt[:, oc * 128:(oc + 1) * 128], tp[:]))
                ev_tp_evac[oc % 2] = ev_e
            p.barrier(("v", "s"))
            emit_ln(p, yt, sq, mrow, rrow, D)
            p.op("v", lambda v: v.tensor_scalar(yt[:], yt[:], mrow[:, 0:1], rrow[:, 0:1],
                                                OP.subtract, OP.mult))
            ev_res = p.op("v", lambda v, i=i: v.tensor_tensor(yt[:], yt[:], xt[i][:], OP.add))
            p.wait("q", ev_res)
            ev_st = p.dma("q", xout[t0:t0 + 128, :], yt[:])
            p.wait("v", ev_st)
            p.wait("t", ev_res)
            p.barrier(("t", "v", "s"))
        p.barrier()
    p.wait("q", p.ev("q"))




D = 1024
T = 2048
DI = 2048
DS = 16
TB = 512
NTB = T // TB
GROUPS = [[0, 1], [2, 3], [4, 5], [6, 7]]


def mamba_emit(p, io):
    xres2 = io["xres2"]
    xres3 = io["xres3"]
    sps = io["sps"]
    ps0 = sps[:, 0:512]
    ps1 = sps[:, 512:1024]
    tpb0 = io["tpb0"]
    tpb1 = io["tpb1"]
    xpp = io["ptb0"][0:96, 0:256]
    ident_d = io["ident_d"]
    flg_d = io["flg_d"]

    inpT_d = p.dram("m_inpT", [D, 2 * DI])
    xpwT_d = [p.dram(f"m_xpwT{s}", [DI, 96]) for s in "fb"]
    dtpwT_d = [p.dram(f"m_dtpwT{s}", [64, DI]) for s in "fb"]
    dtb_d = [p.dram(f"m_dtb{s}", [DI]) for s in "fb"]
    alog_d = [p.dram(f"m_alog{s}", [DI, DS]) for s in "fb"]
    dvec_d = [p.dram(f"m_dvec{s}", [DI]) for s in "fb"]
    convw_d = [p.dram(f"m_convw{s}", [DI, 1, 4]) for s in "fb"]
    convb_d = [p.dram(f"m_convb{s}", [DI]) for s in "fb"]
    outT_d = p.dram("m_outT", [DI, D])

    xz_dr = p.dram("m_xz", [2 * DI, T], BF16, kind="Internal")
    xbr_dr = [p.dram(f"m_xbr{s}", [DI, T], kind="Internal") for s in "fb"]
    bcrow_dr = [p.dram(f"m_bcrow{s}", [32, T], kind="Internal") for s in "fb"]
    dlt_dr = [p.dram(f"m_dlt{s}", [DI, T], kind="Internal") for s in "fb"]
    dlub_dr = [p.dram(f"m_dlub{s}", [DI, T], BF16, kind="Internal") for s in "fb"]
    ybr_dr = [p.dram(f"m_ybr{s}", [DI, T], BF16, kind="Internal") for s in "fb"]
    cin_h = p.dram("m_cinh", [128, 96], kind="Internal")
    cout_h = p.dram("m_couth", [256, 96], kind="Internal")
    cin_st = p.dram("m_cinst", [128, 512], kind="Internal")
    cout_st = p.dram("m_coutst", [256, 512], kind="Internal")

    ident = p.sbuf("m_ident", [128, 128], BF16)
    flagb = p.sbuf("m_flagb", [128, 1], F32)
    finv = p.sbuf("m_finv", [128, 1], F32)
    sq = p.sbuf("m_sq", [128, D], F32)
    mrow = p.sbuf("m_mrow", [128, 1], F32)
    rrow = p.sbuf("m_rrow", [128, 1], F32)
    ht = p.sbuf("m_ht", [128, D], F32)
    htb = p.sbuf("m_htb", [128, D], BF16)
    hfm = p.sbuf("m_hfm", [128, 8 * 512], BF16)
    est = p.sbuf("m_est", [128, 512], BF16)
    p.dma("q", ident[:], ident_d)
    p.dma("q", flagb[:], flg_d[0].partition_broadcast(128))
    p.wait("v", p.ev("q"))
    p.op("v", lambda v: v.tensor_scalar(finv[:], flagb[:], -1.0, 1.0, OP.mult, OP.add))

    inpT_s = p.sbuf_tmp("m_inpT_s", [128, 8 * 2 * DI], BF16)
    for kc in range(8):
        p.dma("g", inpT_s[:, kc * 2 * DI:(kc + 1) * 2 * DI],
              inpT_d[kc * 128:(kc + 1) * 128, :])
    p.barrier()
    for blk in range(T // 512):
        for i in range(4):
            t0 = blk * 512 + i * 128
            p.wait("q", p.ev("v")); p.wait("q", p.ev("s"))
            ev_x = p.dma("q", ht[:], xres2[t0:t0 + 128, :])
            p.wait("v", ev_x); p.wait("s", ev_x)
            emit_ln(p, ht, sq, mrow, rrow, D)
            p.wait("v", p.ev("t"))
            ev_xn = p.op("v", lambda v: v.tensor_scalar(
                htb[:], ht[:], mrow[:, 0:1], rrow[:, 0:1], OP.subtract, OP.mult))
            ev_tp = {0: None, 1: None}
            for kc in range(8):
                tp = (tpb0, tpb1)[kc % 2]
                p.wait("t", ev_xn)
                p.wait("t", ev_tp[kc % 2])
                ev_t = p.op("t", lambda t, kc=kc, tp=tp: t.transpose(
                    tp[:], htb[:, kc * 128:(kc + 1) * 128], ident[:]))
                p.wait("v", ev_t)
                ev_e = p.op("v", lambda v, tp=tp, kc=kc, i=i: v.tensor_copy(
                    hfm[:, kc * 512 + i * 128: kc * 512 + (i + 1) * 128], tp[:]))
                ev_tp[kc % 2] = ev_e
        p.barrier(("t", "v", "s"))
        ev_dma_prev = None
        for mc in range(32):
            psx = (ps0, ps1)[mc % 2]
            def pmm(t, mc=mc, psx=psx):
                for kc in range(8):
                    inst = t.matmul(psx, inpT_s[:, kc * 2 * DI + mc * 128: kc * 2 * DI + (mc + 1) * 128],
                                    hfm[:, kc * 512:(kc + 1) * 512],
                                    start=(kc == 0), stop=(kc == 7))
                return inst
            ev_mm = p.op("t", pmm)
            p.wait("s", ev_mm)
            p.wait("s", ev_dma_prev)
            ev_ev = p.op("s", lambda s, psx=psx: s.activation(est[:], psx, AF.Copy))
            p.wait("q", ev_ev)
            ev_dma_prev = p.dma("q", xz_dr[mc * 128:(mc + 1) * 128, blk * 512:(blk + 1) * 512], est[:])
            p.wait("t", ev_ev)
        p.barrier(("t", "v", "s", "q"))
    p.sfree(inpT_s)
    p.barrier()

    hbuf = p.sbuf_tmp("m_hbuf", [128, 96], F32)
    hbf = p.sbuf_tmp("m_hbf", [128, 96], F32)
    hbb = p.sbuf_tmp("m_hbb", [128, 96], F32)
    for ch in range(16):
        p.dma("g", hbuf[:, ch * 6:ch * 6 + 3], xz_dr[ch * 128:(ch + 1) * 128, 0:3])
        p.dma("g", hbuf[:, ch * 6 + 3:ch * 6 + 6], xz_dr[ch * 128:(ch + 1) * 128, T - 3:T])
    p.barrier(("q", "g", "v"))
    p.dma("q", cin_h, hbuf[:])
    p.wait("g", p.ev("q"))
    ev_cc = p.collective("AllGather", mybir.AluOpType.bypass, GROUPS, [cin_h], [cout_h])
    p.wait("q", ev_cc)
    ev_f = p.dma("q", hbf[:], cout_h[0:128, :])
    ev_b = p.dma("q", hbb[:], cout_h[128:256, :])
    p.wait("v", ev_b)
    p.op("v", lambda v: v.tensor_scalar(hbf[:], hbf[:], flagb[:, 0:1], None, OP.mult))
    p.op("v", lambda v: v.tensor_scalar(hbb[:], hbb[:], finv[:, 0:1], None, OP.mult))
    p.barrier()

    cw_s = p.sbuf_tmp("m_cw", [128, 2 * 64], F32)
    cb_s = p.sbuf_tmp("m_cb", [128, 2 * 16], F32)
    xzc = p.sbuf_tmp("m_xzc", [128, T + 6], F32)
    cva = p.sbuf_tmp("m_cva", [128, T], F32)
    for di, s_ in enumerate("fb"):
        p.dma("q", cw_s[:, di * 64:(di + 1) * 64].rearrange("p (c k) -> p c k", k=4),
              convw_d[di].rearrange("(c p) o k -> p c (o k)", p=128),
              allow_slow_non_contiguous=True)
        p.dma("q", cb_s[:, di * 16:(di + 1) * 16], convb_d[di].rearrange("(c p) -> p c", p=128),
              allow_slow_non_contiguous=True)
    p.barrier(("q", "v", "s", "g"))
    for di in range(2):
        for ch in range(16):
            p.wait("g", p.ev("v"))
            ev_x = p.dma("g", xzc[:, 3:T + 3], xz_dr[ch * 128:(ch + 1) * 128, :])
            p.wait("v", ev_x)
            if di == 0:
                p.op("v", lambda v, ch=ch: v.tensor_copy(xzc[:, 0:3], hbf[:, ch * 6 + 3:ch * 6 + 6]))
                sl = [3, 2, 1, 0]
                offs = [3, 2, 1, 0]
                wj = [3, 2, 1, 0]
            else:
                p.op("v", lambda v, ch=ch: v.tensor_copy(xzc[:, T + 3:T + 6], hbb[:, ch * 6:ch * 6 + 3]))
                offs = [3, 4, 5, 6]
                wj = [3, 2, 1, 0]
            p.op("v", lambda v, di=di, ch=ch, o=offs[0], j=wj[0]: v.tensor_scalar(
                cva[:], xzc[:, o:o + T], cw_s[:, di * 64 + ch * 4 + j: di * 64 + ch * 4 + j + 1],
                None, OP.mult))
            for k in range(1, 4):
                p.op("v", lambda v, di=di, ch=ch, o=offs[k], j=wj[k]: v.scalar_tensor_tensor(
                    cva[:], xzc[:, o:o + T], cw_s[:, di * 64 + ch * 4 + j: di * 64 + ch * 4 + j + 1],
                    cva[:], OP.mult, OP.add))
            ev_c = p.ev("v")
            p.wait("s", ev_c)
            ev_si = p.op("s", lambda s, di=di, ch=ch: s.activation(
                cva[:], cva[:], AF.Silu, bias=cb_s[:, di * 16 + ch:di * 16 + ch + 1]))
            p.wait("q", ev_si)
            ev_st = p.dma("q", xbr_dr[di][ch * 128:(ch + 1) * 128, :], cva[:])
            p.wait("v", ev_st)
    p.sfree(cva); p.sfree(xzc); p.sfree(cb_s); p.sfree(cw_s)
    p.sfree(hbb); p.sfree(hbf); p.sfree(hbuf)
    p.barrier()

    xdbl = [p.sbuf(f"m_xdbl{s}", [96, T], F32) for s in "fb"]
    xpwT_s = p.sbuf_tmp("m_xpwT_s", [128, 2 * 16 * 96], F32)
    xbrc = p.sbuf_tmp("m_xbrc", [128, 16 * 512], F32)
    for di in range(2):
        for kc in range(16):
            p.dma("q", xpwT_s[:, di * 1536 + kc * 96: di * 1536 + (kc + 1) * 96],
                  xpwT_d[di][kc * 128:(kc + 1) * 128, :])
    p.barrier()
    for di in range(2):
        for blk in range(NTB):
            for kc in range(16):
                p.dma(("q", "g")[kc % 2], xbrc[:, kc * 512:(kc + 1) * 512],
                      xbr_dr[di][kc * 128:(kc + 1) * 128, blk * 512:(blk + 1) * 512])
            p.barrier(("t", "q", "g"))
            for half in range(2):
                def xpm(t, di=di, half=half):
                    for kc in range(16):
                        inst = t.matmul(xpp,
                                        xpwT_s[:, di * 1536 + kc * 96: di * 1536 + (kc + 1) * 96],
                                        xbrc[:, kc * 512 + half * 256: kc * 512 + (half + 1) * 256],
                                        start=(kc == 0), stop=(kc == 15))
                    return inst
                ev_mm = p.op("t", xpm)
                p.wait("s", ev_mm)
                ev_ev = p.op("s", lambda s, di=di, blk=blk, half=half: s.copy(
                    xdbl[di][:, blk * 512 + half * 256: blk * 512 + (half + 1) * 256], xpp))
                p.wait("t", ev_ev)
            p.barrier(("t", "s", "q", "g"))
    p.sfree(xbrc); p.sfree(xpwT_s)
    p.wait("q", p.ev("s")); p.wait("q", p.ev("v"))
    for di in range(2):
        p.dma("q", bcrow_dr[di], xdbl[di][64:96, :])
    p.barrier()

    dtpwT_s = p.sbuf("m_dtpwT_s", [64, 2 * DI], F32)
    dtb_s = p.sbuf("m_dtb_s", [128, 2 * 16], F32)
    aexp = p.sbuf("m_aexp", [128, 2 * 16 * DS], F32)
    dvec_s = p.sbuf("m_dvec_s", [128, 2 * 16], F32)
    states = p.sbuf("m_states", [128, 2 * 16 * DS], F32)
    init_s = p.sbuf("m_init", [128, 2 * 16 * DS], F32)
    e1col = p.sbuf("m_e1col", [128, 1], F32)
    for di in range(2):
        p.dma("q", dtpwT_s[:, di * DI:(di + 1) * DI], dtpwT_d[di])
        p.dma("q", dtb_s[:, di * 16:(di + 1) * 16], dtb_d[di].rearrange("(c p) -> p c", p=128),
              allow_slow_non_contiguous=True)
        p.dma("q", dvec_s[:, di * 16:(di + 1) * 16], dvec_d[di].rearrange("(c p) -> p c", p=128),
              allow_slow_non_contiguous=True)
        ev_a = p.dma("q", aexp[:, di * 256:(di + 1) * 256].rearrange("p (c n) -> p c n", n=DS),
                     alog_d[di].rearrange("(c p) n -> p c n", p=128))
    p.wait("s", p.ev("q"))
    ev_a2 = p.op("s", lambda s: s.activation(aexp[:], aexp[:], AF.Exp))
    p.wait("v", ev_a2)
    p.op("v", lambda v: v.tensor_scalar(aexp[:], aexp[:], -1.0, None, OP.mult))
    p.op("v", lambda v: v.memset(states[:], 0.0))
    p.barrier()

    bball = p.sbuf_tmp("m_bball", [128, 16 * T], BF16)
    dltf = p.sbuf_tmp("m_p1dlt", [128, T], F32)
    dluf = p.sbuf_tmp("m_p1dlu", [128, T], F32)
    dlub = p.sbuf_tmp("m_p1dlub", [128, T], BF16)
    lamf2 = [p.sbuf_tmp("m_p1lam", [128, T], BF16),
             p.sbuf_tmp("m_p1lamB", [128, T], BF16)]
    bscf2 = [p.sbuf_tmp("m_p1bsc", [128, T], BF16),
             p.sbuf_tmp("m_p1bscB", [128, T], BF16)]
    hscf = p.sbuf_tmp("m_p1hsc", [128, T], BF16)
    uch = p.sbuf_tmp("m_uch", [128, T], F32)
    spt = p.sbuf_tmp("m_spt", [128, T], F32)
    ev_scan_par = [None, None]
    evv_ps = [None, None]
    evs_ps = [None, None]
    for di in range(2):
        for n in range(DS):
            p.wait("g", p.ev("v"))
            p.dma("g", bball[:, n * T:(n + 1) * T], bcrow_dr[di][n, :].partition_broadcast(128))
        p.barrier(("g", "v", "s", "t", "q"))
        for ch in range(16):
            for sb in range(4):
                psx = (ps0, ps1)[sb % 2]
                p.wait("t", evv_ps[sb % 2]); p.wait("t", evs_ps[sb % 2])
                ev_mm = p.op("t", lambda t, di=di, ch=ch, sb=sb, psx=psx: t.matmul(
                    psx, dtpwT_s[:, di * DI + ch * 128:(di * DI) + (ch + 1) * 128],
                    xdbl[di][0:64, sb * 512:(sb + 1) * 512], start=True, stop=True))
                p.wait("s", ev_mm)
                evs_ps[sb % 2] = p.op("s", lambda s, di=di, ch=ch, sb=sb, psx=psx: s.activation(
                    spt[:, sb * 512:(sb + 1) * 512], psx, AF.Abs,
                    bias=dtb_s[:, di * 16 + ch: di * 16 + ch + 1]))
                ev_sq2 = p.op("s", lambda s, sb=sb: s.activation(
                    spt[:, sb * 512:(sb + 1) * 512], spt[:, sb * 512:(sb + 1) * 512],
                    AF.Exp, scale=-1.0))
                ev_sq3 = p.op("s", lambda s, sb=sb: s.activation(
                    spt[:, sb * 512:(sb + 1) * 512], spt[:, sb * 512:(sb + 1) * 512],
                    AF.Ln, bias=p.constf(1.0)))
                p.wait("v", ev_sq3)
                evv_ps[sb % 2] = p.op("v", lambda v, di=di, ch=ch, sb=sb, psx=psx: v.tensor_scalar(
                    dltf[:, sb * 512:(sb + 1) * 512], psx,
                    dtb_s[:, di * 16 + ch: di * 16 + ch + 1], 0.0, OP.add, OP.max))
                ev_dadd = p.op("v", lambda v, sb=sb: v.tensor_tensor(
                    dltf[:, sb * 512:(sb + 1) * 512], dltf[:, sb * 512:(sb + 1) * 512],
                    spt[:, sb * 512:(sb + 1) * 512], OP.add))
            p.wait("q", p.ev("v"))
            ev_u = p.dma("q", uch[:], xbr_dr[di][ch * 128:(ch + 1) * 128, :])
            p.wait("v", ev_u)
            p.op("v", lambda v: v.tensor_tensor(dluf[:], dltf[:], uch[:], OP.mult))
            ev_dub = p.op("v", lambda v: v.tensor_copy(dlub[:], dluf[:]))
            p.wait("q", ev_dub)
            p.dma("q", dlt_dr[di][ch * 128:(ch + 1) * 128, :], dltf[:])
            ev_sdl = p.dma("q", dlub_dr[di][ch * 128:(ch + 1) * 128, :], dlub[:])
            for n in range(DS):
                pa = n % 2
                lamf = lamf2[pa]
                bscf = bscf2[pa]
                p.wait("s", ev_scan_par[pa])
                if n == 0:
                    p.wait("s", ev_dadd)
                ev_l = p.op("s", lambda s, di=di, ch=ch, n=n, lamf=lamf: s.activation(
                    lamf[:], dltf[:], AF.Exp,
                    scale=aexp[:, di * 256 + ch * DS + n: di * 256 + ch * DS + n + 1]))
                p.op("v", lambda v, n=n, bscf=bscf: v.tensor_tensor(
                    bscf[:], dlub[:], bball[:, n * T:(n + 1) * T], OP.mult))
                p.wait("v", ev_l)
                if di == 0:
                    p.op("v", lambda v, lamf=lamf, bscf=bscf: v.tensor_tensor_scan(
                        hscf[:], lamf[:], bscf[:], 0.0, OP.mult, OP.add))
                else:
                    p.op("v", lambda v, lamf=lamf, bscf=bscf: v.tensor_tensor_scan(
                        hscf[:], lamf[:, ::-1], bscf[:, ::-1], 0.0, OP.mult, OP.add))
                ev_scan_par[pa] = p.op("v", lambda v, di=di, ch=ch, n=n: v.tensor_copy(
                    states[:, di * 256 + ch * DS + n: di * 256 + ch * DS + n + 1],
                    hscf[:, T - 1:T]))
            p.wait("v", ev_sdl)
            p.wait("s", ev_sdl)
        p.barrier(("t", "v", "s", "q", "g"))
        ev_scan_par = [None, None]
        evv_ps = [None, None]
        evs_ps = [None, None]
    p.sfree(spt); p.sfree(uch); p.sfree(hscf)
    p.sfree(bscf2[1]); p.sfree(bscf2[0]); p.sfree(lamf2[1]); p.sfree(lamf2[0])
    p.sfree(dlub); p.sfree(dluf); p.sfree(dltf); p.sfree(bball)
    p.barrier()

    stg = p.sbuf_tmp("m_stg", [128, 512], F32)
    p.wait("q", p.ev("v"))
    p.dma("q", cin_st, states[:])
    p.wait("g", p.ev("q"))
    ev_cc2 = p.collective("AllGather", mybir.AluOpType.bypass, GROUPS, [cin_st], [cout_st])
    p.wait("q", ev_cc2)
    ev_l1 = p.dma("q", stg[:, 0:256], cout_st[0:128, 0:256])
    ev_l2 = p.dma("q", stg[:, 256:512], cout_st[128:256, 256:512])
    p.wait("v", ev_l2)
    p.op("v", lambda v: v.tensor_scalar(init_s[:, 0:256], stg[:, 0:256],
                                        flagb[:, 0:1], None, OP.mult))
    p.op("v", lambda v: v.tensor_scalar(init_s[:, 256:512], stg[:, 256:512],
                                        finv[:, 0:1], None, OP.mult))
    p.sfree(stg)
    p.barrier()

    TB2 = 1024
    bcb = p.sbuf_tmp("m_bcb", [128, DS * TB2], BF16)
    ccb = p.sbuf_tmp("m_ccb", [128, DS * TB2], BF16)
    dlt2d = [p.sbuf_tmp("m_dlt2", [128, TB2], F32),
             p.sbuf_tmp("m_dlt2B", [128, TB2], F32)]
    dlu2d = [p.sbuf_tmp("m_dlu2", [128, TB2], BF16),
             p.sbuf_tmp("m_dlu2B", [128, TB2], BF16)]
    lam2d = [p.sbuf_tmp("m_lam2", [128, TB2], BF16),
             p.sbuf_tmp("m_lam2B", [128, TB2], BF16)]
    bsc2 = p.sbuf_tmp("m_bsc2", [128, TB2], BF16)
    hsc2 = p.sbuf_tmp("m_hsc2", [128, TB2], BF16)
    tmp2d = [p.sbuf_tmp("m_tmp2", [128, TB2], BF16),
             p.sbuf_tmp("m_tmp2B", [128, TB2], BF16)]
    tmp2 = tmp2d[0]
    yac2 = p.sbuf_tmp("m_yac2", [128, TB2], F32)
    uch2d = [p.sbuf_tmp("m_uch2", [128, TB2], F32),
             p.sbuf_tmp("m_uch2B", [128, TB2], F32)]
    zall = p.sbuf_tmp("m_zall", [128, 16 * TB2], BF16)
    st2 = p.sbuf_tmp("m_st2", [128, 2 * 16 * DS], F32)
    NB2 = T // TB2
    for di in range(2):
        for tau in range(NB2):
            tb = tau if di == 0 else (NB2 - 1 - tau)
            tsl = slice(tb * TB2, (tb + 1) * TB2)
            for n in range(DS):
                p.wait("g", p.ev("v"))
                p.dma("g", bcb[:, n * TB2:(n + 1) * TB2], bcrow_dr[di][n, tsl].partition_broadcast(128))
                p.dma("g", ccb[:, n * TB2:(n + 1) * TB2], bcrow_dr[di][DS + n, tsl].partition_broadcast(128))
            for ch in range(16):
                p.dma("g", zall[:, ch * TB2:(ch + 1) * TB2],
                      xz_dr[DI + ch * 128: DI + (ch + 1) * 128, tsl])
            p.barrier(("g", "v", "s", "t", "q"))
            for ch in range(16):
                p.op("s", lambda s, ch=ch: s.activation(
                    zall[:, ch * TB2:(ch + 1) * TB2], zall[:, ch * TB2:(ch + 1) * TB2], AF.Silu))
            p.barrier(("s", "v", "g"))
            ev_scan2 = [None, None]
            ev_tmp2 = [None, None]
            ev_add = None
            ev_st2 = None
            ld = [None, None]
            done = [None, None]

            def issue_loads(chl, pal):
                p.dma("q", dlt2d[pal][:], dlt_dr[di][chl * 128:(chl + 1) * 128, tsl])
                p.dma("q", dlu2d[pal][:], dlub_dr[di][chl * 128:(chl + 1) * 128, tsl])
                e3 = p.dma("q", uch2d[pal][:], xbr_dr[di][chl * 128:(chl + 1) * 128, tsl])
                return e3
            ld[0] = issue_loads(0, 0)
            for ch in range(16):
                pa_ch = ch % 2
                if ch + 1 < 16:
                    pan = (ch + 1) % 2
                    if done[pan] is not None:
                        p.wait("q", done[pan][0]); p.wait("q", done[pan][1])
                    ld[pan] = issue_loads(ch + 1, pan)
                ev_u2 = ld[pa_ch]
                dlt2 = dlt2d[pa_ch]
                dlu2 = dlu2d[pa_ch]
                uch2 = uch2d[pa_ch]
                p.wait("s", ev_u2)
                p.wait("v", ev_u2)
                p.wait("g", ev_st2)
                ev_ms = p.op("g", lambda g: g.memset(yac2[:], 0.0))
                for n in range(DS):
                    pa = n % 2
                    lam2 = lam2d[pa]
                    tmp2n = tmp2d[pa]
                    scol = st2[:, di * 256 + ch * DS + n: di * 256 + ch * DS + n + 1]
                    icol = init_s[:, di * 256 + ch * DS + n: di * 256 + ch * DS + n + 1]
                    p.wait("s", ev_scan2[pa])
                    ev_l = p.op("s", lambda s, di=di, ch=ch, n=n, lam2=lam2, dlt2=dlt2: s.activation(
                        lam2[:], dlt2[:], AF.Exp,
                        scale=aexp[:, di * 256 + ch * DS + n: di * 256 + ch * DS + n + 1]))
                    p.op("v", lambda v, n=n, dlu2=dlu2: v.tensor_tensor(
                        bsc2[:], dlu2[:], bcb[:, n * TB2:(n + 1) * TB2], OP.mult))
                    p.wait("v", ev_l)
                    init_ap = icol if tau == 0 else scol
                    if di == 0:
                        ev_scan2[pa] = p.op("v", lambda v, init_ap=init_ap, lam2=lam2: v.tensor_tensor_scan(
                            hsc2[:], lam2[:], bsc2[:], init_ap, OP.mult, OP.add))
                    else:
                        ev_scan2[pa] = p.op("v", lambda v, init_ap=init_ap, lam2=lam2: v.tensor_tensor_scan(
                            hsc2[:], lam2[:, ::-1], bsc2[:, ::-1], init_ap, OP.mult, OP.add))
                    p.op("v", lambda v, scol=scol: v.tensor_copy(scol, hsc2[:, TB2 - 1:TB2]))
                    p.wait("v", ev_tmp2[pa])
                    if di == 0:
                        ev_t = p.op("v", lambda v, n=n, tmp2n=tmp2n: v.tensor_tensor(
                            tmp2n[:], hsc2[:], ccb[:, n * TB2:(n + 1) * TB2], OP.mult))
                    else:
                        ev_t = p.op("v", lambda v, n=n, tmp2n=tmp2n: v.tensor_tensor(
                            tmp2n[:], hsc2[:], ccb[:, n * TB2:(n + 1) * TB2][:, ::-1], OP.mult))
                    p.wait("g", ev_t)
                    ev_add = p.op("g", lambda g, tmp2n=tmp2n: g.tensor_tensor(
                        yac2[:], yac2[:], tmp2n[:], OP.add))
                    ev_tmp2[pa] = ev_add
                p.wait("v", ev_u2)
                p.wait("v", ev_add)
                if di == 0:
                    p.op("v", lambda v, di=di, ch=ch, uch2=uch2: v.scalar_tensor_tensor(
                        yac2[:], uch2[:], dvec_s[:, di * 16 + ch: di * 16 + ch + 1],
                        yac2[:], OP.mult, OP.add))
                else:
                    p.op("v", lambda v: v.tensor_copy(tmp2[:], yac2[:, ::-1]))
                    p.op("v", lambda v, di=di, ch=ch, uch2=uch2: v.scalar_tensor_tensor(
                        yac2[:], uch2[:], dvec_s[:, di * 16 + ch: di * 16 + ch + 1],
                        tmp2[:], OP.mult, OP.add))
                ev_y = p.op("v", lambda v, ch=ch: v.tensor_tensor(
                    yac2[:], yac2[:], zall[:, ch * TB2:(ch + 1) * TB2], OP.mult))
                p.wait("g", ev_y)
                ev_st2 = p.dma("g", ybr_dr[di][ch * 128:(ch + 1) * 128, tsl], yac2[:])
                done[pa_ch] = (p.ev("v"), p.ev("s"))
        p.barrier(("t", "v", "s", "q", "g"))
    p.sfree(st2); p.sfree(zall); p.sfree(uch2d[1]); p.sfree(uch2d[0]); p.sfree(yac2)
    p.sfree(tmp2d[1]); p.sfree(tmp2d[0]); p.sfree(hsc2); p.sfree(bsc2)
    p.sfree(lam2d[1]); p.sfree(lam2d[0]); p.sfree(dlu2d[1]); p.sfree(dlu2d[0])
    p.sfree(dlt2d[1]); p.sfree(dlt2d[0])
    p.sfree(ccb); p.sfree(bcb)
    p.barrier()

    ysum = p.sbuf_tmp("m_ysum", [128, 16 * T], BF16)
    outT_s = p.sbuf_tmp("m_outT_s", [128, 16 * D], BF16)
    ya = p.sbuf_tmp("m_ya", [128, T], BF16)
    ybb2 = p.sbuf_tmp("m_ybb2", [128, T], BF16)
    yt = p.sbuf_tmp("m_yt", [128, D], F32)
    for kc in range(16):
        p.dma("g", outT_s[:, kc * D:(kc + 1) * D], outT_d[kc * 128:(kc + 1) * 128, :])
    p.barrier()
    for kc in range(16):
        p.wait("q", p.ev("v"))
        ev1 = p.dma("q", ya[:], ybr_dr[0][kc * 128:(kc + 1) * 128, :])
        ev2 = p.dma("q", ybb2[:], ybr_dr[1][kc * 128:(kc + 1) * 128, :])
        p.wait("v", ev2)
        p.op("v", lambda v, kc=kc: v.tensor_tensor(
            ysum[:, kc * T:(kc + 1) * T], ya[:], ybb2[:], OP.add))
    p.barrier(("t", "v", "q"))
    ev_pst = None
    for ti in range(16):
        p.wait("q", p.ev("v"))
        ev_x = p.dma("q", ht[:], xres2[ti * 128:(ti + 1) * 128, :])
        p.wait("s", ev_pst)
        for nh2 in range(2):
            psx = (ps0, ps1)[nh2]
            def omm(t, ti=ti, nh2=nh2, psx=psx):
                for kc in range(16):
                    inst = t.matmul(psx, ysum[:, kc * T + ti * 128: kc * T + (ti + 1) * 128],
                                    outT_s[:, kc * D + nh2 * 512: kc * D + (nh2 + 1) * 512],
                                    start=(kc == 0), stop=(kc == 15))
                return inst
            ev_mm = p.op("t", omm)
            p.wait("s", ev_mm)
            ev_y = p.op("s", lambda s, nh2=nh2, psx=psx: s.copy(
                yt[:, nh2 * 512:(nh2 + 1) * 512], psx))
            p.wait("t", ev_y)
        p.wait("v", p.ev("s"))
        p.wait("s", p.ev("v"))
        emit_ln(p, yt, sq, mrow, rrow, D)
        p.op("v", lambda v: v.tensor_scalar(yt[:], yt[:], mrow[:, 0:1], rrow[:, 0:1],
                                            OP.subtract, OP.mult))
        p.wait("v", ev_x)
        ev_r = p.op("v", lambda v: v.tensor_tensor(yt[:], yt[:], ht[:], OP.add))
        p.wait("q", ev_r)
        ev_pst = p.dma("q", xres3[ti * 128:(ti + 1) * 128, :], yt[:])
        p.wait("v", ev_pst)
    p.sfree(yt); p.sfree(ybb2); p.sfree(ya); p.sfree(outT_s); p.sfree(ysum)
    p.barrier()


def mamba_host(z, core):
    out = dict(
        m_inpT=np.ascontiguousarray(z["L1_attn_in_proj_w"].T),
        m_outT=np.ascontiguousarray(z["L1_attn_out_proj_w"].T),
    )
    for di, sfx in enumerate(["", "_b"]):
        s_ = "fb"[di]
        out[f"m_xpwT{s_}"] = np.ascontiguousarray(z["L1_attn_x_proj_w" + sfx].T)
        out[f"m_dtpwT{s_}"] = np.ascontiguousarray(z["L1_attn_dt_proj_w" + sfx].T)
        out[f"m_dtb{s_}"] = z["L1_attn_dt_proj_b" + sfx]
        out[f"m_alog{s_}"] = z["L1_attn_A_log" if di == 0 else "L1_attn_A_b_log"]
        out[f"m_dvec{s_}"] = z["L1_attn_D" if di == 0 else "L1_attn_D_b"]
        out[f"m_convw{s_}"] = z["L1_attn_conv_w" + sfx]
        out[f"m_convb{s_}"] = z["L1_attn_conv_b" + sfx]
    return out


def attn_tables():
    pvec = np.arange(128)
    omega = 10000.0 ** (-(2.0 * pvec) / 256.0)
    i = np.arange(S, dtype=np.float64)
    ang = np.outer(omega, i)
    CP = np.sqrt(2.0) / 16.0
    a1 = np.zeros((256, S)); a2 = np.zeros((256, S))
    vtab = np.zeros((256, S))
    a1[0::2] = CP * np.sin(ang); a1[1::2] = CP * np.sin(ang)
    a2[0::2] = CP * np.cos(ang); a2[1::2] = -CP * np.cos(ang)
    vtab[0::2] = np.cos(ang); vtab[1::2] = np.sin(ang)
    masks = np.zeros((128, 4 * 512), np.float32)
    ii = np.arange(128)[:, None]; jj = np.arange(512)[None, :]
    for o in range(4):
        masks[:, o * 512:(o + 1) * 512] = (jj <= ii + o * 128).astype(np.float32)
    return a1.astype(np.float32), a2.astype(np.float32), vtab.astype(np.float32), \
        masks.astype(ml_dtypes.bfloat16)



def attn_host(z, core, x0):
    b, hg = core // 2, core % 2
    heads = slice(hg * 8, hg * 8 + 8)
    qkv = z["L0_attn_qkv"]
    wq_ = np.ascontiguousarray(qkv[:, 0, heads, :].reshape(D, 512))
    wk_ = np.ascontiguousarray(qkv[:, 1, heads, :].reshape(D, 512))
    wv_ = np.ascontiguousarray(qkv[:, 2, heads, :].reshape(D, 512))
    qb_ = np.ascontiguousarray(z["L0_attn_q_bias"][heads].reshape(512))
    pos = z["L0_attn_positional"]
    pt_ = np.ascontiguousarray(pos[:, heads, :].transpose(1, 2, 0))
    swap = np.arange(256) ^ 1
    pts_ = np.ascontiguousarray(pos[swap][:, heads, :].transpose(1, 2, 0))
    a1, a2, vtab, masks = attn_tables()
    return dict(
        x0b=np.ascontiguousarray(x0[b]),
        xown=np.ascontiguousarray(x0[b, hg * TC:(hg + 1) * TC]),
        wq=wq_, wk=wk_, wv=wv_, qb=qb_, pt=pt_, pts=pts_,
        a1=a1, a2=a2, vtab=vtab,
        wout=np.ascontiguousarray(z["L0_attn_out_kernel"]),
        outb=z["L0_attn_out_bias"].reshape(1, D),
        ident=np.eye(128).astype(ml_dtypes.bfloat16),
        flg=np.array([[float(hg)]], np.float32),
        masks=masks,
    )



def ffn_tables(hg):
    g0 = hg * 2048
    half_idx = np.arange(0, 64, 2, dtype=np.float64)
    inv_freq = 1.0 / 10000.0 ** (half_idx / 64.0)
    base = (half_idx + 0.4 * 64.0) / (1.4 * 64.0)
    def mk(trange):
        t = np.asarray(trange, dtype=np.float64)[:, None]
        fb = t * inv_freq[None, :]
        sc = base[None, :] ** ((t - 2048.0) / 256.0)
        fr = np.concatenate([fb, fb], 1)
        scl = np.concatenate([sc, sc], 1)
        return np.cos(fr), np.sin(fr), scl
    cq, sq_, scq = mk(range(g0, g0 + 2048))
    ck, sk_, sck = mk(range(g0 - 512, g0 + 2048))
    t8 = lambda a: np.tile(a.astype(np.float32), (1, 8))
    return (t8(cq * scq), t8(sq_ * scq), t8(ck / sck), t8(sk_ / sck))



def host_masks():
    ii = np.arange(128)[:, None]; jj = np.arange(512)[None, :]
    md = np.zeros((128, 4 * 512), np.float32)
    mp = np.zeros((128, 4 * 512), np.float32)
    for o in range(4):
        md[:, o * 512:(o + 1) * 512] = (jj <= ii + o * 128)
        mp[:, o * 512:(o + 1) * 512] = (jj >= ii + o * 128)
    return md.astype(ml_dtypes.bfloat16), mp.astype(ml_dtypes.bfloat16)



def ffn_host(z, core, x1):
    b, hg = core // 2, core % 2
    g0 = hg * TC
    if hg == 0:
        halo = np.zeros((HALO, D), np.float32)
    else:
        halo = np.ascontiguousarray(x1[b, g0 - HALO:g0])
    aq, bq, ak, bk = ffn_tables(hg)
    md, mp = host_masks()
    qs = np.stack([np.tile(z["L0_ffn_q_scale"][l], 8)[None] for l in range(2)])
    ks = np.stack([np.tile(z["L0_ffn_k_scale"][l], 8)[None] for l in range(2)])
    return dict(
        x1own=np.ascontiguousarray(x1[b, g0:g0 + TC]),
        x1halo=halo,
        wqkv=z["L0_ffn_wqkv"], qs=qs, ks=ks,
        wo=z["L0_ffn_wo"],
        lg=z["L0_ffn_ln_g"].reshape(2, 1, D), lb=z["L0_ffn_ln_b"].reshape(2, 1, D),
        w1=z["L0_ffn_w1"], b1=z["L0_ffn_b1"], w2=z["L0_ffn_w2"],
        b2=z["L0_ffn_b2"].reshape(2, 1, D),
        aq=aq, bq=bq, ak=ak, bk=bk,
        ident=np.eye(128).astype(ml_dtypes.bfloat16),
        masksd=md, masksp=mp,
        flg=np.array([[float(hg)]], np.float32),
    )



def moe_host(z, core):
    b, h = core // 2, core % 2
    NE_, R_, D_, FF_ = 8, 4, 1024, 4096
    return dict(
        wiT=np.ascontiguousarray(z["L1_ffn_wi_W"].T),
        woT=np.ascontiguousarray(z["L1_ffn_wo_W"].T),
        gateT=np.ascontiguousarray(z["L1_ffn_gate_W"].T),
        gateb=z["L1_ffn_gate_b"].reshape(1, NE_),
        wib=z["L1_ffn_wi_b"],
        wob=z["L1_ffn_wo_b"],
        AcatT=np.ascontiguousarray(z["L1_ffn_lora_A"].reshape(NE_ * R_, D_).T),
        Bcat=np.ascontiguousarray(z["L1_ffn_lora_B"].transpose(0, 2, 1).reshape(NE_ * R_, FF_)),
    )


_NC = None



TC = 2048
D = 1024


def fused_build():
    p = Prog()
    io = {}
    io["ident_d"] = p.dram("ident", [128, 128], BF16)
    io["ident32_d"] = p.dram("ident32", [128, 128], F32)
    io["flg_d"] = p.dram("flg", [1, 1])
    io["xres1"] = p.dram("xres1", [TC, D], kind="Internal")
    io["xres2"] = p.dram("xres2", [TC, D], kind="Internal")
    io["xres3"] = p.dram("xres3", [TC, D], kind="Internal")
    io["xout"] = p.dram("xout", [TC, D], kind="ExternalOutput")
    io["sps"] = p.psum("sps", [128, 1024], F32)
    io["ptb0"] = p.psum("ptb0", [128, 512], F32)
    io["ptb1"] = p.psum("ptb1", [128, 512], F32)
    io["avq"] = p.psum("avq", [128, 512], F32)
    io["obx"] = p.psum("obx", [128, 512], F32)
    io["tpb0"] = p.psum("tpb0", [128, 128], BF16)
    io["tpb1"] = p.psum("tpb1", [128, 128], BF16)
    p.constf(1e-5)
    p.constf(1.0)

    for emitfn in (attn_emit, ffn_emit, mamba_emit, moe_emit):
        p.push_scope()
        emitfn(p, io)
        p.barrier()
        p.pop_scope()
        p.barrier()
        p.new_epoch()
    p.wait("q", p.ev("q"))
    p.wait("g", p.ev("g"))
    return p


def fused_host(z, core, x0):
    b, hg = core // 2, core % 2
    im = {}
    a = attn_host(z, core, x0)
    im.update(a)
    f = ffn_host(z, core, x0)
    for k in ("x1own", "x1halo"):
        f.pop(k, None)
    im.update(f)
    im.update(mamba_host(z, core))
    mo = moe_host(z, core)
    mo.pop("xin", None)
    im.update(mo)
    im["ident"] = np.eye(128).astype(ml_dtypes.bfloat16)
    im["ident32"] = np.eye(128).astype(np.float32)
    im["flg"] = np.array([[float(hg)]], np.float32)
    return im


def kernel(**inputs):
    z = {k: np.asarray(v) for k, v in inputs.items()}
    x0 = z["L0_attn_input"]
    global _NC
    if _NC is None:
        _NC = fused_build().finalize()
    nc = _NC
    cores = list(range(8))
    in_maps = [fused_host(z, c, x0) for c in cores]
    res = run_bass_kernel_spmd(nc, in_maps, core_ids=cores)
    x4 = np.stack([
        np.concatenate([res.results[2 * b]["xout"], res.results[2 * b + 1]["xout"]], 0)
        for b in range(4)])
    return x4.astype(np.float32)


